# version 13
# baseline (speedup 1.0000x reference)
import os
import numpy as np
import ml_dtypes
from contextlib import ExitStack

import concourse.bass as bass
import concourse.mybir as mybir
from concourse.bass_utils import run_bass_kernel_spmd

F32 = mybir.dt.float32
BF16 = mybir.dt.bfloat16
I32 = mybir.dt.int32
AF = mybir.ActivationFunctionType
ALU = mybir.AluOpType
AX = mybir.AxisListType

NCORES = 8
SEQ_PER_CORE = 2
S = 2048
D = 2048
NT = S // 128
I_FFN = 5632
GU_OC = 88
DN_OC = 16
SC_B = float(192 ** -0.5)
EPS_LN = 1e-5
EPS_B = 1e-6
MASK_NEG = -30000.0

ENGINES = ("sync", "tensor", "vector", "scalar", "gpsimd")



def _repack_nibbles(packed: np.ndarray, wpr: int) -> np.ndarray:
    u = packed.view(np.uint32)
    o, w = u.shape
    assert w == wpr
    shifts = (4 * np.arange(8, dtype=np.uint32))[None, None, :]
    nib = ((u[:, :, None] >> shifts) & np.uint32(15)).astype(np.uint8)
    unp = nib.reshape(o, w * 8)
    words = np.zeros((o, wpr), np.uint32)
    for pp in range(4):
        blk = unp[:, pp * 2 * wpr:(pp + 1) * 2 * wpr]
        lo = blk[:, 0::2].astype(np.uint32)
        hi = blk[:, 1::2].astype(np.uint32)
        words |= lo << np.uint32(4 * pp)
        words |= hi << np.uint32(4 * pp + 16)
    return words.view(np.int32)


def _rope_tables():
    inv = (1.0 / (10000.0 ** (np.arange(0, 64, 2, dtype=np.float32) / 64.0))).astype(np.float32)
    t = np.arange(S, dtype=np.float32)
    fr = np.outer(t, inv).astype(np.float32)
    emb = np.concatenate([fr, fr], axis=-1)
    cos = np.cos(emb).astype(np.float32)
    sin = np.sin(emb).astype(np.float32)
    cosT = np.ascontiguousarray(cos.T)
    sinT = np.ascontiguousarray(sin.T)
    qcos = np.concatenate([cosT, cosT], axis=0)
    qsin = np.concatenate([sinT, sinT], axis=0)
    return (qcos.astype(ml_dtypes.bfloat16), qsin.astype(ml_dtypes.bfloat16),
            cosT.astype(ml_dtypes.bfloat16), sinT.astype(ml_dtypes.bfloat16))


def _rot_mat():
    R = np.zeros((64, 64), np.float32)
    for i in range(32):
        R[i, i + 32] = -1.0
        R[i + 32, i] = 1.0
    RT = R.T
    blk = np.zeros((128, 128), np.float32)
    blk[:64, :64] = RT
    blk[64:, 64:] = RT
    return (np.ascontiguousarray(blk).astype(ml_dtypes.bfloat16),
            np.ascontiguousarray(RT).astype(ml_dtypes.bfloat16))


def _tile_w(w, W=128):
    K, M = w.shape
    kc = K // 128
    oc = M // W
    t = w.reshape(kc, 128, oc, W).transpose(2, 1, 0, 3).reshape(oc, 128, kc * W)
    return np.ascontiguousarray(t, dtype=np.float32)


def prep_host(inputs):
    g = {}
    f32 = np.float32
    asc = np.ascontiguousarray
    g["ws"] = _tile_w(np.asarray(inputs["L0_attn_a_Ws"], dtype=f32))
    g["wa"] = _tile_w(np.asarray(inputs["L0_attn_a_Wa"], dtype=f32))
    g["woT"] = _tile_w(np.asarray(inputs["L0_attn_a_Wo"], dtype=f32).T)
    g["bo"] = asc(inputs["L0_attn_a_bo"].reshape(16, 128).T, dtype=f32)
    g["qaT"] = _tile_w(np.asarray(inputs["L0_attn_b_q_a_w"], dtype=f32).T)
    g["qalnw"] = asc(inputs["L0_attn_b_q_a_ln_w"].reshape(12, 128).T, dtype=f32)
    qb = np.asarray(inputs["L0_attn_b_q_b_w"], dtype=f32)
    qb3 = qb.reshape(16, 192, 1536)
    qbn = qb3[:, :128, :].reshape(16 * 128, 1536)
    qbp = qb3[:, 128:, :]
    qbp_packed = qbp.reshape(8, 2 * 64, 1536).reshape(16 * 64, 1536)
    g["qbnT"] = _tile_w(asc(qbn.T, dtype=f32))
    g["qbpT"] = _tile_w(asc(qbp_packed.T, dtype=f32))
    kvaT = asc(np.asarray(inputs["L0_attn_b_kv_a_w"], dtype=f32).T)
    g["kvaT"] = _tile_w(kvaT[:, 0:512])
    g["kvapeT"] = _tile_w(kvaT[:, 512:576], W=64)
    g["kvlnw"] = asc(inputs["L0_attn_b_kv_a_ln_w"].reshape(4, 128).T, dtype=f32)
    kvb = np.asarray(inputs["L0_attn_b_kv_b_w"], dtype=f32)
    kvb3 = kvb.reshape(16, 256, 512)
    g["kvbnT"] = _tile_w(asc(kvb3[:, :128, :].reshape(16 * 128, 512).T, dtype=f32))
    g["kvbvT"] = _tile_w(asc(kvb3[:, 128:, :].reshape(16 * 128, 512).T, dtype=f32))
    g["owT"] = _tile_w(np.asarray(inputs["L0_attn_b_o_w"], dtype=f32).T)
    g["gup"] = _repack_nibbles(np.asarray(inputs["L0_ffn_gate_up_packed"]), 256)
    g["dnp"] = _repack_nibbles(np.asarray(inputs["L0_ffn_down_packed"]), 704)
    g["gusT"] = asc(np.asarray(inputs["L0_ffn_gate_up_scales"]).T, dtype=f32)
    g["dnsT"] = asc(np.asarray(inputs["L0_ffn_down_scales"]).T, dtype=f32)
    g["identb"] = np.eye(128, dtype=ml_dtypes.bfloat16)
    m = np.zeros((128, 128), np.float32)
    iu = np.triu_indices(128, 1)
    m[iu] = MASK_NEG
    g["mask"] = m
    qcos, qsin, kcos, ksin = _rope_tables()
    g["ropeqc"], g["ropeqs"], g["ropekc"], g["ropeks"] = qcos, qsin, kcos, ksin
    g["rotq"], g["rotk"] = _rot_mat()
    return g



class Sem:
    __slots__ = ("h", "n", "name")

    def __init__(self, h, name):
        self.h = h
        self.n = 0
        self.name = name


class _Op:
    __slots__ = ("waits", "fn", "incs")

    def __init__(self, waits, fn, incs):
        self.waits = waits
        self.fn = fn
        self.incs = incs


class Builder:
    def __init__(self, nc, stack):
        self.nc = nc
        self.stack = stack
        self.prog = {e: [] for e in ENGINES}
        self.last_wait = {e: {} for e in ENGINES}
        self.last_real = {e: None for e in ENGINES}
        self.last_inc = {e: None for e in ENGINES}
        self.s_pe = self.sem("s_pe")
        self.s_dve = self.sem("s_dve")
        self.s_act = self.sem("s_act")
        self.s_pool = self.sem("s_pool")
        self.s_dsync = self.sem("s_dsync")
        self.s_dgp = self.sem("s_dgp")
        self.s_bar = self.sem("s_bar")
        self.dma_sem = {"sync": self.s_dsync, "gpsimd": self.s_dgp}
        self.eng_sem = {"tensor": self.s_pe, "vector": self.s_dve,
                        "scalar": self.s_act, "gpsimd": self.s_pool}
        self.slot_sems = {}
        self.dma_used = {"sync": set(), "gpsimd": set()}

    def ssem(self, key):
        if key not in self.slot_sems:
            self.slot_sems[key] = self.sem("sl_" + key)
        return self.slot_sems[key]

    def sem(self, name):
        return Sem(self.stack.enter_context(self.nc.semaphore(name)), name)

    _uid = 0

    def _uniq(self, name):
        Builder._uid += 1
        return f"{name}_{Builder._uid}"

    def sb(self, name, shape, dtype, st=None):
        return (st or self.stack).enter_context(
            self.nc.sbuf_tensor(self._uniq(name), shape, dtype))

    def ps(self, name, shape, dtype=F32, st=None):
        return (st or self.stack).enter_context(
            self.nc.psum_tensor(self._uniq(name), shape, dtype))

    def op(self, eng, fn, waits=(), inc=None, k=1, is_dma=False):
        lw = self.last_wait[eng]
        wl = []
        for s, v in waits:
            if v > 0 and lw.get(s, 0) < v:
                lw[s] = v
                wl.append((s.h, v))
        incs = []
        val = None
        if inc is not None:
            kk = 16 if is_dma else k
            inc.n += kk
            val = inc.n
            incs.append((inc, kk))
        o = _Op(wl, fn, incs)
        self.prog[eng].append(o)
        self.last_real[eng] = o
        if inc is not None:
            self.last_inc[eng] = (inc, val)
        return val

    def dma(self, q, out, in_, waits=(), sem=None):
        if sem is None:
            sem = self.dma_sem[q]
        self.dma_used[q].add(sem)
        return self.op(q, (lambda e, out=out, in_=in_: e.dma_start(out=out, in_=in_)),
                       waits=waits, inc=sem, is_dma=True)

    def wait(self, eng, waits):
        lw = self.last_wait[eng]
        wl = []
        for s, v in waits:
            if v > 0 and lw.get(s, 0) < v:
                lw[s] = v
                wl.append((s.h, v))
        if wl:
            self.prog[eng].append(_Op(wl, None, []))

    def barrier(self):
        for eng in ("sync", "gpsimd"):
            for s in sorted(self.dma_used[eng], key=lambda s: s.name):
                self.wait(eng, [(s, s.n)])
            self.dma_used[eng] = set()
        for eng in ENGINES:
            o = self.last_real[eng]
            if o is None:
                continue
            li = self.last_inc[eng]
            if o.incs or (li is None):
                if li is not None:
                    self.wait(eng, [li])
                self.s_bar.n += 1
                self.prog[eng].append(_Op([], (lambda e: e.nop()), [(self.s_bar, 1)]))
            else:
                if li is not None:
                    self.wait(eng, [li])
                o.incs.append((self.s_bar, 1))
                self.s_bar.n += 1
            self.last_real[eng] = None
        tgt = self.s_bar.n
        for eng in ENGINES:
            self.wait(eng, [(self.s_bar, tgt)])

    def finalize(self, block):
        def runner(eng_name):
            def run(e):
                for o in self.prog[eng_name]:
                    for sh, v in o.waits:
                        e.wait_ge(sh, v)
                    if o.fn is not None:
                        ins = o.fn(e)
                        for s, kk in o.incs:
                            ins.then_inc(s.h, kk)
            return run
        block.sync(runner("sync"))
        block.tensor(runner("tensor"))
        block.vector(runner("vector"))
        block.scalar(runner("scalar"))
        block.gpsimd(runner("gpsimd"))



class MK:
    def __init__(self, nc, stack, chain):
        self.nc = nc
        self.st = stack
        self.chain = chain
        b = self.b = Builder(nc, stack)
        dt = nc.dram_tensor
        ap = lambda *a, **k: dt(*a, **k).ap()
        self.x_in = ap("x_in", [SEQ_PER_CORE, S, D], F32, kind="ExternalInput")
        self.ws = ap("ws", [2, 128, 16 * 128], F32, kind="ExternalInput")
        self.wa = ap("wa", [2, 128, 16 * 128], F32, kind="ExternalInput")
        self.woT = ap("woT", [16, 128, 16 * 128], F32, kind="ExternalInput")
        self.bo = ap("bo", [128, 16], F32, kind="ExternalInput")
        self.qaT = ap("qaT", [12, 128, 16 * 128], F32, kind="ExternalInput")
        self.qalnw = ap("qalnw", [128, 12], F32, kind="ExternalInput")
        self.qbnT = ap("qbnT", [16, 128, 12 * 128], F32, kind="ExternalInput")
        self.qbpT = ap("qbpT", [8, 128, 12 * 128], F32, kind="ExternalInput")
        self.kvaT = ap("kvaT", [4, 128, 16 * 128], F32, kind="ExternalInput")
        self.kvapeT = ap("kvapeT", [1, 128, 16 * 64], F32, kind="ExternalInput")
        self.kvlnw = ap("kvlnw", [128, 4], F32, kind="ExternalInput")
        self.kvbnT = ap("kvbnT", [16, 128, 4 * 128], F32, kind="ExternalInput")
        self.kvbvT = ap("kvbvT", [16, 128, 4 * 128], F32, kind="ExternalInput")
        self.owT = ap("owT", [16, 128, 16 * 128], F32, kind="ExternalInput")
        self.wsrc = {"ws": self.ws, "wa": self.wa, "wo": self.woT, "qa": self.qaT,
                     "qbn": self.qbnT, "qbp": self.qbpT, "kva": self.kvaT,
                     "kvape": self.kvapeT, "kvbn": self.kvbnT, "kvbv": self.kvbvT,
                     "ow": self.owT}
        self.gup = ap("gup", [2 * I_FFN, 256], I32, kind="ExternalInput")
        self.dnp = ap("dnp", [D, 704], I32, kind="ExternalInput")
        self.gusT = ap("gusT", [2 * I_FFN, 16], F32, kind="ExternalInput")
        self.dnsT = ap("dnsT", [D, 44], F32, kind="ExternalInput")
        self.identb = ap("identb", [128, 128], BF16, kind="ExternalInput")
        self.mask = ap("mask", [128, 128], F32, kind="ExternalInput")
        self.ropeqc = ap("ropeqc", [128, S], BF16, kind="ExternalInput")
        self.ropeqs = ap("ropeqs", [128, S], BF16, kind="ExternalInput")
        self.ropekc = ap("ropekc", [64, S], BF16, kind="ExternalInput")
        self.ropeks = ap("ropeks", [64, S], BF16, kind="ExternalInput")
        self.rotq = ap("rotq", [128, 128], BF16, kind="ExternalInput")
        self.rotk = ap("rotk", [64, 64], BF16, kind="ExternalInput")
        self.y = ap("y", [SEQ_PER_CORE, S, D], F32, kind="ExternalOutput")
        self.resid = ap("resid", [SEQ_PER_CORE, S, D], F32, kind="Internal")
        self.wgu_dec = ap("wgu_dec", [GU_OC, 128, 16 * 128], BF16, kind="Internal")
        self.wdn_dec = ap("wdn_dec", [DN_OC, 128, 44 * 128], BF16, kind="Internal")
        self.av_spill = ap("av_spill", [16, 128, S], BF16, kind="Internal")
        self.qpe_spill = ap("qpe_spill", [16, 64, S], BF16, kind="Internal")
        self.t_identb = b.sb("g_identb", [128, 128], BF16)
        self.t_mask = b.sb("g_mask", [128, 128], F32)
        self.t_eps5 = b.sb("g_eps5", [128, 1], F32)
        self.t_eps6 = b.sb("g_eps6", [128, 1], F32)
        self.t_ones = b.sb("g_ones", [1, 128], F32)
        self.t_scb = b.sb("g_scb", [1, 128], F32)
        self.t_trash = b.sb("g_trash", [128, 2048], BF16)
        self.t_rs = b.sb("g_rs", [128, 16], F32)
        self.t_ssq = b.sb("g_ssq", [128, 16], F32)
        self.t_mu = b.sb("g_mu", [128, 16], F32)
        self.t_var = b.sb("g_var", [128, 16], F32)
        self.t_std = b.sb("g_std", [128, 16], F32)
        self.t_rstd = b.sb("g_rstd", [128, 16], F32)
        self.t_nmr = b.sb("g_nmr", [128, 16], F32)
        self.t_onesb = b.sb("g_onesb", [128, 1], BF16)
        self.t_one1 = b.sb("g_one1", [128, 1], F32)
        self.wc_done = {}
        self.wc = {}

    def setup(self):
        b = self.b
        b.dma("sync", self.t_identb[:], self.identb[:])
        b.dma("sync", self.t_mask[:], self.mask[:])
        b.op("vector", lambda e: e.memset(self.t_eps5[:], EPS_LN), inc=b.s_dve)
        b.op("vector", lambda e: e.memset(self.t_eps6[:], EPS_B), inc=b.s_dve)
        b.op("vector", lambda e: e.memset(self.t_ones[:], 1.0), inc=b.s_dve)
        b.op("vector", lambda e: e.memset(self.t_scb[:], SC_B), inc=b.s_dve)
        b.op("vector", lambda e: e.memset(self.t_onesb[:], 1.0), inc=b.s_dve)
        b.op("vector", lambda e: e.memset(self.t_one1[:], 1.0), inc=b.s_dve)
        b.barrier()

    WSRC = None

    def load_w(self, name, oc, dst, sem, waits=()):
        srcap = self.wsrc[name]
        w = srcap.shape[2]
        v = self.b.dma("gpsimd", dst[:, 0:w], srcap[oc, :, :], waits=waits, sem=sem)
        return (sem, v)

    def decode_ffn(self):
        b, nc = self.b, self.nc
        from concourse.ap import AP
        with ExitStack() as st:
            words = [b.sb(f"dc_w{i}", [128, 704], I32, st) for i in range(2)]
            stile = [b.sb(f"dc_s{i}", [128, 44], F32, st) for i in range(2)]
            stil2 = [b.sb(f"dc_s2{i}", [128, 44], F32, st) for i in range(2)]
            t_m6 = [b.sb(f"dc_m6{i}", [128, 704], I32, st) for i in range(2)]
            t_s1 = [b.sb(f"dc_sa{i}", [128, 704], I32, st) for i in range(2)]
            t_s2 = [b.sb(f"dc_sb{i}", [128, 704], I32, st) for i in range(2)]
            t_bits = [b.sb(f"dc_bt{i}", [128, 704], I32, st) for i in range(2)]
            t_cv = [b.sb(f"dc_cv{i}", [128, 1408], F32, st) for i in range(2)]
            t_cv2 = [b.sb(f"dc_c2{i}", [128, 1408], F32, st) for i in range(2)]
            t_srep = [b.sb(f"dc_sr{i}", [128, 1408], F32, st) for i in range(2)]
            dsc = [b.sb(f"dc_dsc{i}", [128, I_FFN], BF16, st) for i in range(2)]
            stage = [b.sb(f"dc_st{i}", [128, 44 * 128], BF16, st) for i in range(2)]
            pst = [b.ps(f"dc_ps{i}", [128, 1024], BF16, st) for i in range(2)]

            jobs = [(self.gup, self.gusT, self.wgu_dec, GU_OC, 256, 16),
                    (self.dnp, self.dnsT, self.wdn_dec, DN_OC, 704, 44)]
            slot_free = {0: None, 1: None}
            stage_free = {0: None, 1: None}
            pst_free = {0: None, 1: None}
            pair_free = [{"m6": 0, "s": 0, "bits": 0, "cv": 0, "cv2": 0, "srep": 0}
                         for _ in range(2)]
            dsc_free = {0: None, 1: None}
            it = 0
            gpp = 0
            for packed, scales, outd, n_ot, wpr, nkc in jobs:
                ng2 = 2 * wpr // 128
                for ot in range(n_ot):
                    sl = it % 2
                    dscb = dsc[sl]
                    s_w = b.ssem(f"dcw{sl}")
                    w_waits = [slot_free[sl]] if slot_free[sl] else []
                    b.dma("sync", words[sl][:, 0:wpr], packed[ot * 128:(ot + 1) * 128, :],
                          waits=w_waits, sem=s_w)
                    vw = b.dma("sync", stile[sl][:, 0:nkc], scales[ot * 128:(ot + 1) * 128, :],
                               waits=[], sem=s_w)
                    pmv = b.op("vector", lambda e, sl=sl, nkc=nkc: e.tensor_scalar(
                        stil2[sl][:, 0:nkc], stile[sl][:, 0:nkc], float(2.0 ** 63), None,
                        ALU.mult), waits=[(s_w, vw)], inc=b.s_dve)
                    last_act = None
                    for pp in range(4):
                        bb = gpp % 2
                        gpp += 1
                        pv = pair_free[bb]
                        net = 4 * pp - 6
                        shs = (ALU.logical_shift_right if net > 0 else ALU.logical_shift_left)
                        mg = b.op("vector", lambda e, sl=sl, bb=bb, wpr=wpr, net=net, shs=shs: e.tensor_scalar(
                            t_m6[bb][:, 0:wpr], words[sl][:, 0:wpr], abs(net), 0x01C001C0,
                            shs, ALU.bitwise_and),
                            waits=([(s_w, vw)] if pp == 0 else [])
                                  + ([(b.s_pool, pv["m6"])] if pv["m6"] else []),
                            inc=b.s_dve)
                        b.op("vector", lambda e, sl=sl, bb=bb, wpr=wpr, pp=pp: e.tensor_scalar(
                            t_s1[bb][:, 0:wpr], words[sl][:, 0:wpr], 4 * pp + 3, 0x00010001,
                            ALU.logical_shift_right, ALU.bitwise_and), inc=b.s_dve)
                        sgv = b.op("vector", lambda e, bb=bb, wpr=wpr: e.tensor_scalar(
                            t_s2[bb][:, 0:wpr], t_s1[bb][:, 0:wpr], 15, None,
                            ALU.logical_shift_left),
                            waits=([(b.s_pool, pv["s"])] if pv["s"] else []), inc=b.s_dve)
                        orv = b.op("vector", lambda e, bb=bb, wpr=wpr: e.tensor_tensor(
                            t_bits[bb][:, 0:wpr], t_m6[bb][:, 0:wpr], t_s2[bb][:, 0:wpr],
                            ALU.bitwise_or), inc=b.s_dve)
                        cvv = b.op("vector", lambda e, bb=bb, wpr=wpr: e.tensor_copy(
                            t_cv[bb][:, 0:2 * wpr], t_bits[bb][:, 0:wpr].bitcast(BF16)),
                            waits=[(b.s_dve, orv)], inc=b.s_dve)
                        g0 = pp * ng2
                        sap = stil2[sl][:, g0:g0 + ng2]
                        rep = AP(sap.tensor, sap.offset, [[44, 128], [1, ng2], [0, 128]])
                        spv = b.op("vector", lambda e, bb=bb, wpr=wpr, ng2=ng2, rep=rep: e.tensor_copy(
                            t_srep[bb][:, 0:2 * wpr].rearrange("p (g r) -> p g r", g=ng2), rep),
                            waits=[(b.s_dve, pmv)], inc=b.s_dve)
                        m1v = b.op("vector", lambda e, bb=bb, wpr=wpr: e.tensor_mul(
                            t_cv2[bb][:, 0:2 * wpr], t_cv[bb][:, 0:2 * wpr],
                            t_srep[bb][:, 0:2 * wpr]),
                            waits=([(b.s_act, pv["cv2"])] if pv["cv2"] else []),
                            inc=b.s_dve)
                        fw = []
                        if pp == 0 and dsc_free[sl] is not None:
                            fw.append(dsc_free[sl])
                        last_act = b.op("scalar", lambda e, bb=bb, wpr=wpr, pp=pp, dscb=dscb: e.activation(
                            dscb[:, pp * 2 * wpr:(pp + 1) * 2 * wpr], t_cv2[bb][:, 0:2 * wpr],
                            AF.Identity, scale=float(2.0 ** 63)),
                            waits=[(b.s_dve, m1v)] + fw, inc=b.s_act)
                        pair_free[bb] = {"m6": 0, "s": 0, "bits": 0, "cv": m1v,
                                         "cv2": last_act, "srep": m1v}
                    slot_free[sl] = (b.s_act, last_act)
                    ssl = it % 2
                    st_waits = [stage_free[ssl]] if stage_free[ssl] else []
                    ev_last = None
                    lastt = None
                    for kg in range(0, nkc, 4):
                        psl = (kg // 4) % 2
                        pe_waits = [(b.s_act, last_act)]
                        if pst_free[psl]:
                            pe_waits.append(pst_free[psl])
                        for kk in range(kg, min(kg + 4, nkc)):
                            lastt = b.op("tensor", lambda e, kk=kk, kg=kg, psl=psl, dscb=dscb: e.transpose(
                                pst[psl][:, (kk - kg) * 128:(kk - kg + 1) * 128],
                                dscb[:, kk * 128:(kk + 1) * 128], self.t_identb[:]),
                                waits=pe_waits, inc=b.s_pe)
                            pe_waits = []
                        nb = min(4, nkc - kg)
                        ev_last = b.op("vector", lambda e, kg=kg, psl=psl, nb=nb, ssl=ssl: e.tensor_copy(
                            stage[ssl][:, kg * 128:(kg + nb) * 128], pst[psl][:, 0:nb * 128]),
                            waits=[(b.s_pe, lastt)] + st_waits, inc=b.s_dve)
                        st_waits = []
                        pst_free[psl] = (b.s_dve, ev_last)
                    s_st = b.ssem(f"dcs{ssl}")
                    dv = b.dma("sync", outd[ot, :, :], stage[ssl][:, 0:nkc * 128],
                               waits=[(b.s_dve, ev_last)], sem=s_st)
                    stage_free[ssl] = (s_st, dv)
                    dsc_free[sl] = (b.s_pe, lastt)
                    it += 1
        b.barrier()

    def ln_pass(self, src_tile_fn, ntiles, xn_sink, rx_keep=None):
        raise NotImplementedError

    def emit_ln(self, src_fn, tiles, rx, rx_sems, rx_free, xn_of, xn_free, eps_tile,
                col0=0):
        b = self.b
        n = len(tiles)
        c0, c1 = col0, col0 + n
        ld = []
        for i, tj in enumerate(tiles):
            wl = [rx_free[i]] if rx_free and rx_free[i] is not None else []
            v = b.dma("sync", rx[i][:], src_fn(tj), waits=wl, sem=rx_sems[i])
            ld.append((rx_sems[i], v))
        for i in range(n):
            b.op("vector", lambda e, i=i, c0=c0: e.reduce_sum(
                self.t_rs[:, c0 + i:c0 + i + 1], rx[i][:], axis=AX.X),
                waits=[ld[i]], inc=b.s_dve)
            b.op("scalar", lambda e, i=i, c0=c0: e.activation(
                self.t_trash[:], rx[i][:], AF.Square, accum_out=self.t_ssq[:, c0 + i:c0 + i + 1]),
                waits=[ld[i]], inc=b.s_act)
        dvev = b.s_dve.n
        actv = b.s_act.n
        b.op("gpsimd", lambda e, c0=c0, c1=c1: e.tensor_scalar(
            self.t_mu[:, c0:c1], self.t_rs[:, c0:c1], 1.0 / D, None, ALU.mult),
            waits=[(b.s_dve, dvev), (b.s_act, actv)], inc=b.s_pool)
        b.op("gpsimd", lambda e, c0=c0, c1=c1: e.tensor_scalar(
            self.t_var[:, c0:c1], self.t_ssq[:, c0:c1], 1.0 / D, None, ALU.mult), inc=b.s_pool)
        b.op("gpsimd", lambda e, c0=c0, c1=c1: e.tensor_mul(
            self.t_std[:, c0:c1], self.t_mu[:, c0:c1], self.t_mu[:, c0:c1]), inc=b.s_pool)
        pv = b.op("gpsimd", lambda e, c0=c0, c1=c1: e.tensor_sub(
            self.t_var[:, c0:c1], self.t_var[:, c0:c1], self.t_std[:, c0:c1]), inc=b.s_pool)
        sv = b.op("scalar", lambda e, c0=c0, c1=c1, eps=eps_tile: e.activation(
            self.t_std[:, c0:c1], self.t_var[:, c0:c1], AF.Sqrt, bias=eps[:], scale=1.0),
            waits=[(b.s_pool, pv)], inc=b.s_act)
        rv = b.op("vector", lambda e, c0=c0, c1=c1: e.reciprocal(
            self.t_rstd[:, c0:c1], self.t_std[:, c0:c1]), waits=[(b.s_act, sv)], inc=b.s_dve)
        b.op("gpsimd", lambda e, c0=c0, c1=c1: e.tensor_mul(
            self.t_nmr[:, c0:c1], self.t_mu[:, c0:c1], self.t_rstd[:, c0:c1]),
            waits=[(b.s_dve, rv)], inc=b.s_pool)
        nv = b.op("gpsimd", lambda e, c0=c0, c1=c1: e.tensor_scalar(
            self.t_nmr[:, c0:c1], self.t_nmr[:, c0:c1], -1.0, None, ALU.mult), inc=b.s_pool)
        xn_vals = []
        for i in range(n):
            wl = [(b.s_pool, nv), (b.s_dve, rv)]
            if xn_free and xn_free[i] is not None:
                wl.append(xn_free[i])
            xn_vals.append(b.op("scalar", lambda e, i=i, c0=c0: e.activation(
                xn_of(i), rx[i][:], AF.Identity, bias=self.t_nmr[:, c0 + i:c0 + i + 1],
                scale=self.t_rstd[:, c0 + i:c0 + i + 1]), waits=wl, inc=b.s_act))
        return ld, xn_vals

    def emit_y_add(self, yT, yT_ready, rx_tile, rx_ld, tt_col, dst_ap, slab, slab_free,
                   rms_col=None):
        b = self.b
        lastt = None
        wl = [yT_ready]
        if slab_free is not None:
            wl.append(slab_free)
        for oc in range(16):
            lastt = b.op("tensor", lambda e, oc=oc: e.transpose(
                slab[:, oc * 128:(oc + 1) * 128],
                yT[:, oc * 512 + tt_col * 128: oc * 512 + (tt_col + 1) * 128],
                self.t_identb[:]), waits=wl, inc=b.s_pe)
            wl = []
        if rms_col is not None:
            i = rms_col
            sqv = b.op("scalar", lambda e, i=i: e.activation(self.t_trash[:], slab[:],
                 AF.Square, accum_out=self.t_ssq[:, i:i + 1]),
                 waits=[(b.s_pe, lastt)], inc=b.s_act)
            sv = b.op("scalar", lambda e, i=i: e.activation(self.t_std[:, i:i + 1],
                 self.t_ssq[:, i:i + 1], AF.Sqrt, bias=self.t_eps6[:], scale=1.0 / D),
                 inc=b.s_act)
            rv = b.op("vector", lambda e, i=i: e.reciprocal(self.t_rstd[:, i:i + 1],
                 self.t_std[:, i:i + 1]), waits=[(b.s_act, sv)], inc=b.s_dve)
            scal = self.t_rstd[:, i:i + 1]
            addw = [(b.s_dve, rv), rx_ld]
        else:
            scal = 1.0
            addw = [(b.s_pe, lastt), rx_ld]
        av = b.op("vector", lambda e, scal=scal: e.scalar_tensor_tensor(
            rx_tile[:], slab[:], scal, rx_tile[:], ALU.mult, ALU.add),
            waits=addw, inc=b.s_dve)
        stv = b.dma("sync", dst_ap, rx_tile[:], waits=[(b.s_dve, av)], sem=rx_ld[0])
        return (rx_ld[0], stv), (b.s_dve, av)

    def ffn(self, seq, src, dst):
        b = self.b
        src_fn = lambda t: src[seq, t * 128:(t + 1) * 128, :]
        dst_fn = lambda t: dst[seq, t * 128:(t + 1) * 128, :]
        with ExitStack() as st:
            rx = [b.sb(f"f_rx{i}", [128, D], F32, st) for i in range(4)]
            xt = [b.sb(f"f_xt{i}", [128, D], BF16, st) for i in range(4)]
            xnT = b.sb("f_xnT", [128, 16 * 512], BF16, st)
            hT = b.sb("f_hT", [128, 44 * 512], BF16, st)
            yT = b.sb("f_yT", [128, 16 * 512], BF16, st)
            sil = [b.sb(f"f_sil{i}", [128, 512], BF16, st) for i in range(2)]
            wg = [b.sb(f"f_wg{i}", [128, 16 * 128], BF16, st) for i in range(2)]
            wu = [b.sb(f"f_wu{i}", [128, 16 * 128], BF16, st) for i in range(2)]
            wd = [b.sb(f"f_wd{i}", [128, 44 * 128], BF16, st) for i in range(2)]
            psm = [b.ps(f"f_ps{i}", [128, 512], F32, st) for i in range(4)]
            psbt = [b.ps(f"f_pt{i}", [128, 1024], BF16, st) for i in range(2)]
            slab = [b.ps("f_sl0", [128, 2048], BF16, st)]

            rx_free = [None] * 4
            xn_free = [None] * 4
            wg_free = [None, None]
            wu_free = [None, None]
            wd_free = [None, None]
            psm_free = [None] * 4
            psbt_free = [None, None]
            slab_free = [None]
            sil_free = [None, None]

            rx_sems = [b.ssem(f"frx{i}") for i in range(4)]
            for tb in range(4):
                tiles = [tb * 4 + i for i in range(4)]
                ld, xnv = self.emit_ln(src_fn, tiles, rx, rx_sems, rx_free,
                                       lambda i: xt[i][:], xn_free, self.t_eps5)
                xnT_last = None
                for kc in range(16):
                    psl = kc % 2
                    for i in range(4):
                        wl = [(b.s_act, xnv[i])]
                        if i == 0 and psbt_free[psl] is not None:
                            wl.append(psbt_free[psl])
                        lastt = b.op("tensor", lambda e, kc=kc, i=i, psl=psl: e.transpose(
                            psbt[psl][:, i * 128:(i + 1) * 128],
                            xt[i][:, kc * 128:(kc + 1) * 128], self.t_identb[:]),
                            waits=wl, inc=b.s_pe)
                    xnT_last = b.op("vector", lambda e, kc=kc, psl=psl: e.tensor_copy(
                        xnT[:, kc * 512:(kc + 1) * 512], psbt[psl][:, 0:512]),
                        waits=[(b.s_pe, lastt)], inc=b.s_dve)
                    psbt_free[psl] = (b.s_dve, xnT_last)
                for i in range(4):
                    xn_free[i] = (b.s_dve, xnT_last)

                for pc in range(44):
                    sl = pc % 2
                    s_g = b.ssem(f"fwg{sl}")
                    s_u = b.ssem(f"fwu{sl}")
                    wl = [wg_free[sl]] if wg_free[sl] is not None else []
                    gv = b.dma("sync", wg[sl][:], self.wgu_dec[pc, :, :],
                               waits=wl, sem=s_g)
                    wl = [wu_free[sl]] if wu_free[sl] is not None else []
                    uv = b.dma("sync", wu[sl][:], self.wgu_dec[pc + 44, :, :],
                               waits=wl, sem=s_u)
                    pa = (2 * pc) % 4
                    pb = (2 * pc + 1) % 4
                    wlA = [(s_g, gv), (b.s_dve, xnT_last)]
                    if psm_free[pa] is not None:
                        wlA.append(psm_free[pa])
                    lastA = None
                    for kc in range(16):
                        lastA = b.op("tensor", lambda e, kc=kc, sl=sl, pa=pa: e.matmul(
                            psm[pa][:], wg[sl][:, kc * 128:(kc + 1) * 128],
                            xnT[:, kc * 512:(kc + 1) * 512], start=(kc == 0), stop=(kc == 15)),
                            waits=wlA, inc=(b.s_pe if kc == 15 else None))
                        wlA = []
                    wg_free[sl] = (b.s_pe, lastA)
                    wlB = [(s_u, uv)]
                    if psm_free[pb] is not None:
                        wlB.append(psm_free[pb])
                    lastB = None
                    for kc in range(16):
                        lastB = b.op("tensor", lambda e, kc=kc, sl=sl, pb=pb: e.matmul(
                            psm[pb][:], wu[sl][:, kc * 128:(kc + 1) * 128],
                            xnT[:, kc * 512:(kc + 1) * 512], start=(kc == 0), stop=(kc == 15)),
                            waits=wlB, inc=(b.s_pe if kc == 15 else None))
                        wlB = []
                    wu_free[sl] = (b.s_pe, lastB)
                    ssl = pc % 2
                    wl = [(b.s_pe, lastA)]
                    if sil_free[ssl] is not None:
                        wl.append(sil_free[ssl])
                    sv = b.op("scalar", lambda e, ssl=ssl, pa=pa: e.activation(
                        sil[ssl][:], psm[pa][:], AF.Silu), waits=wl, inc=b.s_act)
                    psm_free[pa] = (b.s_act, sv)
                    hv = b.op("vector", lambda e, pc=pc, pb=pb, ssl=ssl: e.scalar_tensor_tensor(
                        hT[:, pc * 512:(pc + 1) * 512], psm[pb][:], 1.0, sil[ssl][:],
                        ALU.mult, ALU.mult), waits=[(b.s_pe, lastB), (b.s_act, sv)], inc=b.s_dve)
                    psm_free[pb] = (b.s_dve, hv)
                    sil_free[ssl] = (b.s_dve, hv)
                h_done = b.s_dve.n

                for oc in range(16):
                    sl = oc % 2
                    s_d = b.ssem(f"fwd{sl}")
                    wl = [wd_free[sl]] if wd_free[sl] is not None else []
                    dv = b.dma("sync", wd[sl][:], self.wdn_dec[oc, :, :],
                               waits=wl, sem=s_d)
                    pp = oc % 4
                    wl = [(s_d, dv), (b.s_dve, h_done)]
                    if psm_free[pp] is not None:
                        wl.append(psm_free[pp])
                    last = None
                    for kc in range(44):
                        last = b.op("tensor", lambda e, kc=kc, sl=sl, pp=pp: e.matmul(
                            psm[pp][:], wd[sl][:, kc * 128:(kc + 1) * 128],
                            hT[:, kc * 512:(kc + 1) * 512], start=(kc == 0), stop=(kc == 43)),
                            waits=wl, inc=(b.s_pe if kc == 43 else None))
                        wl = []
                    wd_free[sl] = (b.s_pe, last)
                    ev = b.op("vector", lambda e, oc=oc, pp=pp: e.tensor_copy(
                        yT[:, oc * 512:(oc + 1) * 512], psm[pp][:]),
                        waits=[(b.s_pe, last)], inc=b.s_dve)
                    psm_free[pp] = (b.s_dve, ev)
                yT_done = b.s_dve.n

                for i in range(4):
                    stv, sfree = self.emit_y_add(yT, (b.s_dve, yT_done), rx[i], ld[i], i,
                                                 dst_fn(tiles[i]), slab[0][:],
                                                 slab_free[0], rms_col=8 + i)
                    slab_free[0] = sfree
                    rx_free[i] = stv
        b.barrier()

    def build(self):
        self.setup()
        need_ffn = any(p[0] == "ffn" for p in self.chain)
        if need_ffn:
            self.decode_ffn()
        for idx, ph in enumerate(self.chain):
            kind, seq = ph[0], ph[1]
            first = (idx == 0) or all(p[1] != seq for p in self.chain[:idx])
            last = (idx == len(self.chain) - 1) or all(p[1] != seq for p in self.chain[idx + 1:])
            src = self.x_in if first else self.resid
            dst = self.y if last else self.resid
            if kind == "ffn":
                self.ffn(seq, src, dst)
            elif kind == "attn_a":
                self.attn_a(seq, src, dst)
            elif kind == "attn_b":
                self.attn_b(seq, src, dst)
            else:
                raise ValueError(kind)
        self.b.barrier()

    def attn_a(self, seq, src, dst):
        b = self.b
        src_fn = lambda t: src[seq, t * 128:(t + 1) * 128, :]
        dst_fn = lambda t: dst[seq, t * 128:(t + 1) * 128, :]
        with ExitStack() as st:
            xn = b.sb("a_xn", [128, NT * D], BF16, st)
            rx = [b.sb(f"a_rx{i}", [128, D], F32, st) for i in range(2)]
            wsb = b.sb("a_ws", [128, 2 * 2048], BF16, st)
            wab = b.sb("a_wa", [128, 2 * 2048], BF16, st)
            bo_sb = b.sb("a_bo", [128, 16], F32, st)
            stT = b.sb("a_st", [128, 2 * S], BF16, st)
            acT = b.sb("a_ac", [128, 2 * S], BF16, st)
            cbias = b.sb("a_cb", [128, 1], F32, st)
            s_aw = b.ssem("aw")
            s_awb = b.ssem("awb")
            for oc in range(2):
                self.load_w("ws", oc, wsb[:, oc * 2048:(oc + 1) * 2048], s_aw)
                self.load_w("wa", oc, wab[:, oc * 2048:(oc + 1) * 2048], s_aw)
            bo_v = b.dma("sync", bo_sb[:], self.bo[:], sem=s_awb)
            bo_ld = (s_awb, bo_v)
            aw_ld = (s_aw, s_aw.n)
            with ExitStack() as st1:
                xnT = b.sb("a_xnT", [128, 16 * 512], BF16, st1)
                psbt = [b.ps(f"a_pt{i}", [128, 1024], BF16, st1) for i in range(2)]
                psv = [b.ps(f"a_pv{i}", [128, 512], F32, st1) for i in range(2)]
                rx_free = [None, None]
                psbt_free = [None, None]
                psv_free = [None, None]
                rx_sems = [b.ssem("arx0"), b.ssem("arx1")]
                for tb in range(4):
                    xn_vals = []
                    for gg in range(2):
                        tiles = [tb * 4 + gg * 2, tb * 4 + gg * 2 + 1]
                        ld, xv = self.emit_ln(src_fn, tiles, rx, rx_sems, rx_free,
                                              lambda i, tiles=tiles: xn[:, tiles[i] * D:(tiles[i] + 1) * D],
                                              None, self.t_eps5, col0=gg * 2)
                        xn_vals += xv
                        rx_free = [(b.s_act, xv[0]), (b.s_act, xv[1])]
                    xnT_last = None
                    for kc in range(16):
                        psl = kc % 2
                        for i in range(4):
                            t = tb * 4 + i
                            wl = [(b.s_act, xn_vals[i])]
                            if i == 0 and psbt_free[psl] is not None:
                                wl.append(psbt_free[psl])
                            lastt = b.op("tensor", lambda e, t=t, kc=kc, i=i, psl=psl: e.transpose(
                                psbt[psl][:, i * 128:(i + 1) * 128],
                                xn[:, t * D + kc * 128: t * D + (kc + 1) * 128],
                                self.t_identb[:]), waits=wl, inc=b.s_pe)
                        xnT_last = b.op("vector", lambda e, kc=kc, psl=psl: e.tensor_copy(
                            xnT[:, kc * 512:(kc + 1) * 512], psbt[psl][:, 0:512]),
                            waits=[(b.s_pe, lastt)], inc=b.s_dve)
                        psbt_free[psl] = (b.s_dve, xnT_last)
                    for wbuf, dT, sc in ((wsb, stT, 1.0 / 16), (wab, acT, None)):
                        for m in range(2):
                            pi = m % 2
                            wl = [(b.s_dve, xnT_last), aw_ld]
                            if psv_free[pi] is not None:
                                wl.append(psv_free[pi])
                            last = None
                            for kc in range(16):
                                last = b.op("tensor", lambda e, kc=kc, m=m, pi=pi, wbuf=wbuf: e.matmul(
                                    psv[pi][:], wbuf[:, m * 2048 + kc * 128: m * 2048 + (kc + 1) * 128],
                                    xnT[:, kc * 512:(kc + 1) * 512], start=(kc == 0), stop=(kc == 15)),
                                    waits=wl, inc=(b.s_pe if kc == 15 else None))
                                wl = []
                            if sc is None:
                                ev = b.op("vector", lambda e, m=m, tb=tb, pi=pi, dT=dT: e.tensor_copy(
                                    dT[:, m * S + tb * 512: m * S + (tb + 1) * 512], psv[pi][:]),
                                    waits=[(b.s_pe, last)], inc=b.s_dve)
                            else:
                                ev = b.op("vector", lambda e, m=m, tb=tb, pi=pi, dT=dT, sc=sc: e.tensor_scalar(
                                    dT[:, m * S + tb * 512: m * S + (tb + 1) * 512], psv[pi][:],
                                    sc, None, ALU.mult), waits=[(b.s_pe, last)], inc=b.s_dve)
                            psv_free[pi] = (b.s_dve, ev)
                b.barrier()
            with ExitStack() as st2:
                sqb = [b.sb(f"a_sq{i}", [128, D], BF16, st2) for i in range(2)]
                nmx = b.sb("a_nmx", [1, 16], F32, st2)
                ctmp = b.sb("a_ctmp", [1, 4], F32, st2)
                psn = b.ps("a_psn", [1, 512], F32, st2)
                psc = b.ps("a_psc", [128, 1], F32, st2)
                for sidx, TT in enumerate((stT, acT)):
                    sq_vals = []
                    sq_w = [(b.s_pe, b.s_pe.n)] if sidx > 0 else []
                    for c in range(2):
                        sq_vals.append(b.op("scalar", lambda e, c=c, sidx=sidx, TT=TT: e.activation(
                            sqb[c][:], TT[:, c * S:(c + 1) * S], AF.Square), waits=sq_w, inc=b.s_act))
                        sq_w = []
                    psn_free = None
                    for blk in range(4):
                        wl = [(b.s_act, sq_vals[1])]
                        if psn_free is not None:
                            wl.append(psn_free)
                        last = None
                        for c in range(2):
                            last = b.op("tensor", lambda e, c=c, blk=blk: e.matmul(
                                psn[0:1, :], self.t_onesb[:], sqb[c][:, blk * 512:(blk + 1) * 512],
                                start=(c == 0), stop=(c == 1)), waits=wl, inc=(b.s_pe if c == 1 else None))
                            wl = []
                        rv = b.op("vector", lambda e, sidx=sidx, blk=blk: e.reduce_max(
                            nmx[0:1, sidx * 4 + blk: sidx * 4 + blk + 1], psn[0:1, :], axis=AX.X),
                            waits=[(b.s_pe, last)], inc=b.s_dve)
                        psn_free = (b.s_dve, rv)
                for sidx in range(2):
                    b.op("vector", lambda e, sidx=sidx: e.reduce_max(
                        ctmp[0:1, sidx:sidx + 1], nmx[0:1, sidx * 4:sidx * 4 + 4], axis=AX.X),
                        inc=b.s_dve)
                dv = b.s_dve.n
                pv = b.op("gpsimd", lambda e: e.tensor_mul(ctmp[0:1, 2:3], ctmp[0:1, 0:1],
                     ctmp[0:1, 1:2]), waits=[(b.s_dve, dv)], inc=b.s_pool)
                sv = b.op("scalar", lambda e: e.activation(ctmp[0:1, 3:4], ctmp[0:1, 2:3],
                     AF.Sqrt), waits=[(b.s_pool, pv)], inc=b.s_act)
                nv = b.op("gpsimd", lambda e: e.tensor_scalar(ctmp[0:1, 2:3], ctmp[0:1, 3:4],
                     -1.0, None, ALU.mult), waits=[(b.s_act, sv)], inc=b.s_pool)
                mv = b.op("tensor", lambda e: e.matmul(psc[:, 0:1], self.t_ones[:],
                     ctmp[0:1, 2:3], start=True, stop=True), waits=[(b.s_pool, nv)], inc=b.s_pe)
                b.op("vector", lambda e: e.tensor_copy(cbias[:], psc[:, 0:1]),
                     waits=[(b.s_pe, mv)], inc=b.s_dve)
                b.barrier()
            with ExitStack() as st3:
                p4 = b.sb("a_p4", [128, 4 * S], BF16, st3)
                PT = b.sb("a_PT", [128, 16 * 512], BF16, st3)
                avT = b.sb("a_av", [128, 16 * 512], BF16, st3)
                yT = b.sb("a_yT", [128, 16 * 512], BF16, st3)
                wob = [b.sb(f"a_wo{i}", [128, 16 * 128], BF16, st3) for i in range(2)]
                asum = b.sb("a_as", [128, 16], F32, st3)
                atot = b.sb("a_at", [128, 4], F32, st3)
                arcp = b.sb("a_ar", [128, 4], F32, st3)
                pss = [b.ps(f"a_ss{i}", [128, 512], F32, st3) for i in range(2)]
                pst2 = [b.ps(f"a_pt2{i}", [128, 1024], BF16, st3) for i in range(2)]
                psa = [b.ps(f"a_pa{i}", [128, 512], F32, st3) for i in range(2)]
                slab = b.ps("a_slab", [128, 2048], BF16, st3)
                pss_free = [None, None]
                pst2_free = [None, None]
                psa_free = [None, None]
                wob_free = [None, None]
                slab_free = None
                rx_free = [None, None]
                sum_done = 0
                for qb in range(4):
                    ssi = 0
                    exp_last = [0] * 4
                    for qt in range(4):
                        qtg = qb * 4 + qt
                        klen = (qtg + 1) * 128
                        nblk = (klen + 511) // 512
                        for blk in range(nblk):
                            w = min(512, klen - blk * 512)
                            pi = ssi % 2
                            ssi += 1
                            wl = [pss_free[pi]] if pss_free[pi] is not None else []
                            last = None
                            for m in range(2):
                                last = b.op("tensor", lambda e, m=m, qtg=qtg, blk=blk, w=w, pi=pi: e.matmul(
                                    pss[pi][:, 0:w], stT[:, m * S + qtg * 128: m * S + (qtg + 1) * 128],
                                    acT[:, m * S + blk * 512: m * S + blk * 512 + w],
                                    start=(m == 0), stop=(m == 1)), waits=wl,
                                    inc=(b.s_pe if m == 1 else None))
                                wl = []
                            expw = [(b.s_pe, last)]
                            if blk == nblk - 1:
                                mkv = b.op("vector", lambda e, w=w, pi=pi: e.tensor_tensor(
                                    pss[pi][:, w - 128:w], pss[pi][:, w - 128:w], self.t_mask[:],
                                    ALU.add), waits=[(b.s_pe, last)], inc=b.s_dve)
                                expw = [(b.s_dve, mkv)]
                            ev = b.op("scalar", lambda e, qt=qt, blk=blk, w=w, pi=pi: e.activation(
                                p4[:, qt * S + blk * 512: qt * S + blk * 512 + w], pss[pi][:, 0:w],
                                AF.Exp, bias=cbias[:], scale=1.0,
                                accum_out=asum[:, qt * 4 + blk: qt * 4 + blk + 1]),
                                waits=expw + [(b.s_dve, sum_done)], inc=b.s_act)
                            exp_last[qt] = ev
                            pss_free[pi] = (b.s_act, ev)
                    for qt in range(4):
                        qtg = qb * 4 + qt
                        nblk = ((qtg + 1) * 128 + 511) // 512
                        b.op("vector", lambda e, qt=qt, nblk=nblk: e.reduce_sum(
                            atot[:, qt:qt + 1], asum[:, qt * 4: qt * 4 + nblk], axis=AX.X),
                            waits=[(b.s_act, exp_last[qt])], inc=b.s_dve)
                    for qt in range(4):
                        b.op("vector", lambda e, qt=qt: e.reciprocal(
                            arcp[:, qt:qt + 1], atot[:, qt:qt + 1]), inc=b.s_dve)
                    pm_last = None
                    for qt in range(4):
                        qtg = qb * 4 + qt
                        klen = (qtg + 1) * 128
                        pm_last = b.op("vector", lambda e, qt=qt, klen=klen: e.tensor_scalar(
                            p4[:, qt * S: qt * S + klen], p4[:, qt * S: qt * S + klen],
                            arcp[:, qt:qt + 1], None, ALU.mult), inc=b.s_dve)
                    sum_done = pm_last
                    kmax = (qb + 1) * 4
                    ptev = None
                    for kc in range(kmax):
                        psl = kc % 2
                        wl = [(b.s_dve, pm_last)]
                        if pst2_free[psl] is not None:
                            wl.append(pst2_free[psl])
                        lastt = None
                        for qt in range(4):
                            if qb * 4 + qt < kc:
                                continue
                            lastt = b.op("tensor", lambda e, qt=qt, kc=kc, psl=psl: e.transpose(
                                pst2[psl][:, qt * 128:(qt + 1) * 128],
                                p4[:, qt * S + kc * 128: qt * S + (kc + 1) * 128],
                                self.t_identb[:]), waits=wl, inc=b.s_pe)
                            wl = []
                        q0 = max(0, kc - qb * 4)
                        if q0 > 0:
                            b.op("vector", lambda e, kc=kc, q0=q0: e.memset(
                                PT[:, kc * 512: kc * 512 + q0 * 128], 0.0), inc=b.s_dve)
                        ptev = b.op("vector", lambda e, kc=kc, psl=psl, q0=q0: e.tensor_copy(
                            PT[:, kc * 512 + q0 * 128:(kc + 1) * 512],
                            pst2[psl][:, q0 * 128:512]),
                            waits=[(b.s_pe, lastt)], inc=b.s_dve)
                        pst2_free[psl] = (b.s_dve, ptev)
                    for vc in range(16):
                        pi = vc % 2
                        wl = [(b.s_dve, ptev)]
                        if psa_free[pi] is not None:
                            wl.append(psa_free[pi])
                        last = None
                        for kc in range(kmax):
                            last = b.op("tensor", lambda e, kc=kc, vc=vc, pi=pi,
                                        sp_=(kc == kmax - 1): e.matmul(
                                psa[pi][:], xn[:, kc * D + vc * 128: kc * D + (vc + 1) * 128],
                                PT[:, kc * 512: (kc + 1) * 512],
                                start=(kc == 0), stop=sp_), waits=wl,
                                inc=(b.s_pe if kc == kmax - 1 else None))
                            wl = []
                        av = b.op("vector", lambda e, vc=vc, pi=pi: e.tensor_copy(
                            avT[:, vc * 512:(vc + 1) * 512], psa[pi][:]),
                            waits=[(b.s_pe, last)], inc=b.s_dve)
                        psa_free[pi] = (b.s_dve, av)
                    av_done = b.s_dve.n
                    for oc in range(16):
                        sl = oc % 2
                        wl = [wob_free[sl]] if wob_free[sl] is not None else []
                        wv = self.load_w("wo", oc, wob[sl], b.ssem(f"awo{sl}"), waits=wl)
                        pi = oc % 2
                        wl = [wv, (b.s_dve, av_done)]
                        if psa_free[pi] is not None:
                            wl.append(psa_free[pi])
                        last = None
                        for hc in range(16):
                            last = b.op("tensor", lambda e, hc=hc, sl=sl, pi=pi: e.matmul(
                                psa[pi][:], wob[sl][:, hc * 128:(hc + 1) * 128],
                                avT[:, hc * 512:(hc + 1) * 512], start=(hc == 0), stop=(hc == 15)),
                                waits=wl, inc=(b.s_pe if hc == 15 else None))
                            wl = []
                        wob_free[sl] = (b.s_pe, last)
                        yv = b.op("vector", lambda e, oc=oc, pi=pi: e.tensor_scalar(
                            yT[:, oc * 512:(oc + 1) * 512], psa[pi][:], bo_sb[:, oc:oc + 1],
                            None, ALU.add), waits=[(b.s_pe, last), bo_ld], inc=b.s_dve)
                        psa_free[pi] = (b.s_dve, yv)
                    yT_done = b.s_dve.n
                    for i in range(4):
                        t = qb * 4 + i
                        ri = i % 2
                        wl = [rx_free[ri]] if rx_free[ri] is not None else []
                        rxl = b.dma("sync", rx[ri][:], src_fn(t), waits=wl, sem=rx_sems[ri])
                        stv, sfree = self.emit_y_add(yT, (b.s_dve, yT_done), rx[ri],
                                                     (rx_sems[ri], rxl), i,
                                                     dst_fn(t), slab[:], slab_free, rms_col=None)
                        slab_free = sfree
                        rx_free[ri] = stv
        b.barrier()

    def attn_b(self, seq, src, dst):
        stages = int(os.environ.get("MK_B_STAGES", "4"))
        b = self.b
        src_fn = lambda t: src[seq, t * 128:(t + 1) * 128, :]
        dst_fn = lambda t: dst[seq, t * 128:(t + 1) * 128, :]
        with ExitStack() as st:
            qanT = b.sb("b_qanT", [128, 12 * S], BF16, st)
            ckvnT = b.sb("b_ckvnT", [128, 4 * S], BF16, st)
            kpeT = b.sb("b_kpeT", [64, S], BF16, st)
            with ExitStack() as st1:
                rx = [b.sb(f"b_rx{i}", [128, D], F32, st1) for i in range(2)]
                xt4 = [b.sb(f"b_xt{i}", [128, D], BF16, st1) for i in range(4)]
                xnT = b.sb("b_xnT", [128, 16 * 512], BF16, st1)
                qaw = [b.sb(f"b_qaw{i}", [128, 16 * 128], BF16, st1) for i in range(2)]
                kvaw = b.sb("b_kvaw", [128, 4 * 2048], BF16, st1)
                kvpw = b.sb("b_kvpw", [128, 16 * 64], BF16, st1)
                qalnw_sb = b.sb("b_qalnw", [128, 12], F32, st1)
                kvlnw_sb = b.sb("b_kvlnw", [128, 4], F32, st1)
                sqq = b.sb("b_sqq", [128, 12 * 512], BF16, st1)
                sqk = b.sb("b_sqk", [128, 4 * 512], BF16, st1)
                qaraw = b.sb("b_qaraw", [128, 12 * 512], BF16, st1)
                kvraw = b.sb("b_kvraw", [128, 4 * 512], BF16, st1)
                rrow = b.sb("b_rrow", [1, 512], F32, st1)
                rkc = b.sb("b_rkc", [64, S], BF16, st1)
                rks = b.sb("b_rks", [64, S], BF16, st1)
                rotk_sb = b.sb("b_rotk", [64, 64], BF16, st1)
                t1b = b.sb("b_t1", [128, 512], BF16, st1)
                t2b = b.sb("b_t2", [128, 512], BF16, st1)
                psq = [b.ps(f"b_pq{i}", [128, 512], F32, st1) for i in range(2)]
                psbt = [b.ps(f"b_pt{i}", [128, 1024], BF16, st1) for i in range(2)]
                psr = b.ps("b_psr", [1, 512], F32, st1)
                psrep = b.ps("b_psrep", [128, 512], F32, st1)
                psro = b.ps("b_psro", [64, 512], F32, st1)
                s_kw = b.ssem("bkw")
                s_kt = b.ssem("bkt")
                self.load_w("kva", 0, kvaw[:, 0:2048], s_kw)
                self.load_w("kva", 1, kvaw[:, 2048:4096], s_kw)
                self.load_w("kva", 2, kvaw[:, 4096:6144], s_kw)
                self.load_w("kva", 3, kvaw[:, 6144:8192], s_kw)
                self.load_w("kvape", 0, kvpw, s_kw)
                b.dma("sync", qalnw_sb[:], self.qalnw[:], sem=s_kt)
                b.dma("sync", kvlnw_sb[:], self.kvlnw[:], sem=s_kt)
                b.dma("sync", rkc[:], self.ropekc[:], sem=s_kt)
                b.dma("sync", rks[:], self.ropeks[:], sem=s_kt)
                b.dma("sync", rotk_sb[:], self.rotk[:], sem=s_kt)
                kv_ld = (s_kw, s_kw.n)
                kt_ld = (s_kt, s_kt.n)
                rx_sems = [b.ssem("brx0"), b.ssem("brx1")]
                rx_free = [None, None]
                psbt_free = [None, None]
                psq_free = [None, None]
                qaw_free = [None, None]
                s_qw = [b.ssem("bqw0"), b.ssem("bqw1")]
                for stt_ in range(4):
                    xn_vals = []
                    for gg in range(2):
                        tiles = [stt_ * 4 + gg * 2, stt_ * 4 + gg * 2 + 1]
                        ld, xv = self.emit_ln(src_fn, tiles, rx, rx_sems, rx_free,
                                              lambda i, gg=gg: xt4[gg * 2 + i][:],
                                              None, self.t_eps5, col0=gg * 2)
                        xn_vals += xv
                        rx_free = [(b.s_act, xv[0]), (b.s_act, xv[1])]
                    xnT_last = None
                    for kc in range(16):
                        psl = kc % 2
                        for i in range(4):
                            wl = [(b.s_act, xn_vals[i])]
                            if i == 0 and psbt_free[psl] is not None:
                                wl.append(psbt_free[psl])
                            lastt = b.op("tensor", lambda e, kc=kc, i=i, psl=psl: e.transpose(
                                psbt[psl][:, i * 128:(i + 1) * 128],
                                xt4[i][:, kc * 128:(kc + 1) * 128],
                                self.t_identb[:]), waits=wl, inc=b.s_pe)
                        xnT_last = b.op("vector", lambda e, kc=kc, psl=psl: e.tensor_copy(
                            xnT[:, kc * 512:(kc + 1) * 512], psbt[psl][:, 0:512]),
                            waits=[(b.s_pe, lastt)], inc=b.s_dve)
                        psbt_free[psl] = (b.s_dve, xnT_last)
                    for m in range(12):
                        sl = m % 2
                        wlw = qaw_free[sl] or []
                        wv = self.load_w("qa", m, qaw[sl], s_qw[sl], waits=wlw)
                        pi = m % 2
                        wl = [wv, (b.s_dve, xnT_last)]
                        wl += psq_free[pi] or []
                        last = None
                        for kc in range(16):
                            last = b.op("tensor", lambda e, kc=kc, sl=sl, pi=pi: e.matmul(
                                psq[pi][:], qaw[sl][:, kc * 128:(kc + 1) * 128],
                                xnT[:, kc * 512:(kc + 1) * 512], start=(kc == 0), stop=(kc == 15)),
                                waits=wl, inc=(b.s_pe if kc == 15 else None))
                            wl = []
                        qaw_free[sl] = [(b.s_pe, last)]
                        ev = b.op("vector", lambda e, m=m, pi=pi: e.tensor_copy(
                            qaraw[:, m * 512:(m + 1) * 512], psq[pi][:]),
                            waits=[(b.s_pe, last)], inc=b.s_dve)
                        b.op("scalar", lambda e, m=m: e.activation(
                            sqq[:, m * 512:(m + 1) * 512], qaraw[:, m * 512:(m + 1) * 512],
                            AF.Square), waits=[(b.s_dve, ev)], inc=b.s_act)
                        psq_free[pi] = [(b.s_dve, ev)]
                    for m in range(4):
                        pi = m % 2
                        wl = [kv_ld, kt_ld, (b.s_dve, xnT_last)]
                        wl += psq_free[pi] or []
                        last = None
                        for kc in range(16):
                            last = b.op("tensor", lambda e, kc=kc, m=m, pi=pi: e.matmul(
                                psq[pi][:], kvaw[:, m * 2048 + kc * 128: m * 2048 + (kc + 1) * 128],
                                xnT[:, kc * 512:(kc + 1) * 512], start=(kc == 0), stop=(kc == 15)),
                                waits=wl, inc=(b.s_pe if kc == 15 else None))
                            wl = []
                        ev = b.op("vector", lambda e, m=m, pi=pi: e.tensor_copy(
                            kvraw[:, m * 512:(m + 1) * 512], psq[pi][:]),
                            waits=[(b.s_pe, last)], inc=b.s_dve)
                        b.op("scalar", lambda e, m=m: e.activation(
                            sqk[:, m * 512:(m + 1) * 512], kvraw[:, m * 512:(m + 1) * 512],
                            AF.Square), waits=[(b.s_dve, ev)], inc=b.s_act)
                        psq_free[pi] = [(b.s_dve, ev)]
                    wl = list(psq_free[0] or [])
                    last = None
                    for kc in range(16):
                        last = b.op("tensor", lambda e, kc=kc: e.matmul(
                            psq[0][0:64, :], kvpw[:, kc * 64:(kc + 1) * 64],
                            xnT[:, kc * 512:(kc + 1) * 512], start=(kc == 0), stop=(kc == 15)),
                            waits=wl, inc=(b.s_pe if kc == 15 else None))
                        wl = []
                    kpv = b.op("vector", lambda e, stt_=stt_: e.tensor_copy(
                        kpeT[0:64, stt_ * 512:(stt_ + 1) * 512], psq[0][0:64, :]),
                        waits=[(b.s_pe, last)], inc=b.s_dve)
                    psq_free[0] = [(b.s_dve, kpv)]
                    sq_done = (b.s_act, b.s_act.n)
                    last = None
                    wl = [sq_done, (b.s_dve, b.s_dve.n)]
                    for m in range(12):
                        last = b.op("tensor", lambda e, m=m: e.matmul(
                            psr[0:1, :], self.t_onesb[:], sqq[:, m * 512:(m + 1) * 512],
                            start=(m == 0), stop=(m == 11)), waits=wl,
                            inc=(b.s_pe if m == 11 else None))
                        wl = []
                    s1 = b.op("scalar", lambda e: e.activation(rrow[0:1, :], psr[0:1, :],
                         AF.Sqrt, bias=self.t_eps6[0:1, :], scale=1.0 / 1536),
                         waits=[(b.s_pe, last)], inc=b.s_act)
                    r1 = b.op("vector", lambda e: e.reciprocal(rrow[0:1, :], rrow[0:1, :]),
                         waits=[(b.s_act, s1)], inc=b.s_dve)
                    m1 = b.op("tensor", lambda e: e.matmul(psrep[:], self.t_scb[:],
                         rrow[0:1, :], start=True, stop=True), waits=[(b.s_dve, r1)], inc=b.s_pe)
                    ap_last = None
                    for m in range(12):
                        ap_last = b.op("vector", lambda e, m=m, stt_=stt_: e.scalar_tensor_tensor(
                            qanT[:, m * S + stt_ * 512: m * S + (stt_ + 1) * 512],
                            qaraw[:, m * 512:(m + 1) * 512], qalnw_sb[:, m:m + 1], psrep[:],
                            ALU.mult, ALU.mult), waits=[(b.s_pe, m1)], inc=b.s_dve)
                    last = None
                    wl = [(b.s_dve, ap_last)]
                    for m in range(4):
                        last = b.op("tensor", lambda e, m=m: e.matmul(
                            psr[0:1, :], self.t_onesb[:], sqk[:, m * 512:(m + 1) * 512],
                            start=(m == 0), stop=(m == 3)), waits=wl,
                            inc=(b.s_pe if m == 3 else None))
                        wl = []
                    s2 = b.op("scalar", lambda e: e.activation(rrow[0:1, :], psr[0:1, :],
                         AF.Sqrt, bias=self.t_eps6[0:1, :], scale=1.0 / 512),
                         waits=[(b.s_pe, last)], inc=b.s_act)
                    r2 = b.op("vector", lambda e: e.reciprocal(rrow[0:1, :], rrow[0:1, :]),
                         waits=[(b.s_act, s2)], inc=b.s_dve)
                    m2 = b.op("tensor", lambda e: e.matmul(psrep[:], self.t_ones[:],
                         rrow[0:1, :], start=True, stop=True), waits=[(b.s_dve, r2)], inc=b.s_pe)
                    for m in range(4):
                        b.op("vector", lambda e, m=m, stt_=stt_: e.scalar_tensor_tensor(
                            ckvnT[:, m * S + stt_ * 512: m * S + (stt_ + 1) * 512],
                            kvraw[:, m * 512:(m + 1) * 512], kvlnw_sb[:, m:m + 1], psrep[:],
                            ALU.mult, ALU.mult), waits=[(b.s_pe, m2)], inc=b.s_dve)
                kpe_done = (b.s_dve, b.s_dve.n)
                psro_free = None
                for blk in range(4):
                    wl = [kpe_done, kt_ld]
                    if psro_free is not None:
                        wl.append(psro_free)
                    rmv = b.op("tensor", lambda e, blk=blk: e.matmul(
                        psro[:], rotk_sb[:], kpeT[0:64, blk * 512:(blk + 1) * 512],
                        start=True, stop=True), waits=wl, inc=b.s_pe)
                    tv1 = b.op("vector", lambda e, blk=blk: e.tensor_tensor(
                        t1b[0:64, :], kpeT[0:64, blk * 512:(blk + 1) * 512],
                        rkc[0:64, blk * 512:(blk + 1) * 512], ALU.mult), inc=b.s_dve)
                    tv2 = b.op("vector", lambda e, blk=blk: e.tensor_tensor(
                        t2b[0:64, :], psro[:],
                        rks[0:64, blk * 512:(blk + 1) * 512], ALU.mult),
                        waits=[(b.s_pe, rmv)], inc=b.s_dve)
                    fv = b.op("vector", lambda e, blk=blk: e.tensor_tensor(
                        kpeT[0:64, blk * 512:(blk + 1) * 512], t1b[0:64, :], t2b[0:64, :],
                        ALU.add), inc=b.s_dve)
                    psro_free = (b.s_dve, fv)
                b.barrier()
            if stages < 2:
                return
            with ExitStack() as st2:
                qstage = b.sb("b_qst", [128, S], BF16, st2)
                qpw = [b.sb(f"b_qpw{i}", [128, 12 * 128], BF16, st2) for i in range(2)]
                rqc = b.sb("b_rqc", [128, S], BF16, st2)
                rqs = b.sb("b_rqs", [128, S], BF16, st2)
                rotq_sb = b.sb("b_rotq", [128, 128], BF16, st2)
                t1q = b.sb("b_t1q", [128, 512], BF16, st2)
                t2q = b.sb("b_t2q", [128, 512], BF16, st2)
                psq2 = [b.ps(f"b2_pq{i}", [128, 512], F32, st2) for i in range(2)]
                psb = [b.ps(f"b2_pb{i}", [128, 512], F32, st2) for i in range(2)]
                s_t = b.ssem("bqt")
                b.dma("sync", rqc[:], self.ropeqc[:], sem=s_t)
                b.dma("sync", rqs[:], self.ropeqs[:], sem=s_t)
                b.dma("sync", rotq_sb[:], self.rotq[:], sem=s_t)
                tab_ld = (s_t, s_t.n)
                s_qp = [b.ssem("bqp0"), b.ssem("bqp1")]
                qpw_free = [None, None]
                psq_free = [None, None]
                psb_free = [None, None]
                s_qpe_st = b.ssem("bqpes")
                qstage_war = None
                for j in range(8):
                    sl = j % 2
                    wlw = qpw_free[sl] or []
                    wv = self.load_w("qbp", j, qpw[sl], s_qp[sl], waits=wlw)
                    stage_vals = []
                    for tblk in range(4):
                        pi = tblk % 2
                        wl = [wv]
                        wl += psq_free[pi] or []
                        if tblk == 0 and qstage_war is not None:
                            wl.append(qstage_war)
                        last = None
                        for kc in range(12):
                            last = b.op("tensor", lambda e, kc=kc, sl=sl, tblk=tblk, pi=pi: e.matmul(
                                psq2[pi][:], qpw[sl][:, kc * 128:(kc + 1) * 128],
                                qanT[:, kc * S + tblk * 512: kc * S + (tblk + 1) * 512],
                                start=(kc == 0), stop=(kc == 11)),
                                waits=wl, inc=(b.s_pe if kc == 11 else None))
                            wl = []
                        evw = [(b.s_pe, last)]
                        if tblk == 0 and qstage_war is not None:
                            evw.append(qstage_war)
                        ev = b.op("vector", lambda e, tblk=tblk, pi=pi: e.tensor_copy(
                            qstage[:, tblk * 512:(tblk + 1) * 512], psq2[pi][:]),
                            waits=evw, inc=b.s_dve)
                        psq_free[pi] = [(b.s_dve, ev)]
                        stage_vals.append(ev)
                    qpw_free[sl] = [(b.s_pe, b.s_pe.n)]
                    rope_last = None
                    for tblk in range(4):
                        psl = tblk % 2
                        wl = [(b.s_dve, stage_vals[tblk]), tab_ld]
                        wl += psb_free[psl] or []
                        rmv = b.op("tensor", lambda e, tblk=tblk, psl=psl: e.matmul(
                            psb[psl][:], rotq_sb[:], qstage[:, tblk * 512:(tblk + 1) * 512],
                            start=True, stop=True), waits=wl, inc=b.s_pe)
                        b.op("vector", lambda e, tblk=tblk: e.tensor_tensor(
                            t1q[:], qstage[:, tblk * 512:(tblk + 1) * 512],
                            rqc[:, tblk * 512:(tblk + 1) * 512], ALU.mult), inc=b.s_dve)
                        b.op("vector", lambda e, tblk=tblk, psl=psl: e.tensor_tensor(
                            t2q[:], psb[psl][:],
                            rqs[:, tblk * 512:(tblk + 1) * 512], ALU.mult),
                            waits=[(b.s_pe, rmv)], inc=b.s_dve)
                        rope_last = b.op("vector", lambda e, tblk=tblk: e.tensor_tensor(
                            qstage[:, tblk * 512:(tblk + 1) * 512], t1q[:], t2q[:],
                            ALU.add), inc=b.s_dve)
                        psb_free[psl] = [(b.s_dve, rope_last)]
                    b.dma("sync", self.qpe_spill[2 * j, :, :], qstage[0:64, :],
                          waits=[(b.s_dve, rope_last)], sem=s_qpe_st)
                    v2 = b.dma("sync", self.qpe_spill[2 * j + 1, :, :], qstage[64:128, :],
                               sem=s_qpe_st)
                    qstage_war = (s_qpe_st, v2)
                b.barrier()
            if stages < 3:
                return
            with ExitStack() as st3:
                knT = b.sb("b_knT", [128, S], BF16, st3)
                Vb = b.sb("b_V", [128, 16 * 128], BF16, st3)
                qnT = b.sb("b_qnT", [128, S], BF16, st3)
                qpeh = [b.sb(f"b_qpeh{i}", [64, S], BF16, st3) for i in range(2)]
                sq4 = b.sb("b_sq4", [128, S], BF16, st3)
                sq4k = b.sb("b_sq4k", [128, S], BF16, st3)
                sqpe = b.sb("b_sqpe", [64, S], BF16, st3)
                sqkpe = b.sb("b_sqkpe", [64, S], BF16, st3)
                p4 = [b.sb(f"b_p4{i}", [128, 4 * S], BF16, st3) for i in range(2)]
                PT = b.sb("b_PT", [128, 16 * 512], BF16, st3)
                avb = b.sb("b_avb", [128, 512], BF16, st3)
                qbnw = [b.sb(f"b_qbnw{i}", [128, 12 * 128], BF16, st3) for i in range(2)]
                kvbnw = [b.sb(f"b_kvbnw{i}", [128, 4 * 128], BF16, st3) for i in range(2)]
                kvbvw = [b.sb(f"b_kvbvw{i}", [128, 4 * 128], BF16, st3) for i in range(2)]
                nmx = b.sb("b_nmx", [1, 16], F32, st3)
                ctmp = b.sb("b_ctmp", [1, 4], F32, st3)
                cbias = b.sb("b_cb", [128, 1], F32, st3)
                asum = b.sb("b_as", [128, 16], F32, st3)
                atot = b.sb("b_at", [128, 4], F32, st3)
                arcp = b.sb("b_ar", [128, 4], F32, st3)
                pss = [b.ps(f"b_ss{i}", [128, 512], F32, st3) for i in range(2)]
                pst2 = [b.ps(f"b_pt2{i}", [128, 1024], BF16, st3) for i in range(2)]
                psa = [b.ps(f"b_pa{i}", [128, 512], F32, st3) for i in range(2)]
                psn = b.ps("b_psn", [1, 512], F32, st3)
                psc = b.ps("b_psc", [128, 1], F32, st3)
                s_bw = [b.ssem("bbw0"), b.ssem("bbw1")]
                s_qh = [b.ssem("bqh0"), b.ssem("bqh1")]
                s_av = b.ssem("bav")
                kq_last = None
                for blk in range(4):
                    kq_last = b.op("scalar", lambda e, blk=blk: e.activation(
                        sqkpe[0:64, blk * 512:(blk + 1) * 512],
                        kpeT[0:64, blk * 512:(blk + 1) * 512], AF.Square), inc=b.s_act)
                bw_free = [None, None]
                qh_free = [None, None]
                pss_free = [None, None]
                pst2_free = [None, None]
                psa_free = [None, None]
                av_pend = None
                for h in range(16):
                    sl = h % 2
                    wlw = bw_free[sl] or []
                    wq = self.load_w("qbn", h, qbnw[sl], s_bw[sl], waits=wlw)
                    wn = self.load_w("kvbn", h, kvbnw[sl], s_bw[sl])
                    wv_ = self.load_w("kvbv", h, kvbvw[sl], s_bw[sl])
                    wload = (s_bw[sl], s_bw[sl].n)
                    wlq = qh_free[sl] or []
                    qpl = b.dma("sync", qpeh[sl][0:64, :], self.qpe_spill[h, :, :],
                                waits=wlq, sem=s_qh[sl])
                    qpe_ld = (s_qh[sl], qpl)
                    kn_vals = []
                    for blk in range(4):
                        pi = blk % 2
                        wl = [wload]
                        wl += pss_free[pi] or []
                        last = None
                        for kc in range(4):
                            last = b.op("tensor", lambda e, kc=kc, blk=blk, pi=pi, sl=sl: e.matmul(
                                pss[pi][:], kvbnw[sl][:, kc * 128:(kc + 1) * 128],
                                ckvnT[:, kc * S + blk * 512: kc * S + (blk + 1) * 512],
                                start=(kc == 0), stop=(kc == 3)),
                                waits=wl, inc=(b.s_pe if kc == 3 else None))
                            wl = []
                        ev = b.op("vector", lambda e, blk=blk, pi=pi: e.tensor_copy(
                            knT[:, blk * 512:(blk + 1) * 512], pss[pi][:]),
                            waits=[(b.s_pe, last)], inc=b.s_dve)
                        sv = b.op("scalar", lambda e, blk=blk: e.activation(
                            sq4k[:, blk * 512:(blk + 1) * 512], knT[:, blk * 512:(blk + 1) * 512],
                            AF.Square), waits=[(b.s_dve, ev)], inc=b.s_act)
                        pss_free[pi] = [(b.s_dve, ev)]
                        kn_vals.append((ev, sv))
                    v_vals = []
                    for kt in range(16):
                        pi = kt % 2
                        wl = [wload]
                        wl += psa_free[pi] or []
                        last = None
                        for kc in range(4):
                            last = b.op("tensor", lambda e, kc=kc, kt=kt, pi=pi, sl=sl: e.matmul(
                                psa[pi][:, 0:128], ckvnT[:, kc * S + kt * 128: kc * S + (kt + 1) * 128],
                                kvbvw[sl][:, kc * 128:(kc + 1) * 128],
                                start=(kc == 0), stop=(kc == 3)),
                                waits=wl, inc=(b.s_pe if kc == 3 else None))
                            wl = []
                        ev = b.op("vector", lambda e, kt=kt, pi=pi: e.tensor_copy(
                            Vb[:, kt * 128:(kt + 1) * 128], psa[pi][:, 0:128]),
                            waits=[(b.s_pe, last)], inc=b.s_dve)
                        psa_free[pi] = [(b.s_dve, ev)]
                        v_vals.append(ev)
                    v_done = b.s_dve.n
                    qn_vals = []
                    for tblk in range(4):
                        pi = tblk % 2
                        wl = [wload]
                        wl += pss_free[pi] or []
                        last = None
                        for kc in range(12):
                            last = b.op("tensor", lambda e, kc=kc, tblk=tblk, pi=pi, sl=sl: e.matmul(
                                pss[pi][:], qbnw[sl][:, kc * 128:(kc + 1) * 128],
                                qanT[:, kc * S + tblk * 512: kc * S + (tblk + 1) * 512],
                                start=(kc == 0), stop=(kc == 11)),
                                waits=wl, inc=(b.s_pe if kc == 11 else None))
                            wl = []
                        ev = b.op("vector", lambda e, tblk=tblk, pi=pi: e.tensor_copy(
                            qnT[:, tblk * 512:(tblk + 1) * 512], pss[pi][:]),
                            waits=[(b.s_pe, last)], inc=b.s_dve)
                        sv = b.op("scalar", lambda e, tblk=tblk: e.activation(
                            sq4[:, tblk * 512:(tblk + 1) * 512], qnT[:, tblk * 512:(tblk + 1) * 512],
                            AF.Square), waits=[(b.s_dve, ev)], inc=b.s_act)
                        pss_free[pi] = [(b.s_dve, ev)]
                        qn_vals.append((ev, sv))
                    qpe_sq = []
                    for blk in range(4):
                        qpe_sq.append(b.op("scalar", lambda e, blk=blk, sl=sl: e.activation(
                            sqpe[0:64, blk * 512:(blk + 1) * 512],
                            qpeh[sl][0:64, blk * 512:(blk + 1) * 512], AF.Square),
                            waits=[qpe_ld], inc=b.s_act))
                    psn_free = None
                    for blk in range(4):
                        wl = [(b.s_act, kn_vals[blk][1]), (b.s_act, kq_last)]
                        if psn_free is not None:
                            wl.append(psn_free)
                        b.op("tensor", lambda e, blk=blk: e.matmul(
                            psn[0:1, :], self.t_onesb[:], sq4k[:, blk * 512:(blk + 1) * 512],
                            start=True, stop=False), waits=wl, inc=None)
                        last = b.op("tensor", lambda e, blk=blk: e.matmul(
                            psn[0:1, :], self.t_onesb[0:64, :], sqkpe[0:64, blk * 512:(blk + 1) * 512],
                            start=False, stop=True), inc=b.s_pe)
                        rv = b.op("vector", lambda e, blk=blk: e.reduce_max(
                            nmx[0:1, 4 + blk:5 + blk], psn[0:1, :], axis=AX.X),
                            waits=[(b.s_pe, last)], inc=b.s_dve)
                        psn_free = (b.s_dve, rv)
                    psn_free2 = psn_free
                    for blk in range(4):
                        wl = [(b.s_act, qn_vals[blk][1]), (b.s_act, qpe_sq[blk])]
                        if psn_free2 is not None:
                            wl.append(psn_free2)
                        b.op("tensor", lambda e, blk=blk: e.matmul(
                            psn[0:1, :], self.t_onesb[:], sq4[:, blk * 512:(blk + 1) * 512],
                            start=True, stop=False), waits=wl, inc=None)
                        last = b.op("tensor", lambda e, blk=blk: e.matmul(
                            psn[0:1, :], self.t_onesb[0:64, :], sqpe[0:64, blk * 512:(blk + 1) * 512],
                            start=False, stop=True), inc=b.s_pe)
                        rv = b.op("vector", lambda e, blk=blk: e.reduce_max(
                            nmx[0:1, blk:blk + 1], psn[0:1, :], axis=AX.X),
                            waits=[(b.s_pe, last)], inc=b.s_dve)
                        psn_free2 = (b.s_dve, rv)
                    b.op("vector", lambda e: e.reduce_max(ctmp[0:1, 1:2], nmx[0:1, 4:8],
                         axis=AX.X), inc=b.s_dve)
                    dv0 = b.op("vector", lambda e: e.reduce_max(ctmp[0:1, 0:1], nmx[0:1, 0:4],
                         axis=AX.X), inc=b.s_dve)
                    pv = b.op("gpsimd", lambda e: e.tensor_mul(ctmp[0:1, 2:3], ctmp[0:1, 0:1],
                         ctmp[0:1, 1:2]), waits=[(b.s_dve, dv0)], inc=b.s_pool)
                    sv = b.op("scalar", lambda e: e.activation(ctmp[0:1, 3:4], ctmp[0:1, 2:3],
                         AF.Sqrt), waits=[(b.s_pool, pv)], inc=b.s_act)
                    nv = b.op("gpsimd", lambda e: e.tensor_scalar(ctmp[0:1, 2:3], ctmp[0:1, 3:4],
                         -1.0, None, ALU.mult), waits=[(b.s_act, sv)], inc=b.s_pool)
                    mv = b.op("tensor", lambda e: e.matmul(psc[:, 0:1], self.t_ones[:],
                         ctmp[0:1, 2:3], start=True, stop=True), waits=[(b.s_pool, nv)], inc=b.s_pe)
                    cbv = b.op("vector", lambda e: e.tensor_copy(cbias[:], psc[:, 0:1]),
                         waits=[(b.s_pe, mv), (b.s_act, b.s_act.n)], inc=b.s_dve)
                    sum_done = cbv
                    pm_of = {}

                    def stage1(qb, sl=sl, qpe_ld=qpe_ld, cbv=cbv):
                        nonlocal sum_done
                        pb = p4[qb % 2][:]
                        ssi = 0
                        exp_last = [0] * 4
                        for qt in range(4):
                            qtg = qb * 4 + qt
                            klen = (qtg + 1) * 128
                            nblk = (klen + 511) // 512
                            for blk in range(nblk):
                                w = min(512, klen - blk * 512)
                                pi = ssi % 2
                                ssi += 1
                                wl = list(pss_free[pi] or [])
                                last = b.op("tensor", lambda e, qtg=qtg, blk=blk, w=w, pi=pi: e.matmul(
                                    pss[pi][:, 0:w], qnT[:, qtg * 128:(qtg + 1) * 128],
                                    knT[:, blk * 512: blk * 512 + w],
                                    start=True, stop=False), waits=wl, inc=None)
                                last = b.op("tensor", lambda e, qtg=qtg, blk=blk, w=w, pi=pi, sl=sl: e.matmul(
                                    pss[pi][:, 0:w], qpeh[sl][0:64, qtg * 128:(qtg + 1) * 128],
                                    kpeT[0:64, blk * 512: blk * 512 + w],
                                    start=False, stop=True), waits=[qpe_ld], inc=b.s_pe)
                                expw = [(b.s_pe, last)]
                                if blk == nblk - 1:
                                    mkv = b.op("vector", lambda e, w=w, pi=pi: e.tensor_tensor(
                                        pss[pi][:, w - 128:w], pss[pi][:, w - 128:w], self.t_mask[:],
                                        ALU.add), waits=[(b.s_pe, last)], inc=b.s_dve)
                                    expw = [(b.s_dve, mkv)]
                                ev = b.op("scalar", lambda e, qt=qt, blk=blk, w=w, pi=pi, pb=pb: e.activation(
                                    pb[:, qt * S + blk * 512: qt * S + blk * 512 + w], pss[pi][:, 0:w],
                                    AF.Exp, bias=cbias[:], scale=1.0,
                                    accum_out=asum[:, qt * 4 + blk: qt * 4 + blk + 1]),
                                    waits=expw + [(b.s_dve, sum_done), (b.s_dve, cbv)], inc=b.s_act)
                                exp_last[qt] = ev
                                pss_free[pi] = [(b.s_act, ev)]
                        for qt in range(4):
                            qtg = qb * 4 + qt
                            nblk = ((qtg + 1) * 128 + 511) // 512
                            b.op("vector", lambda e, qt=qt, nblk=nblk: e.reduce_sum(
                                atot[:, qt:qt + 1], asum[:, qt * 4: qt * 4 + nblk], axis=AX.X),
                                waits=[(b.s_act, exp_last[qt])], inc=b.s_dve)
                        for qt in range(4):
                            b.op("vector", lambda e, qt=qt: e.reciprocal(
                                arcp[:, qt:qt + 1], atot[:, qt:qt + 1]), inc=b.s_dve)
                        pm_last = None
                        for qt in range(4):
                            qtg = qb * 4 + qt
                            klen = (qtg + 1) * 128
                            pm_last = b.op("vector", lambda e, qt=qt, klen=klen, pb=pb: e.tensor_scalar(
                                pb[:, qt * S: qt * S + klen], pb[:, qt * S: qt * S + klen],
                                arcp[:, qt:qt + 1], None, ALU.mult), inc=b.s_dve)
                        sum_done = pm_last
                        pm_of[qb] = pm_last

                    def stage2(qb, h=h, sl=sl):
                        nonlocal av_pend
                        pb = p4[qb % 2][:]
                        pm_last = pm_of[qb]
                        kmax = (qb + 1) * 4
                        ptev = None
                        for kc in range(kmax):
                            psl = kc % 2
                            wl = [(b.s_dve, pm_last)]
                            wl += pst2_free[psl] or []
                            lastt = None
                            for qt in range(4):
                                if qb * 4 + qt < kc:
                                    continue
                                lastt = b.op("tensor", lambda e, qt=qt, kc=kc, psl=psl, pb=pb: e.transpose(
                                    pst2[psl][:, qt * 128:(qt + 1) * 128],
                                    pb[:, qt * S + kc * 128: qt * S + (kc + 1) * 128],
                                    self.t_identb[:]), waits=wl, inc=b.s_pe)
                                wl = []
                            q0 = max(0, kc - qb * 4)
                            if q0 > 0:
                                b.op("vector", lambda e, kc=kc, q0=q0: e.memset(
                                    PT[:, kc * 512: kc * 512 + q0 * 128], 0.0), inc=b.s_dve)
                            ptev = b.op("vector", lambda e, kc=kc, psl=psl, q0=q0: e.tensor_copy(
                                PT[:, kc * 512 + q0 * 128:(kc + 1) * 512],
                                pst2[psl][:, q0 * 128:512]),
                                waits=[(b.s_pe, lastt)], inc=b.s_dve)
                            pst2_free[psl] = [(b.s_dve, ptev)]
                        pi = qb % 2
                        wl = [(b.s_dve, ptev), (b.s_dve, v_done)]
                        wl += psa_free[pi] or []
                        last = None
                        for kc in range(kmax):
                            last = b.op("tensor", lambda e, kc=kc, pi=pi, sp_=(kc == kmax - 1): e.matmul(
                                psa[pi][:], Vb[:, kc * 128:(kc + 1) * 128],
                                PT[:, kc * 512:(kc + 1) * 512],
                                start=(kc == 0), stop=sp_), waits=wl,
                                inc=(b.s_pe if kc == kmax - 1 else None))
                            wl = []
                        avw = [(b.s_pe, last)]
                        if av_pend is not None:
                            avw.append(av_pend)
                        avv = b.op("vector", lambda e, pi=pi: e.tensor_copy(
                            avb[:], psa[pi][:]), waits=avw, inc=b.s_dve)
                        psa_free[pi] = [(b.s_dve, avv)]
                        stv = b.dma("sync", self.av_spill[h, :, qb * 512:(qb + 1) * 512],
                                    avb[:], waits=[(b.s_dve, avv)], sem=s_av)
                        av_pend = (s_av, stv)

                    stage1(0)
                    stage1(1)
                    stage2(0)
                    stage1(2)
                    stage2(1)
                    stage1(3)
                    stage2(2)
                    stage2(3)
                    bw_free[sl] = [(b.s_pe, b.s_pe.n)]
                    qh_free[sl] = [(b.s_pe, b.s_pe.n)]
                b.barrier()
            if stages < 4:
                return
            with ExitStack() as st4:
                avTs = b.sb("b_avTs", [128, 16 * 512], BF16, st4)
                yT = b.sb("b_yT", [128, 16 * 512], BF16, st4)
                owb = [b.sb(f"b_ow{i}", [128, 16 * 128], BF16, st4) for i in range(2)]
                rx3 = [b.sb(f"b3_rx{i}", [128, D], F32, st4) for i in range(2)]
                psa3 = [b.ps(f"b3_pa{i}", [128, 512], F32, st4) for i in range(2)]
                slab = b.ps("b3_slab", [128, 2048], BF16, st4)
                s_avl = [b.ssem("bavl0"), b.ssem("bavl1")]
                rx_sems = [b.ssem("brx0"), b.ssem("brx1")]
                ow_free = [None, None]
                psa_free = [None, None]
                slab_free = None
                rx_free = [None, None]
                avl_free = [None, None]
                for qb in range(4):
                    asl = qb % 2
                    avw = [avl_free[asl]] if avl_free[asl] is not None else []
                    avv = None
                    for hc in range(16):
                        avv = b.dma("sync", avTs[:, hc * 512:(hc + 1) * 512],
                                    self.av_spill[hc, :, qb * 512:(qb + 1) * 512],
                                    waits=avw, sem=s_avl[asl])
                        avw = []
                    av_ld = (s_avl[asl], avv)
                    for oc in range(16):
                        sl = oc % 2
                        wl = [ow_free[sl]] if ow_free[sl] is not None else []
                        wv = self.load_w("ow", oc, owb[sl], b.ssem(f"bow{sl}"), waits=wl)
                        pi = oc % 2
                        wl = [wv, av_ld]
                        if psa_free[pi] is not None:
                            wl.append(psa_free[pi])
                        last = None
                        for hc in range(16):
                            last = b.op("tensor", lambda e, hc=hc, sl=sl, pi=pi: e.matmul(
                                psa3[pi][:], owb[sl][:, hc * 128:(hc + 1) * 128],
                                avTs[:, hc * 512:(hc + 1) * 512], start=(hc == 0), stop=(hc == 15)),
                                waits=wl, inc=(b.s_pe if hc == 15 else None))
                            wl = []
                        ow_free[sl] = (b.s_pe, last)
                        yv = b.op("vector", lambda e, oc=oc, pi=pi: e.tensor_copy(
                            yT[:, oc * 512:(oc + 1) * 512], psa3[pi][:]),
                            waits=[(b.s_pe, last)], inc=b.s_dve)
                        psa_free[pi] = (b.s_dve, yv)
                    yT_done = b.s_dve.n
                    avl_free[asl] = (b.s_pe, b.s_pe.n)
                    for i in range(4):
                        t = qb * 4 + i
                        ri = i % 2
                        wl = [rx_free[ri]] if rx_free[ri] is not None else []
                        rxl = b.dma("sync", rx3[ri][:], src_fn(t), waits=wl, sem=rx_sems[ri])
                        stv, sfree = self.emit_y_add(yT, (b.s_dve, yT_done), rx3[ri],
                                                     (rx_sems[ri], rxl), i,
                                                     dst_fn(t), slab[:], slab_free, rms_col=None)
                        slab_free = sfree
                        rx_free[ri] = stv
        b.barrier()


def build_nc(chain):
    nc = bass.Bass("TRN2", target_bir_lowering=False, detect_race_conditions=False)
    with ExitStack() as stack:
        mk = MK(nc, stack, chain)
        mk.build()
        block = stack.enter_context(nc.Block())
        mk.b.finalize(block)
    return nc


FULL_CHAIN = [(k, s) for s in range(SEQ_PER_CORE)
              for k in ("attn_a", "ffn", "attn_b", "ffn")]


def make_in_maps(inputs, chain=None):
    g = prep_host(inputs)
    x = np.asarray(inputs["L0_attn_a_x"], np.float32)
    in_maps = []
    for c in range(NCORES):
        m = dict(
            ws=g["ws"], wa=g["wa"], woT=g["woT"], bo=g["bo"], qaT=g["qaT"],
            qalnw=g["qalnw"], qbnT=g["qbnT"], qbpT=g["qbpT"], kvaT=g["kvaT"],
            kvlnw=g["kvlnw"], kvbnT=g["kvbnT"], kvbvT=g["kvbvT"], owT=g["owT"],
            kvapeT=g["kvapeT"],
            gup=g["gup"], dnp=g["dnp"], gusT=g["gusT"], dnsT=g["dnsT"],
            identb=g["identb"], mask=g["mask"], ropeqc=g["ropeqc"], ropeqs=g["ropeqs"],
            ropekc=g["ropekc"], ropeks=g["ropeks"], rotq=g["rotq"], rotk=g["rotk"],
            x_in=np.ascontiguousarray(x[c * SEQ_PER_CORE:(c + 1) * SEQ_PER_CORE]),
        )
        in_maps.append(m)
    return in_maps


def kernel(**inputs):
    chain = FULL_CHAIN
    in_maps = make_in_maps(inputs, chain)
    nc = build_nc(chain)
    res = run_bass_kernel_spmd(nc, in_maps, core_ids=list(range(NCORES)))
    out = np.concatenate([np.asarray(r["y"], np.float32) for r in res.results], axis=0)
    return out.reshape(16, S, D)


# version 14
# speedup vs baseline: 1.0064x; 1.0064x over previous
import os
import numpy as np
import ml_dtypes
from contextlib import ExitStack

import concourse.bass as bass
import concourse.mybir as mybir
from concourse.bass_utils import run_bass_kernel_spmd

F32 = mybir.dt.float32
BF16 = mybir.dt.bfloat16
I32 = mybir.dt.int32
AF = mybir.ActivationFunctionType
ALU = mybir.AluOpType
AX = mybir.AxisListType

NCORES = 8
SEQ_PER_CORE = 2
S = 2048
D = 2048
NT = S // 128
I_FFN = 5632
GU_OC = 88
DN_OC = 16
SC_B = float(192 ** -0.5)
EPS_LN = 1e-5
EPS_B = 1e-6
MASK_NEG = -30000.0

ENGINES = ("sync", "tensor", "vector", "scalar", "gpsimd")



def _repack_nibbles(packed: np.ndarray, wpr: int) -> np.ndarray:
    u = packed.view(np.uint32)
    o, w = u.shape
    assert w == wpr
    shifts = (4 * np.arange(8, dtype=np.uint32))[None, None, :]
    nib = ((u[:, :, None] >> shifts) & np.uint32(15)).astype(np.uint8)
    unp = nib.reshape(o, w * 8)
    words = np.zeros((o, wpr), np.uint32)
    for pp in range(4):
        blk = unp[:, pp * 2 * wpr:(pp + 1) * 2 * wpr]
        lo = blk[:, 0::2].astype(np.uint32)
        hi = blk[:, 1::2].astype(np.uint32)
        words |= lo << np.uint32(4 * pp)
        words |= hi << np.uint32(4 * pp + 16)
    return words.view(np.int32)


def _rope_tables():
    inv = (1.0 / (10000.0 ** (np.arange(0, 64, 2, dtype=np.float32) / 64.0))).astype(np.float32)
    t = np.arange(S, dtype=np.float32)
    fr = np.outer(t, inv).astype(np.float32)
    emb = np.concatenate([fr, fr], axis=-1)
    cos = np.cos(emb).astype(np.float32)
    sin = np.sin(emb).astype(np.float32)
    cosT = np.ascontiguousarray(cos.T)
    sinT = np.ascontiguousarray(sin.T)
    qcos = np.concatenate([cosT, cosT], axis=0)
    qsin = np.concatenate([sinT, sinT], axis=0)
    return (qcos.astype(ml_dtypes.bfloat16), qsin.astype(ml_dtypes.bfloat16),
            cosT.astype(ml_dtypes.bfloat16), sinT.astype(ml_dtypes.bfloat16))


def _rot_mat():
    R = np.zeros((64, 64), np.float32)
    for i in range(32):
        R[i, i + 32] = -1.0
        R[i + 32, i] = 1.0
    RT = R.T
    blk = np.zeros((128, 128), np.float32)
    blk[:64, :64] = RT
    blk[64:, 64:] = RT
    return (np.ascontiguousarray(blk).astype(ml_dtypes.bfloat16),
            np.ascontiguousarray(RT).astype(ml_dtypes.bfloat16))


def _tile_w(w, W=128):
    K, M = w.shape
    kc = K // 128
    oc = M // W
    t = w.reshape(kc, 128, oc, W).transpose(2, 1, 0, 3).reshape(oc, 128, kc * W)
    return np.ascontiguousarray(t, dtype=np.float32)


def prep_host(inputs):
    g = {}
    f32 = np.float32
    asc = np.ascontiguousarray
    g["ws"] = _tile_w(np.asarray(inputs["L0_attn_a_Ws"], dtype=f32))
    g["wa"] = _tile_w(np.asarray(inputs["L0_attn_a_Wa"], dtype=f32))
    g["woT"] = _tile_w(np.asarray(inputs["L0_attn_a_Wo"], dtype=f32).T)
    g["bo"] = asc(inputs["L0_attn_a_bo"].reshape(16, 128).T, dtype=f32)
    g["qaT"] = _tile_w(np.asarray(inputs["L0_attn_b_q_a_w"], dtype=f32).T)
    g["qalnw"] = asc(inputs["L0_attn_b_q_a_ln_w"].reshape(12, 128).T, dtype=f32)
    qb = np.asarray(inputs["L0_attn_b_q_b_w"], dtype=f32)
    qb3 = qb.reshape(16, 192, 1536)
    qbn = qb3[:, :128, :].reshape(16 * 128, 1536)
    qbp = qb3[:, 128:, :]
    qbp_packed = qbp.reshape(8, 2 * 64, 1536).reshape(16 * 64, 1536)
    g["qbnT"] = _tile_w(asc(qbn.T, dtype=f32))
    g["qbpT"] = _tile_w(asc(qbp_packed.T, dtype=f32))
    kvaT = asc(np.asarray(inputs["L0_attn_b_kv_a_w"], dtype=f32).T)
    g["kvaT"] = _tile_w(kvaT[:, 0:512])
    g["kvapeT"] = _tile_w(kvaT[:, 512:576], W=64)
    g["kvlnw"] = asc(inputs["L0_attn_b_kv_a_ln_w"].reshape(4, 128).T, dtype=f32)
    kvb = np.asarray(inputs["L0_attn_b_kv_b_w"], dtype=f32)
    kvb3 = kvb.reshape(16, 256, 512)
    g["kvbnT"] = _tile_w(asc(kvb3[:, :128, :].reshape(16 * 128, 512).T, dtype=f32))
    g["kvbvT"] = _tile_w(asc(kvb3[:, 128:, :].reshape(16 * 128, 512).T, dtype=f32))
    g["owT"] = _tile_w(np.asarray(inputs["L0_attn_b_o_w"], dtype=f32).T)
    g["gup"] = _repack_nibbles(np.asarray(inputs["L0_ffn_gate_up_packed"]), 256)
    g["dnp"] = _repack_nibbles(np.asarray(inputs["L0_ffn_down_packed"]), 704)
    g["gusT"] = asc(np.asarray(inputs["L0_ffn_gate_up_scales"]).T, dtype=f32)
    g["dnsT"] = asc(np.asarray(inputs["L0_ffn_down_scales"]).T, dtype=f32)
    g["identb"] = np.eye(128, dtype=ml_dtypes.bfloat16)
    m = np.zeros((128, 128), np.float32)
    iu = np.triu_indices(128, 1)
    m[iu] = MASK_NEG
    g["mask"] = m
    qcos, qsin, kcos, ksin = _rope_tables()
    g["ropeqc"], g["ropeqs"], g["ropekc"], g["ropeks"] = qcos, qsin, kcos, ksin
    g["rotq"], g["rotk"] = _rot_mat()
    return g



class Sem:
    __slots__ = ("h", "n", "name")

    def __init__(self, h, name):
        self.h = h
        self.n = 0
        self.name = name


class _Op:
    __slots__ = ("waits", "fn", "incs")

    def __init__(self, waits, fn, incs):
        self.waits = waits
        self.fn = fn
        self.incs = incs


class Builder:
    def __init__(self, nc, stack):
        self.nc = nc
        self.stack = stack
        self.prog = {e: [] for e in ENGINES}
        self.last_wait = {e: {} for e in ENGINES}
        self.last_real = {e: None for e in ENGINES}
        self.last_inc = {e: None for e in ENGINES}
        self.s_pe = self.sem("s_pe")
        self.s_dve = self.sem("s_dve")
        self.s_act = self.sem("s_act")
        self.s_pool = self.sem("s_pool")
        self.s_dsync = self.sem("s_dsync")
        self.s_dgp = self.sem("s_dgp")
        self.s_bar = self.sem("s_bar")
        self.dma_sem = {"sync": self.s_dsync, "gpsimd": self.s_dgp}
        self.eng_sem = {"tensor": self.s_pe, "vector": self.s_dve,
                        "scalar": self.s_act, "gpsimd": self.s_pool}
        self.slot_sems = {}
        self.dma_used = {"sync": set(), "gpsimd": set()}

    def ssem(self, key):
        if key not in self.slot_sems:
            self.slot_sems[key] = self.sem("sl_" + key)
        return self.slot_sems[key]

    def sem(self, name):
        return Sem(self.stack.enter_context(self.nc.semaphore(name)), name)

    _uid = 0

    def _uniq(self, name):
        Builder._uid += 1
        return f"{name}_{Builder._uid}"

    def sb(self, name, shape, dtype, st=None):
        return (st or self.stack).enter_context(
            self.nc.sbuf_tensor(self._uniq(name), shape, dtype))

    def ps(self, name, shape, dtype=F32, st=None):
        return (st or self.stack).enter_context(
            self.nc.psum_tensor(self._uniq(name), shape, dtype))

    def op(self, eng, fn, waits=(), inc=None, k=1, is_dma=False):
        lw = self.last_wait[eng]
        wl = []
        for s, v in waits:
            if v > 0 and lw.get(s, 0) < v:
                lw[s] = v
                wl.append((s.h, v))
        incs = []
        val = None
        if inc is not None:
            kk = 16 if is_dma else k
            inc.n += kk
            val = inc.n
            incs.append((inc, kk))
        o = _Op(wl, fn, incs)
        self.prog[eng].append(o)
        self.last_real[eng] = o
        if inc is not None:
            self.last_inc[eng] = (inc, val)
        return val

    def dma(self, q, out, in_, waits=(), sem=None):
        if sem is None:
            sem = self.dma_sem[q]
        self.dma_used[q].add(sem)
        return self.op(q, (lambda e, out=out, in_=in_: e.dma_start(out=out, in_=in_)),
                       waits=waits, inc=sem, is_dma=True)

    def wait(self, eng, waits):
        lw = self.last_wait[eng]
        wl = []
        for s, v in waits:
            if v > 0 and lw.get(s, 0) < v:
                lw[s] = v
                wl.append((s.h, v))
        if wl:
            self.prog[eng].append(_Op(wl, None, []))

    def barrier(self):
        for eng in ("sync", "gpsimd"):
            for s in sorted(self.dma_used[eng], key=lambda s: s.name):
                self.wait(eng, [(s, s.n)])
            self.dma_used[eng] = set()
        for eng in ENGINES:
            o = self.last_real[eng]
            if o is None:
                continue
            li = self.last_inc[eng]
            if o.incs or (li is None):
                if li is not None:
                    self.wait(eng, [li])
                self.s_bar.n += 1
                self.prog[eng].append(_Op([], (lambda e: e.nop()), [(self.s_bar, 1)]))
            else:
                if li is not None:
                    self.wait(eng, [li])
                o.incs.append((self.s_bar, 1))
                self.s_bar.n += 1
            self.last_real[eng] = None
        tgt = self.s_bar.n
        for eng in ENGINES:
            self.wait(eng, [(self.s_bar, tgt)])

    def finalize(self, block):
        def runner(eng_name):
            def run(e):
                for o in self.prog[eng_name]:
                    for sh, v in o.waits:
                        e.wait_ge(sh, v)
                    if o.fn is not None:
                        ins = o.fn(e)
                        for s, kk in o.incs:
                            ins.then_inc(s.h, kk)
            return run
        block.sync(runner("sync"))
        block.tensor(runner("tensor"))
        block.vector(runner("vector"))
        block.scalar(runner("scalar"))
        block.gpsimd(runner("gpsimd"))



class MK:
    def __init__(self, nc, stack, chain):
        self.nc = nc
        self.st = stack
        self.chain = chain
        b = self.b = Builder(nc, stack)
        dt = nc.dram_tensor
        ap = lambda *a, **k: dt(*a, **k).ap()
        self.x_in = ap("x_in", [SEQ_PER_CORE, S, D], F32, kind="ExternalInput")
        self.ws = ap("ws", [2, 128, 16 * 128], F32, kind="ExternalInput")
        self.wa = ap("wa", [2, 128, 16 * 128], F32, kind="ExternalInput")
        self.woT = ap("woT", [16, 128, 16 * 128], F32, kind="ExternalInput")
        self.bo = ap("bo", [128, 16], F32, kind="ExternalInput")
        self.qaT = ap("qaT", [12, 128, 16 * 128], F32, kind="ExternalInput")
        self.qalnw = ap("qalnw", [128, 12], F32, kind="ExternalInput")
        self.qbnT = ap("qbnT", [16, 128, 12 * 128], F32, kind="ExternalInput")
        self.qbpT = ap("qbpT", [8, 128, 12 * 128], F32, kind="ExternalInput")
        self.kvaT = ap("kvaT", [4, 128, 16 * 128], F32, kind="ExternalInput")
        self.kvapeT = ap("kvapeT", [1, 128, 16 * 64], F32, kind="ExternalInput")
        self.kvlnw = ap("kvlnw", [128, 4], F32, kind="ExternalInput")
        self.kvbnT = ap("kvbnT", [16, 128, 4 * 128], F32, kind="ExternalInput")
        self.kvbvT = ap("kvbvT", [16, 128, 4 * 128], F32, kind="ExternalInput")
        self.owT = ap("owT", [16, 128, 16 * 128], F32, kind="ExternalInput")
        self.wsrc = {"ws": self.ws, "wa": self.wa, "wo": self.woT, "qa": self.qaT,
                     "qbn": self.qbnT, "qbp": self.qbpT, "kva": self.kvaT,
                     "kvape": self.kvapeT, "kvbn": self.kvbnT, "kvbv": self.kvbvT,
                     "ow": self.owT}
        self.gup = ap("gup", [2 * I_FFN, 256], I32, kind="ExternalInput")
        self.dnp = ap("dnp", [D, 704], I32, kind="ExternalInput")
        self.gusT = ap("gusT", [2 * I_FFN, 16], F32, kind="ExternalInput")
        self.dnsT = ap("dnsT", [D, 44], F32, kind="ExternalInput")
        self.identb = ap("identb", [128, 128], BF16, kind="ExternalInput")
        self.mask = ap("mask", [128, 128], F32, kind="ExternalInput")
        self.ropeqc = ap("ropeqc", [128, S], BF16, kind="ExternalInput")
        self.ropeqs = ap("ropeqs", [128, S], BF16, kind="ExternalInput")
        self.ropekc = ap("ropekc", [64, S], BF16, kind="ExternalInput")
        self.ropeks = ap("ropeks", [64, S], BF16, kind="ExternalInput")
        self.rotq = ap("rotq", [128, 128], BF16, kind="ExternalInput")
        self.rotk = ap("rotk", [64, 64], BF16, kind="ExternalInput")
        self.y = ap("y", [SEQ_PER_CORE, S, D], F32, kind="ExternalOutput")
        self.resid = ap("resid", [SEQ_PER_CORE, S, D], F32, kind="Internal")
        self.wgu_dec = ap("wgu_dec", [GU_OC, 128, 16 * 128], BF16, kind="Internal")
        self.wdn_dec = ap("wdn_dec", [DN_OC, 128, 44 * 128], BF16, kind="Internal")
        self.av_spill = ap("av_spill", [16, 128, S], BF16, kind="Internal")
        self.qpe_spill = ap("qpe_spill", [16, 64, S], BF16, kind="Internal")
        self.t_identb = b.sb("g_identb", [128, 128], BF16)
        self.t_mask = b.sb("g_mask", [128, 128], F32)
        self.t_eps5 = b.sb("g_eps5", [128, 1], F32)
        self.t_eps6 = b.sb("g_eps6", [128, 1], F32)
        self.t_ones = b.sb("g_ones", [1, 128], F32)
        self.t_scb = b.sb("g_scb", [1, 128], F32)
        self.t_trash = b.sb("g_trash", [128, 2048], BF16)
        self.t_rs = b.sb("g_rs", [128, 16], F32)
        self.t_ssq = b.sb("g_ssq", [128, 16], F32)
        self.t_mu = b.sb("g_mu", [128, 16], F32)
        self.t_var = b.sb("g_var", [128, 16], F32)
        self.t_std = b.sb("g_std", [128, 16], F32)
        self.t_rstd = b.sb("g_rstd", [128, 16], F32)
        self.t_nmr = b.sb("g_nmr", [128, 16], F32)
        self.t_onesb = b.sb("g_onesb", [128, 1], BF16)
        self.t_one1 = b.sb("g_one1", [128, 1], F32)
        self.wc_done = {}
        self.wc = {}

    def setup(self):
        b = self.b
        b.dma("sync", self.t_identb[:], self.identb[:])
        b.dma("sync", self.t_mask[:], self.mask[:])
        b.op("vector", lambda e: e.memset(self.t_eps5[:], EPS_LN), inc=b.s_dve)
        b.op("vector", lambda e: e.memset(self.t_eps6[:], EPS_B), inc=b.s_dve)
        b.op("vector", lambda e: e.memset(self.t_ones[:], 1.0), inc=b.s_dve)
        b.op("vector", lambda e: e.memset(self.t_scb[:], SC_B), inc=b.s_dve)
        b.op("vector", lambda e: e.memset(self.t_onesb[:], 1.0), inc=b.s_dve)
        b.op("vector", lambda e: e.memset(self.t_one1[:], 1.0), inc=b.s_dve)
        b.barrier()

    WSRC = None

    def load_w(self, name, oc, dst, sem, waits=()):
        srcap = self.wsrc[name]
        w = srcap.shape[2]
        v = self.b.dma("gpsimd", dst[:, 0:w], srcap[oc, :, :], waits=waits, sem=sem)
        return (sem, v)

    def decode_ffn(self):
        b, nc = self.b, self.nc
        from concourse.ap import AP
        with ExitStack() as st:
            words = [b.sb(f"dc_w{i}", [128, 704], I32, st) for i in range(2)]
            stile = [b.sb(f"dc_s{i}", [128, 44], F32, st) for i in range(2)]
            stil2 = [b.sb(f"dc_s2{i}", [128, 44], F32, st) for i in range(2)]
            t_m6 = [b.sb(f"dc_m6{i}", [128, 704], I32, st) for i in range(2)]
            t_s1 = [b.sb(f"dc_sa{i}", [128, 704], I32, st) for i in range(2)]
            t_s2 = [b.sb(f"dc_sb{i}", [128, 704], I32, st) for i in range(2)]
            t_bits = [b.sb(f"dc_bt{i}", [128, 704], I32, st) for i in range(2)]
            t_cv = [b.sb(f"dc_cv{i}", [128, 1408], F32, st) for i in range(2)]
            t_cv2 = [b.sb(f"dc_c2{i}", [128, 1408], F32, st) for i in range(2)]
            t_srep = [b.sb(f"dc_sr{i}", [128, 1408], F32, st) for i in range(2)]
            dsc = [b.sb(f"dc_dsc{i}", [128, I_FFN], BF16, st) for i in range(2)]
            stage = [b.sb(f"dc_st{i}", [128, 44 * 128], BF16, st) for i in range(2)]
            pst = [b.ps(f"dc_ps{i}", [128, 1024], BF16, st) for i in range(2)]

            jobs = [(self.gup, self.gusT, self.wgu_dec, GU_OC, 256, 16),
                    (self.dnp, self.dnsT, self.wdn_dec, DN_OC, 704, 44)]
            slot_free = {0: None, 1: None}
            stage_free = {0: None, 1: None}
            pst_free = {0: None, 1: None}
            pair_free = [{"m6": 0, "s": 0, "bits": 0, "cv": 0, "cv2": 0, "srep": 0}
                         for _ in range(2)]
            dsc_free = {0: None, 1: None}
            it = 0
            gpp = 0
            for packed, scales, outd, n_ot, wpr, nkc in jobs:
                ng2 = 2 * wpr // 128
                for ot in range(n_ot):
                    sl = it % 2
                    dscb = dsc[sl]
                    s_w = b.ssem(f"dcw{sl}")
                    w_waits = [slot_free[sl]] if slot_free[sl] else []
                    b.dma("sync", words[sl][:, 0:wpr], packed[ot * 128:(ot + 1) * 128, :],
                          waits=w_waits, sem=s_w)
                    vw = b.dma("sync", stile[sl][:, 0:nkc], scales[ot * 128:(ot + 1) * 128, :],
                               waits=[], sem=s_w)
                    pmv = b.op("vector", lambda e, sl=sl, nkc=nkc: e.tensor_scalar(
                        stil2[sl][:, 0:nkc], stile[sl][:, 0:nkc], float(2.0 ** 63), None,
                        ALU.mult), waits=[(s_w, vw)], inc=b.s_dve)
                    last_act = None
                    for pp in range(4):
                        bb = gpp % 2
                        gpp += 1
                        pv = pair_free[bb]
                        net = 4 * pp - 6
                        shs = (ALU.logical_shift_right if net > 0 else ALU.logical_shift_left)
                        mg = b.op("vector", lambda e, sl=sl, bb=bb, wpr=wpr, net=net, shs=shs: e.tensor_scalar(
                            t_m6[bb][:, 0:wpr], words[sl][:, 0:wpr], abs(net), 0x01C001C0,
                            shs, ALU.bitwise_and),
                            waits=([(s_w, vw)] if pp == 0 else [])
                                  + ([(b.s_pool, pv["m6"])] if pv["m6"] else []),
                            inc=b.s_dve)
                        b.op("vector", lambda e, sl=sl, bb=bb, wpr=wpr, pp=pp: e.tensor_scalar(
                            t_s1[bb][:, 0:wpr], words[sl][:, 0:wpr], 4 * pp + 3, 0x00010001,
                            ALU.logical_shift_right, ALU.bitwise_and), inc=b.s_dve)
                        sgv = b.op("vector", lambda e, bb=bb, wpr=wpr: e.tensor_scalar(
                            t_s2[bb][:, 0:wpr], t_s1[bb][:, 0:wpr], 15, None,
                            ALU.logical_shift_left),
                            waits=([(b.s_pool, pv["s"])] if pv["s"] else []), inc=b.s_dve)
                        orv = b.op("vector", lambda e, bb=bb, wpr=wpr: e.tensor_tensor(
                            t_bits[bb][:, 0:wpr], t_m6[bb][:, 0:wpr], t_s2[bb][:, 0:wpr],
                            ALU.bitwise_or), inc=b.s_dve)
                        cvv = b.op("vector", lambda e, bb=bb, wpr=wpr: e.tensor_copy(
                            t_cv[bb][:, 0:2 * wpr], t_bits[bb][:, 0:wpr].bitcast(BF16)),
                            waits=[(b.s_dve, orv)], inc=b.s_dve)
                        g0 = pp * ng2
                        sap = stil2[sl][:, g0:g0 + ng2]
                        rep = AP(sap.tensor, sap.offset, [[44, 128], [1, ng2], [0, 128]])
                        spv = b.op("vector", lambda e, bb=bb, wpr=wpr, ng2=ng2, rep=rep: e.tensor_copy(
                            t_srep[bb][:, 0:2 * wpr].rearrange("p (g r) -> p g r", g=ng2), rep),
                            waits=[(b.s_dve, pmv)], inc=b.s_dve)
                        m1v = b.op("vector", lambda e, bb=bb, wpr=wpr: e.tensor_mul(
                            t_cv2[bb][:, 0:2 * wpr], t_cv[bb][:, 0:2 * wpr],
                            t_srep[bb][:, 0:2 * wpr]),
                            waits=([(b.s_act, pv["cv2"])] if pv["cv2"] else []),
                            inc=b.s_dve)
                        fw = []
                        if pp == 0 and dsc_free[sl] is not None:
                            fw.append(dsc_free[sl])
                        last_act = b.op("scalar", lambda e, bb=bb, wpr=wpr, pp=pp, dscb=dscb: e.activation(
                            dscb[:, pp * 2 * wpr:(pp + 1) * 2 * wpr], t_cv2[bb][:, 0:2 * wpr],
                            AF.Identity, scale=float(2.0 ** 63)),
                            waits=[(b.s_dve, m1v)] + fw, inc=b.s_act)
                        pair_free[bb] = {"m6": 0, "s": 0, "bits": 0, "cv": m1v,
                                         "cv2": last_act, "srep": m1v}
                    slot_free[sl] = (b.s_act, last_act)
                    ssl = it % 2
                    st_waits = [stage_free[ssl]] if stage_free[ssl] else []
                    ev_last = None
                    lastt = None
                    for kg in range(0, nkc, 4):
                        psl = (kg // 4) % 2
                        pe_waits = [(b.s_act, last_act)]
                        if pst_free[psl]:
                            pe_waits.append(pst_free[psl])
                        for kk in range(kg, min(kg + 4, nkc)):
                            lastt = b.op("tensor", lambda e, kk=kk, kg=kg, psl=psl, dscb=dscb: e.transpose(
                                pst[psl][:, (kk - kg) * 128:(kk - kg + 1) * 128],
                                dscb[:, kk * 128:(kk + 1) * 128], self.t_identb[:]),
                                waits=pe_waits, inc=b.s_pe)
                            pe_waits = []
                        nb = min(4, nkc - kg)
                        ev_last = b.op("vector", lambda e, kg=kg, psl=psl, nb=nb, ssl=ssl: e.tensor_copy(
                            stage[ssl][:, kg * 128:(kg + nb) * 128], pst[psl][:, 0:nb * 128]),
                            waits=[(b.s_pe, lastt)] + st_waits, inc=b.s_dve)
                        st_waits = []
                        pst_free[psl] = (b.s_dve, ev_last)
                    s_st = b.ssem(f"dcs{ssl}")
                    dv = b.dma("sync", outd[ot, :, :], stage[ssl][:, 0:nkc * 128],
                               waits=[(b.s_dve, ev_last)], sem=s_st)
                    stage_free[ssl] = (s_st, dv)
                    dsc_free[sl] = (b.s_pe, lastt)
                    it += 1
        b.barrier()

    def ln_pass(self, src_tile_fn, ntiles, xn_sink, rx_keep=None):
        raise NotImplementedError

    def emit_ln(self, src_fn, tiles, rx, rx_sems, rx_free, xn_of, xn_free, eps_tile,
                col0=0):
        b = self.b
        n = len(tiles)
        c0, c1 = col0, col0 + n
        ld = []
        for i, tj in enumerate(tiles):
            wl = [rx_free[i]] if rx_free and rx_free[i] is not None else []
            v = b.dma("sync", rx[i][:], src_fn(tj), waits=wl, sem=rx_sems[i])
            ld.append((rx_sems[i], v))
        for i in range(n):
            b.op("vector", lambda e, i=i, c0=c0: e.reduce_sum(
                self.t_rs[:, c0 + i:c0 + i + 1], rx[i][:], axis=AX.X),
                waits=[ld[i]], inc=b.s_dve)
            b.op("scalar", lambda e, i=i, c0=c0: e.activation(
                self.t_trash[:], rx[i][:], AF.Square, accum_out=self.t_ssq[:, c0 + i:c0 + i + 1]),
                waits=[ld[i]], inc=b.s_act)
        dvev = b.s_dve.n
        actv = b.s_act.n
        b.op("gpsimd", lambda e, c0=c0, c1=c1: e.tensor_scalar(
            self.t_mu[:, c0:c1], self.t_rs[:, c0:c1], 1.0 / D, None, ALU.mult),
            waits=[(b.s_dve, dvev), (b.s_act, actv)], inc=b.s_pool)
        b.op("gpsimd", lambda e, c0=c0, c1=c1: e.tensor_scalar(
            self.t_var[:, c0:c1], self.t_ssq[:, c0:c1], 1.0 / D, None, ALU.mult), inc=b.s_pool)
        b.op("gpsimd", lambda e, c0=c0, c1=c1: e.tensor_mul(
            self.t_std[:, c0:c1], self.t_mu[:, c0:c1], self.t_mu[:, c0:c1]), inc=b.s_pool)
        pv = b.op("gpsimd", lambda e, c0=c0, c1=c1: e.tensor_sub(
            self.t_var[:, c0:c1], self.t_var[:, c0:c1], self.t_std[:, c0:c1]), inc=b.s_pool)
        sv = b.op("scalar", lambda e, c0=c0, c1=c1, eps=eps_tile: e.activation(
            self.t_std[:, c0:c1], self.t_var[:, c0:c1], AF.Sqrt, bias=eps[:], scale=1.0),
            waits=[(b.s_pool, pv)], inc=b.s_act)
        rv = b.op("vector", lambda e, c0=c0, c1=c1: e.reciprocal(
            self.t_rstd[:, c0:c1], self.t_std[:, c0:c1]), waits=[(b.s_act, sv)], inc=b.s_dve)
        b.op("gpsimd", lambda e, c0=c0, c1=c1: e.tensor_mul(
            self.t_nmr[:, c0:c1], self.t_mu[:, c0:c1], self.t_rstd[:, c0:c1]),
            waits=[(b.s_dve, rv)], inc=b.s_pool)
        nv = b.op("gpsimd", lambda e, c0=c0, c1=c1: e.tensor_scalar(
            self.t_nmr[:, c0:c1], self.t_nmr[:, c0:c1], -1.0, None, ALU.mult), inc=b.s_pool)
        xn_vals = []
        for i in range(n):
            wl = [(b.s_pool, nv), (b.s_dve, rv)]
            if xn_free and xn_free[i] is not None:
                wl.append(xn_free[i])
            xn_vals.append(b.op("scalar", lambda e, i=i, c0=c0: e.activation(
                xn_of(i), rx[i][:], AF.Identity, bias=self.t_nmr[:, c0 + i:c0 + i + 1],
                scale=self.t_rstd[:, c0 + i:c0 + i + 1]), waits=wl, inc=b.s_act))
        return ld, xn_vals

    def emit_y_add(self, yT, yT_ready, rx_tile, rx_ld, tt_col, dst_ap, slab, slab_free,
                   rms_col=None):
        b = self.b
        lastt = None
        wl = [yT_ready]
        if slab_free is not None:
            wl.append(slab_free)
        for oc in range(16):
            lastt = b.op("tensor", lambda e, oc=oc: e.transpose(
                slab[:, oc * 128:(oc + 1) * 128],
                yT[:, oc * 512 + tt_col * 128: oc * 512 + (tt_col + 1) * 128],
                self.t_identb[:]), waits=wl, inc=b.s_pe)
            wl = []
        if rms_col is not None:
            i = rms_col
            sqv = b.op("scalar", lambda e, i=i: e.activation(self.t_trash[:], slab[:],
                 AF.Square, accum_out=self.t_ssq[:, i:i + 1]),
                 waits=[(b.s_pe, lastt)], inc=b.s_act)
            sv = b.op("scalar", lambda e, i=i: e.activation(self.t_std[:, i:i + 1],
                 self.t_ssq[:, i:i + 1], AF.Sqrt, bias=self.t_eps6[:], scale=1.0 / D),
                 inc=b.s_act)
            rv = b.op("vector", lambda e, i=i: e.reciprocal(self.t_rstd[:, i:i + 1],
                 self.t_std[:, i:i + 1]), waits=[(b.s_act, sv)], inc=b.s_dve)
            scal = self.t_rstd[:, i:i + 1]
            addw = [(b.s_dve, rv), rx_ld]
        else:
            scal = 1.0
            addw = [(b.s_pe, lastt), rx_ld]
        av = b.op("vector", lambda e, scal=scal: e.scalar_tensor_tensor(
            rx_tile[:], slab[:], scal, rx_tile[:], ALU.mult, ALU.add),
            waits=addw, inc=b.s_dve)
        stv = b.dma("sync", dst_ap, rx_tile[:], waits=[(b.s_dve, av)], sem=rx_ld[0])
        return (rx_ld[0], stv), (b.s_dve, av)

    def ffn(self, seq, src, dst):
        b = self.b
        src_fn = lambda t: src[seq, t * 128:(t + 1) * 128, :]
        dst_fn = lambda t: dst[seq, t * 128:(t + 1) * 128, :]
        with ExitStack() as st:
            rx = [b.sb(f"f_rx{i}", [128, D], F32, st) for i in range(4)]
            xt = [b.sb(f"f_xt{i}", [128, D], BF16, st) for i in range(4)]
            xnT = b.sb("f_xnT", [128, 16 * 512], BF16, st)
            hT = b.sb("f_hT", [128, 44 * 512], BF16, st)
            yT = b.sb("f_yT", [128, 16 * 512], BF16, st)
            sil = [b.sb(f"f_sil{i}", [128, 512], BF16, st) for i in range(2)]
            wg = [b.sb(f"f_wg{i}", [128, 16 * 128], BF16, st) for i in range(2)]
            wu = [b.sb(f"f_wu{i}", [128, 16 * 128], BF16, st) for i in range(2)]
            wd = [b.sb(f"f_wd{i}", [128, 44 * 128], BF16, st) for i in range(2)]
            psm = [b.ps(f"f_ps{i}", [128, 512], F32, st) for i in range(4)]
            psbt = [b.ps(f"f_pt{i}", [128, 1024], BF16, st) for i in range(2)]
            slab = [b.ps("f_sl0", [128, 2048], BF16, st)]

            rx_free = [None] * 4
            xn_free = [None] * 4
            wg_free = [None, None]
            wu_free = [None, None]
            wd_free = [None, None]
            psm_free = [None] * 4
            psbt_free = [None, None]
            slab_free = [None]
            sil_free = [None, None]

            rx_sems = [b.ssem(f"frx{i}") for i in range(4)]
            for tb in range(4):
                tiles = [tb * 4 + i for i in range(4)]
                ld, xnv = self.emit_ln(src_fn, tiles, rx, rx_sems, rx_free,
                                       lambda i: xt[i][:], xn_free, self.t_eps5)
                xnT_last = None
                for kc in range(16):
                    psl = kc % 2
                    for i in range(4):
                        wl = [(b.s_act, xnv[i])]
                        if i == 0 and psbt_free[psl] is not None:
                            wl.append(psbt_free[psl])
                        lastt = b.op("tensor", lambda e, kc=kc, i=i, psl=psl: e.transpose(
                            psbt[psl][:, i * 128:(i + 1) * 128],
                            xt[i][:, kc * 128:(kc + 1) * 128], self.t_identb[:]),
                            waits=wl, inc=b.s_pe)
                    xnT_last = b.op("vector", lambda e, kc=kc, psl=psl: e.tensor_copy(
                        xnT[:, kc * 512:(kc + 1) * 512], psbt[psl][:, 0:512]),
                        waits=[(b.s_pe, lastt)], inc=b.s_dve)
                    psbt_free[psl] = (b.s_dve, xnT_last)
                for i in range(4):
                    xn_free[i] = (b.s_dve, xnT_last)

                for pc in range(44):
                    sl = pc % 2
                    s_g = b.ssem(f"fwg{sl}")
                    s_u = b.ssem(f"fwu{sl}")
                    wl = [wg_free[sl]] if wg_free[sl] is not None else []
                    gv = b.dma("sync", wg[sl][:], self.wgu_dec[pc, :, :],
                               waits=wl, sem=s_g)
                    wl = [wu_free[sl]] if wu_free[sl] is not None else []
                    uv = b.dma("sync", wu[sl][:], self.wgu_dec[pc + 44, :, :],
                               waits=wl, sem=s_u)
                    pa = (2 * pc) % 4
                    pb = (2 * pc + 1) % 4
                    wlA = [(s_g, gv), (b.s_dve, xnT_last)]
                    if psm_free[pa] is not None:
                        wlA.append(psm_free[pa])
                    lastA = None
                    for kc in range(16):
                        lastA = b.op("tensor", lambda e, kc=kc, sl=sl, pa=pa: e.matmul(
                            psm[pa][:], wg[sl][:, kc * 128:(kc + 1) * 128],
                            xnT[:, kc * 512:(kc + 1) * 512], start=(kc == 0), stop=(kc == 15)),
                            waits=wlA, inc=(b.s_pe if kc == 15 else None))
                        wlA = []
                    wg_free[sl] = (b.s_pe, lastA)
                    wlB = [(s_u, uv)]
                    if psm_free[pb] is not None:
                        wlB.append(psm_free[pb])
                    lastB = None
                    for kc in range(16):
                        lastB = b.op("tensor", lambda e, kc=kc, sl=sl, pb=pb: e.matmul(
                            psm[pb][:], wu[sl][:, kc * 128:(kc + 1) * 128],
                            xnT[:, kc * 512:(kc + 1) * 512], start=(kc == 0), stop=(kc == 15)),
                            waits=wlB, inc=(b.s_pe if kc == 15 else None))
                        wlB = []
                    wu_free[sl] = (b.s_pe, lastB)
                    ssl = pc % 2
                    wl = [(b.s_pe, lastA)]
                    if sil_free[ssl] is not None:
                        wl.append(sil_free[ssl])
                    sv = b.op("scalar", lambda e, ssl=ssl, pa=pa: e.activation(
                        sil[ssl][:], psm[pa][:], AF.Silu), waits=wl, inc=b.s_act)
                    psm_free[pa] = (b.s_act, sv)
                    hv = b.op("vector", lambda e, pc=pc, pb=pb, ssl=ssl: e.scalar_tensor_tensor(
                        hT[:, pc * 512:(pc + 1) * 512], psm[pb][:], 1.0, sil[ssl][:],
                        ALU.mult, ALU.mult), waits=[(b.s_pe, lastB), (b.s_act, sv)], inc=b.s_dve)
                    psm_free[pb] = (b.s_dve, hv)
                    sil_free[ssl] = (b.s_dve, hv)
                h_done = b.s_dve.n

                for oc in range(16):
                    sl = oc % 2
                    s_d = b.ssem(f"fwd{sl}")
                    wl = [wd_free[sl]] if wd_free[sl] is not None else []
                    dv = b.dma("sync", wd[sl][:], self.wdn_dec[oc, :, :],
                               waits=wl, sem=s_d)
                    pp = oc % 4
                    wl = [(s_d, dv), (b.s_dve, h_done)]
                    if psm_free[pp] is not None:
                        wl.append(psm_free[pp])
                    last = None
                    for kc in range(44):
                        last = b.op("tensor", lambda e, kc=kc, sl=sl, pp=pp: e.matmul(
                            psm[pp][:], wd[sl][:, kc * 128:(kc + 1) * 128],
                            hT[:, kc * 512:(kc + 1) * 512], start=(kc == 0), stop=(kc == 43)),
                            waits=wl, inc=(b.s_pe if kc == 43 else None))
                        wl = []
                    wd_free[sl] = (b.s_pe, last)
                    ev = b.op("vector", lambda e, oc=oc, pp=pp: e.tensor_copy(
                        yT[:, oc * 512:(oc + 1) * 512], psm[pp][:]),
                        waits=[(b.s_pe, last)], inc=b.s_dve)
                    psm_free[pp] = (b.s_dve, ev)
                yT_done = b.s_dve.n

                for i in range(4):
                    stv, sfree = self.emit_y_add(yT, (b.s_dve, yT_done), rx[i], ld[i], i,
                                                 dst_fn(tiles[i]), slab[0][:],
                                                 slab_free[0], rms_col=8 + i)
                    slab_free[0] = sfree
                    rx_free[i] = stv
        b.barrier()

    def build(self):
        self.setup()
        need_ffn = any(p[0] == "ffn" for p in self.chain)
        if need_ffn:
            self.decode_ffn()
        for idx, ph in enumerate(self.chain):
            kind, seq = ph[0], ph[1]
            first = (idx == 0) or all(p[1] != seq for p in self.chain[:idx])
            last = (idx == len(self.chain) - 1) or all(p[1] != seq for p in self.chain[idx + 1:])
            src = self.x_in if first else self.resid
            dst = self.y if last else self.resid
            if kind == "ffn":
                self.ffn(seq, src, dst)
            elif kind == "attn_a":
                self.attn_a(seq, src, dst)
            elif kind == "attn_b":
                self.attn_b(seq, src, dst)
            else:
                raise ValueError(kind)
        self.b.barrier()

    def attn_a(self, seq, src, dst):
        b = self.b
        src_fn = lambda t: src[seq, t * 128:(t + 1) * 128, :]
        dst_fn = lambda t: dst[seq, t * 128:(t + 1) * 128, :]
        with ExitStack() as st:
            xn = b.sb("a_xn", [128, NT * D], BF16, st)
            rx = [b.sb(f"a_rx{i}", [128, D], F32, st) for i in range(2)]
            bo_sb = b.sb("a_bo", [128, 16], F32, st)
            stT = b.sb("a_st", [128, 2 * S], BF16, st)
            acT = b.sb("a_ac", [128, 2 * S], BF16, st)
            cbias = b.sb("a_cb", [128, 1], F32, st)
            s_aw = b.ssem("aw")
            s_awb = b.ssem("awb")
            bo_v = b.dma("sync", bo_sb[:], self.bo[:], sem=s_awb)
            bo_ld = (s_awb, bo_v)
            with ExitStack() as st1:
                wsb = b.sb("a_ws", [128, 2 * 2048], BF16, st1)
                wab = b.sb("a_wa", [128, 2 * 2048], BF16, st1)
                for oc in range(2):
                    self.load_w("ws", oc, wsb[:, oc * 2048:(oc + 1) * 2048], s_aw)
                    self.load_w("wa", oc, wab[:, oc * 2048:(oc + 1) * 2048], s_aw)
                aw_ld = (s_aw, s_aw.n)
                xnT = b.sb("a_xnT", [128, 16 * 512], BF16, st1)
                psbt = [b.ps(f"a_pt{i}", [128, 1024], BF16, st1) for i in range(2)]
                psv = [b.ps(f"a_pv{i}", [128, 512], F32, st1) for i in range(2)]
                rx_free = [None, None]
                psbt_free = [None, None]
                psv_free = [None, None]
                rx_sems = [b.ssem("arx0"), b.ssem("arx1")]
                for tb in range(4):
                    xn_vals = []
                    for gg in range(2):
                        tiles = [tb * 4 + gg * 2, tb * 4 + gg * 2 + 1]
                        ld, xv = self.emit_ln(src_fn, tiles, rx, rx_sems, rx_free,
                                              lambda i, tiles=tiles: xn[:, tiles[i] * D:(tiles[i] + 1) * D],
                                              None, self.t_eps5, col0=gg * 2)
                        xn_vals += xv
                        rx_free = [(b.s_act, xv[0]), (b.s_act, xv[1])]
                    xnT_last = None
                    for kc in range(16):
                        psl = kc % 2
                        for i in range(4):
                            t = tb * 4 + i
                            wl = [(b.s_act, xn_vals[i])]
                            if i == 0 and psbt_free[psl] is not None:
                                wl.append(psbt_free[psl])
                            lastt = b.op("tensor", lambda e, t=t, kc=kc, i=i, psl=psl: e.transpose(
                                psbt[psl][:, i * 128:(i + 1) * 128],
                                xn[:, t * D + kc * 128: t * D + (kc + 1) * 128],
                                self.t_identb[:]), waits=wl, inc=b.s_pe)
                        xnT_last = b.op("vector", lambda e, kc=kc, psl=psl: e.tensor_copy(
                            xnT[:, kc * 512:(kc + 1) * 512], psbt[psl][:, 0:512]),
                            waits=[(b.s_pe, lastt)], inc=b.s_dve)
                        psbt_free[psl] = (b.s_dve, xnT_last)
                    for wbuf, dT, sc in ((wsb, stT, 1.0 / 16), (wab, acT, None)):
                        for m in range(2):
                            pi = m % 2
                            wl = [(b.s_dve, xnT_last), aw_ld]
                            if psv_free[pi] is not None:
                                wl.append(psv_free[pi])
                            last = None
                            for kc in range(16):
                                last = b.op("tensor", lambda e, kc=kc, m=m, pi=pi, wbuf=wbuf: e.matmul(
                                    psv[pi][:], wbuf[:, m * 2048 + kc * 128: m * 2048 + (kc + 1) * 128],
                                    xnT[:, kc * 512:(kc + 1) * 512], start=(kc == 0), stop=(kc == 15)),
                                    waits=wl, inc=(b.s_pe if kc == 15 else None))
                                wl = []
                            if sc is None:
                                ev = b.op("vector", lambda e, m=m, tb=tb, pi=pi, dT=dT: e.tensor_copy(
                                    dT[:, m * S + tb * 512: m * S + (tb + 1) * 512], psv[pi][:]),
                                    waits=[(b.s_pe, last)], inc=b.s_dve)
                            else:
                                ev = b.op("vector", lambda e, m=m, tb=tb, pi=pi, dT=dT, sc=sc: e.tensor_scalar(
                                    dT[:, m * S + tb * 512: m * S + (tb + 1) * 512], psv[pi][:],
                                    sc, None, ALU.mult), waits=[(b.s_pe, last)], inc=b.s_dve)
                            psv_free[pi] = (b.s_dve, ev)
                b.barrier()
            with ExitStack() as st2:
                sqb = [b.sb(f"a_sq{i}", [128, D], BF16, st2) for i in range(2)]
                nmx = b.sb("a_nmx", [1, 16], F32, st2)
                ctmp = b.sb("a_ctmp", [1, 4], F32, st2)
                psn = b.ps("a_psn", [1, 512], F32, st2)
                psc = b.ps("a_psc", [128, 1], F32, st2)
                for sidx, TT in enumerate((stT, acT)):
                    sq_vals = []
                    sq_w = [(b.s_pe, b.s_pe.n)] if sidx > 0 else []
                    for c in range(2):
                        sq_vals.append(b.op("scalar", lambda e, c=c, sidx=sidx, TT=TT: e.activation(
                            sqb[c][:], TT[:, c * S:(c + 1) * S], AF.Square), waits=sq_w, inc=b.s_act))
                        sq_w = []
                    psn_free = None
                    for blk in range(4):
                        wl = [(b.s_act, sq_vals[1])]
                        if psn_free is not None:
                            wl.append(psn_free)
                        last = None
                        for c in range(2):
                            last = b.op("tensor", lambda e, c=c, blk=blk: e.matmul(
                                psn[0:1, :], self.t_onesb[:], sqb[c][:, blk * 512:(blk + 1) * 512],
                                start=(c == 0), stop=(c == 1)), waits=wl, inc=(b.s_pe if c == 1 else None))
                            wl = []
                        rv = b.op("vector", lambda e, sidx=sidx, blk=blk: e.reduce_max(
                            nmx[0:1, sidx * 4 + blk: sidx * 4 + blk + 1], psn[0:1, :], axis=AX.X),
                            waits=[(b.s_pe, last)], inc=b.s_dve)
                        psn_free = (b.s_dve, rv)
                for sidx in range(2):
                    b.op("vector", lambda e, sidx=sidx: e.reduce_max(
                        ctmp[0:1, sidx:sidx + 1], nmx[0:1, sidx * 4:sidx * 4 + 4], axis=AX.X),
                        inc=b.s_dve)
                dv = b.s_dve.n
                pv = b.op("gpsimd", lambda e: e.tensor_mul(ctmp[0:1, 2:3], ctmp[0:1, 0:1],
                     ctmp[0:1, 1:2]), waits=[(b.s_dve, dv)], inc=b.s_pool)
                sv = b.op("scalar", lambda e: e.activation(ctmp[0:1, 3:4], ctmp[0:1, 2:3],
                     AF.Sqrt), waits=[(b.s_pool, pv)], inc=b.s_act)
                nv = b.op("gpsimd", lambda e: e.tensor_scalar(ctmp[0:1, 2:3], ctmp[0:1, 3:4],
                     -1.0, None, ALU.mult), waits=[(b.s_act, sv)], inc=b.s_pool)
                mv = b.op("tensor", lambda e: e.matmul(psc[:, 0:1], self.t_ones[:],
                     ctmp[0:1, 2:3], start=True, stop=True), waits=[(b.s_pool, nv)], inc=b.s_pe)
                b.op("vector", lambda e: e.tensor_copy(cbias[:], psc[:, 0:1]),
                     waits=[(b.s_pe, mv)], inc=b.s_dve)
                b.barrier()
            with ExitStack() as st3:
                p4 = [b.sb(f"a_p4{i}", [128, 4 * S], BF16, st3) for i in range(2)]
                PT = b.sb("a_PT", [128, 16 * 512], BF16, st3)
                avT = b.sb("a_av", [128, 16 * 512], BF16, st3)
                yT = b.sb("a_yT", [128, 16 * 512], BF16, st3)
                wob = [b.sb(f"a_wo{i}", [128, 16 * 128], BF16, st3) for i in range(2)]
                asum = b.sb("a_as", [128, 16], F32, st3)
                atot = b.sb("a_at", [128, 4], F32, st3)
                arcp = b.sb("a_ar", [128, 4], F32, st3)
                pss = [b.ps(f"a_ss{i}", [128, 512], F32, st3) for i in range(2)]
                pst2 = [b.ps(f"a_pt2{i}", [128, 1024], BF16, st3) for i in range(2)]
                psa = [b.ps(f"a_pa{i}", [128, 512], F32, st3) for i in range(2)]
                slab = b.ps("a_slab", [128, 2048], BF16, st3)
                pss_free = [None, None]
                pst2_free = [None, None]
                psa_free = [None, None]
                wob_free = [None, None]
                slab_free = None
                rx_free = [None, None]
                sum_done = 0
                pm_of = {}
                ya_of = {}

                def a_stage1(qb):
                    nonlocal sum_done
                    pb = p4[qb % 2][:]
                    ssi = 0
                    exp_last = [0] * 4
                    for qt in range(4):
                        qtg = qb * 4 + qt
                        klen = (qtg + 1) * 128
                        nblk = (klen + 511) // 512
                        for blk in range(nblk):
                            w = min(512, klen - blk * 512)
                            pi = ssi % 2
                            ssi += 1
                            wl = [pss_free[pi]] if pss_free[pi] is not None else []
                            last = None
                            for m in range(2):
                                last = b.op("tensor", lambda e, m=m, qtg=qtg, blk=blk, w=w, pi=pi: e.matmul(
                                    pss[pi][:, 0:w], stT[:, m * S + qtg * 128: m * S + (qtg + 1) * 128],
                                    acT[:, m * S + blk * 512: m * S + blk * 512 + w],
                                    start=(m == 0), stop=(m == 1)), waits=wl,
                                    inc=(b.s_pe if m == 1 else None))
                                wl = []
                            expw = [(b.s_pe, last)]
                            if blk == nblk - 1:
                                mkv = b.op("vector", lambda e, w=w, pi=pi: e.tensor_tensor(
                                    pss[pi][:, w - 128:w], pss[pi][:, w - 128:w], self.t_mask[:],
                                    ALU.add), waits=[(b.s_pe, last)], inc=b.s_dve)
                                expw = [(b.s_dve, mkv)]
                            ev = b.op("scalar", lambda e, qt=qt, blk=blk, w=w, pi=pi, pb=pb: e.activation(
                                pb[:, qt * S + blk * 512: qt * S + blk * 512 + w], pss[pi][:, 0:w],
                                AF.Exp, bias=cbias[:], scale=1.0,
                                accum_out=asum[:, qt * 4 + blk: qt * 4 + blk + 1]),
                                waits=expw + [(b.s_dve, sum_done)], inc=b.s_act)
                            exp_last[qt] = ev
                            pss_free[pi] = (b.s_act, ev)
                    for qt in range(4):
                        qtg = qb * 4 + qt
                        nblk = ((qtg + 1) * 128 + 511) // 512
                        b.op("vector", lambda e, qt=qt, nblk=nblk: e.reduce_sum(
                            atot[:, qt:qt + 1], asum[:, qt * 4: qt * 4 + nblk], axis=AX.X),
                            waits=[(b.s_act, exp_last[qt])], inc=b.s_dve)
                    for qt in range(4):
                        b.op("vector", lambda e, qt=qt: e.reciprocal(
                            arcp[:, qt:qt + 1], atot[:, qt:qt + 1]), inc=b.s_dve)
                    pm_last = None
                    for qt in range(4):
                        qtg = qb * 4 + qt
                        klen = (qtg + 1) * 128
                        pm_last = b.op("vector", lambda e, qt=qt, klen=klen, pb=pb: e.tensor_scalar(
                            pb[:, qt * S: qt * S + klen], pb[:, qt * S: qt * S + klen],
                            arcp[:, qt:qt + 1], None, ALU.mult), inc=b.s_dve)
                    sum_done = pm_last
                    pm_of[qb] = pm_last

                def a_stage2(qb):
                    pb = p4[qb % 2][:]
                    pm_last = pm_of[qb]
                    kmax = (qb + 1) * 4
                    ptev = None
                    for kc in range(kmax):
                        psl = kc % 2
                        wl = [(b.s_dve, pm_last)]
                        if pst2_free[psl] is not None:
                            wl.append(pst2_free[psl])
                        lastt = None
                        for qt in range(4):
                            if qb * 4 + qt < kc:
                                continue
                            lastt = b.op("tensor", lambda e, qt=qt, kc=kc, psl=psl, pb=pb: e.transpose(
                                pst2[psl][:, qt * 128:(qt + 1) * 128],
                                pb[:, qt * S + kc * 128: qt * S + (kc + 1) * 128],
                                self.t_identb[:]), waits=wl, inc=b.s_pe)
                            wl = []
                        q0 = max(0, kc - qb * 4)
                        if q0 > 0:
                            b.op("vector", lambda e, kc=kc, q0=q0: e.memset(
                                PT[:, kc * 512: kc * 512 + q0 * 128], 0.0), inc=b.s_dve)
                        ptev = b.op("vector", lambda e, kc=kc, psl=psl, q0=q0: e.tensor_copy(
                            PT[:, kc * 512 + q0 * 128:(kc + 1) * 512],
                            pst2[psl][:, q0 * 128:512]),
                            waits=[(b.s_pe, lastt)], inc=b.s_dve)
                        pst2_free[psl] = (b.s_dve, ptev)
                    for vc in range(16):
                        pi = vc % 2
                        wl = [(b.s_dve, ptev)]
                        if psa_free[pi] is not None:
                            wl.append(psa_free[pi])
                        last = None
                        for kc in range(kmax):
                            last = b.op("tensor", lambda e, kc=kc, vc=vc, pi=pi,
                                        sp_=(kc == kmax - 1): e.matmul(
                                psa[pi][:], xn[:, kc * D + vc * 128: kc * D + (vc + 1) * 128],
                                PT[:, kc * 512: (kc + 1) * 512],
                                start=(kc == 0), stop=sp_), waits=wl,
                                inc=(b.s_pe if kc == kmax - 1 else None))
                            wl = []
                        av = b.op("vector", lambda e, vc=vc, pi=pi: e.tensor_copy(
                            avT[:, vc * 512:(vc + 1) * 512], psa[pi][:]),
                            waits=[(b.s_pe, last)], inc=b.s_dve)
                        psa_free[pi] = (b.s_dve, av)
                    av_done = b.s_dve.n
                    for oc in range(16):
                        sl = oc % 2
                        wl = [wob_free[sl]] if wob_free[sl] is not None else []
                        wv = self.load_w("wo", oc, wob[sl], b.ssem(f"awo{sl}"), waits=wl)
                        pi = oc % 2
                        wl = [wv, (b.s_dve, av_done)]
                        if psa_free[pi] is not None:
                            wl.append(psa_free[pi])
                        last = None
                        for hc in range(16):
                            last = b.op("tensor", lambda e, hc=hc, sl=sl, pi=pi: e.matmul(
                                psa[pi][:], wob[sl][:, hc * 128:(hc + 1) * 128],
                                avT[:, hc * 512:(hc + 1) * 512], start=(hc == 0), stop=(hc == 15)),
                                waits=wl, inc=(b.s_pe if hc == 15 else None))
                            wl = []
                        wob_free[sl] = (b.s_pe, last)
                        yv = b.op("vector", lambda e, oc=oc, pi=pi: e.tensor_scalar(
                            yT[:, oc * 512:(oc + 1) * 512], psa[pi][:], bo_sb[:, oc:oc + 1],
                            None, ALU.add), waits=[(b.s_pe, last), bo_ld], inc=b.s_dve)
                        psa_free[pi] = (b.s_dve, yv)
                    ya_of[qb] = b.s_dve.n

                def a_stage3(qb):
                    nonlocal slab_free
                    yT_done = ya_of[qb]
                    for i in range(4):
                        t = qb * 4 + i
                        ri = i % 2
                        wl = [rx_free[ri]] if rx_free[ri] is not None else []
                        rxl = b.dma("sync", rx[ri][:], src_fn(t), waits=wl, sem=rx_sems[ri])
                        stv, sfree = self.emit_y_add(yT, (b.s_dve, yT_done), rx[ri],
                                                     (rx_sems[ri], rxl), i,
                                                     dst_fn(t), slab[:], slab_free,
                                                     rms_col=None)
                        slab_free = sfree
                        rx_free[ri] = stv

                import os as _os
                if _os.environ.get("MK_A_SEQ"):
                    for _q in range(4):
                        a_stage1(_q); a_stage2(_q); a_stage3(_q)
                else:
                    a_stage1(0)
                    a_stage1(1)
                    a_stage2(0)
                    a_stage3(0)
                    a_stage1(2)
                    a_stage2(1)
                    a_stage3(1)
                    a_stage1(3)
                    a_stage2(2)
                    a_stage3(2)
                    a_stage2(3)
                    a_stage3(3)
        b.barrier()

    def attn_b(self, seq, src, dst):
        stages = int(os.environ.get("MK_B_STAGES", "4"))
        b = self.b
        src_fn = lambda t: src[seq, t * 128:(t + 1) * 128, :]
        dst_fn = lambda t: dst[seq, t * 128:(t + 1) * 128, :]
        with ExitStack() as st:
            qanT = b.sb("b_qanT", [128, 12 * S], BF16, st)
            ckvnT = b.sb("b_ckvnT", [128, 4 * S], BF16, st)
            kpeT = b.sb("b_kpeT", [64, S], BF16, st)
            with ExitStack() as st1:
                rx = [b.sb(f"b_rx{i}", [128, D], F32, st1) for i in range(2)]
                xt4 = [b.sb(f"b_xt{i}", [128, D], BF16, st1) for i in range(4)]
                xnT = b.sb("b_xnT", [128, 16 * 512], BF16, st1)
                qaw = [b.sb(f"b_qaw{i}", [128, 16 * 128], BF16, st1) for i in range(2)]
                kvaw = b.sb("b_kvaw", [128, 4 * 2048], BF16, st1)
                kvpw = b.sb("b_kvpw", [128, 16 * 64], BF16, st1)
                qalnw_sb = b.sb("b_qalnw", [128, 12], F32, st1)
                kvlnw_sb = b.sb("b_kvlnw", [128, 4], F32, st1)
                sqq = b.sb("b_sqq", [128, 12 * 512], BF16, st1)
                sqk = b.sb("b_sqk", [128, 4 * 512], BF16, st1)
                qaraw = b.sb("b_qaraw", [128, 12 * 512], BF16, st1)
                kvraw = b.sb("b_kvraw", [128, 4 * 512], BF16, st1)
                rrow = b.sb("b_rrow", [1, 512], F32, st1)
                rkc = b.sb("b_rkc", [64, S], BF16, st1)
                rks = b.sb("b_rks", [64, S], BF16, st1)
                rotk_sb = b.sb("b_rotk", [64, 64], BF16, st1)
                t1b = b.sb("b_t1", [128, 512], BF16, st1)
                t2b = b.sb("b_t2", [128, 512], BF16, st1)
                psq = [b.ps(f"b_pq{i}", [128, 512], F32, st1) for i in range(2)]
                psbt = [b.ps(f"b_pt{i}", [128, 1024], BF16, st1) for i in range(2)]
                psr = b.ps("b_psr", [1, 512], F32, st1)
                psrep = b.ps("b_psrep", [128, 512], F32, st1)
                psro = b.ps("b_psro", [64, 512], F32, st1)
                s_kw = b.ssem("bkw")
                s_kt = b.ssem("bkt")
                self.load_w("kva", 0, kvaw[:, 0:2048], s_kw)
                self.load_w("kva", 1, kvaw[:, 2048:4096], s_kw)
                self.load_w("kva", 2, kvaw[:, 4096:6144], s_kw)
                self.load_w("kva", 3, kvaw[:, 6144:8192], s_kw)
                self.load_w("kvape", 0, kvpw, s_kw)
                b.dma("sync", qalnw_sb[:], self.qalnw[:], sem=s_kt)
                b.dma("sync", kvlnw_sb[:], self.kvlnw[:], sem=s_kt)
                b.dma("sync", rkc[:], self.ropekc[:], sem=s_kt)
                b.dma("sync", rks[:], self.ropeks[:], sem=s_kt)
                b.dma("sync", rotk_sb[:], self.rotk[:], sem=s_kt)
                kv_ld = (s_kw, s_kw.n)
                kt_ld = (s_kt, s_kt.n)
                rx_sems = [b.ssem("brx0"), b.ssem("brx1")]
                rx_free = [None, None]
                psbt_free = [None, None]
                psq_free = [None, None]
                qaw_free = [None, None]
                s_qw = [b.ssem("bqw0"), b.ssem("bqw1")]
                for stt_ in range(4):
                    xn_vals = []
                    for gg in range(2):
                        tiles = [stt_ * 4 + gg * 2, stt_ * 4 + gg * 2 + 1]
                        ld, xv = self.emit_ln(src_fn, tiles, rx, rx_sems, rx_free,
                                              lambda i, gg=gg: xt4[gg * 2 + i][:],
                                              None, self.t_eps5, col0=gg * 2)
                        xn_vals += xv
                        rx_free = [(b.s_act, xv[0]), (b.s_act, xv[1])]
                    xnT_last = None
                    for kc in range(16):
                        psl = kc % 2
                        for i in range(4):
                            wl = [(b.s_act, xn_vals[i])]
                            if i == 0 and psbt_free[psl] is not None:
                                wl.append(psbt_free[psl])
                            lastt = b.op("tensor", lambda e, kc=kc, i=i, psl=psl: e.transpose(
                                psbt[psl][:, i * 128:(i + 1) * 128],
                                xt4[i][:, kc * 128:(kc + 1) * 128],
                                self.t_identb[:]), waits=wl, inc=b.s_pe)
                        xnT_last = b.op("vector", lambda e, kc=kc, psl=psl: e.tensor_copy(
                            xnT[:, kc * 512:(kc + 1) * 512], psbt[psl][:, 0:512]),
                            waits=[(b.s_pe, lastt)], inc=b.s_dve)
                        psbt_free[psl] = (b.s_dve, xnT_last)
                    for m in range(12):
                        sl = m % 2
                        wlw = qaw_free[sl] or []
                        wv = self.load_w("qa", m, qaw[sl], s_qw[sl], waits=wlw)
                        pi = m % 2
                        wl = [wv, (b.s_dve, xnT_last)]
                        wl += psq_free[pi] or []
                        last = None
                        for kc in range(16):
                            last = b.op("tensor", lambda e, kc=kc, sl=sl, pi=pi: e.matmul(
                                psq[pi][:], qaw[sl][:, kc * 128:(kc + 1) * 128],
                                xnT[:, kc * 512:(kc + 1) * 512], start=(kc == 0), stop=(kc == 15)),
                                waits=wl, inc=(b.s_pe if kc == 15 else None))
                            wl = []
                        qaw_free[sl] = [(b.s_pe, last)]
                        ev = b.op("vector", lambda e, m=m, pi=pi: e.tensor_copy(
                            qaraw[:, m * 512:(m + 1) * 512], psq[pi][:]),
                            waits=[(b.s_pe, last)], inc=b.s_dve)
                        b.op("scalar", lambda e, m=m: e.activation(
                            sqq[:, m * 512:(m + 1) * 512], qaraw[:, m * 512:(m + 1) * 512],
                            AF.Square), waits=[(b.s_dve, ev)], inc=b.s_act)
                        psq_free[pi] = [(b.s_dve, ev)]
                    for m in range(4):
                        pi = m % 2
                        wl = [kv_ld, kt_ld, (b.s_dve, xnT_last)]
                        wl += psq_free[pi] or []
                        last = None
                        for kc in range(16):
                            last = b.op("tensor", lambda e, kc=kc, m=m, pi=pi: e.matmul(
                                psq[pi][:], kvaw[:, m * 2048 + kc * 128: m * 2048 + (kc + 1) * 128],
                                xnT[:, kc * 512:(kc + 1) * 512], start=(kc == 0), stop=(kc == 15)),
                                waits=wl, inc=(b.s_pe if kc == 15 else None))
                            wl = []
                        ev = b.op("vector", lambda e, m=m, pi=pi: e.tensor_copy(
                            kvraw[:, m * 512:(m + 1) * 512], psq[pi][:]),
                            waits=[(b.s_pe, last)], inc=b.s_dve)
                        b.op("scalar", lambda e, m=m: e.activation(
                            sqk[:, m * 512:(m + 1) * 512], kvraw[:, m * 512:(m + 1) * 512],
                            AF.Square), waits=[(b.s_dve, ev)], inc=b.s_act)
                        psq_free[pi] = [(b.s_dve, ev)]
                    wl = list(psq_free[0] or [])
                    last = None
                    for kc in range(16):
                        last = b.op("tensor", lambda e, kc=kc: e.matmul(
                            psq[0][0:64, :], kvpw[:, kc * 64:(kc + 1) * 64],
                            xnT[:, kc * 512:(kc + 1) * 512], start=(kc == 0), stop=(kc == 15)),
                            waits=wl, inc=(b.s_pe if kc == 15 else None))
                        wl = []
                    kpv = b.op("vector", lambda e, stt_=stt_: e.tensor_copy(
                        kpeT[0:64, stt_ * 512:(stt_ + 1) * 512], psq[0][0:64, :]),
                        waits=[(b.s_pe, last)], inc=b.s_dve)
                    psq_free[0] = [(b.s_dve, kpv)]
                    sq_done = (b.s_act, b.s_act.n)
                    last = None
                    wl = [sq_done, (b.s_dve, b.s_dve.n)]
                    for m in range(12):
                        last = b.op("tensor", lambda e, m=m: e.matmul(
                            psr[0:1, :], self.t_onesb[:], sqq[:, m * 512:(m + 1) * 512],
                            start=(m == 0), stop=(m == 11)), waits=wl,
                            inc=(b.s_pe if m == 11 else None))
                        wl = []
                    s1 = b.op("scalar", lambda e: e.activation(rrow[0:1, :], psr[0:1, :],
                         AF.Sqrt, bias=self.t_eps6[0:1, :], scale=1.0 / 1536),
                         waits=[(b.s_pe, last)], inc=b.s_act)
                    r1 = b.op("vector", lambda e: e.reciprocal(rrow[0:1, :], rrow[0:1, :]),
                         waits=[(b.s_act, s1)], inc=b.s_dve)
                    m1 = b.op("tensor", lambda e: e.matmul(psrep[:], self.t_scb[:],
                         rrow[0:1, :], start=True, stop=True), waits=[(b.s_dve, r1)], inc=b.s_pe)
                    ap_last = None
                    for m in range(12):
                        ap_last = b.op("vector", lambda e, m=m, stt_=stt_: e.scalar_tensor_tensor(
                            qanT[:, m * S + stt_ * 512: m * S + (stt_ + 1) * 512],
                            qaraw[:, m * 512:(m + 1) * 512], qalnw_sb[:, m:m + 1], psrep[:],
                            ALU.mult, ALU.mult), waits=[(b.s_pe, m1)], inc=b.s_dve)
                    last = None
                    wl = [(b.s_dve, ap_last)]
                    for m in range(4):
                        last = b.op("tensor", lambda e, m=m: e.matmul(
                            psr[0:1, :], self.t_onesb[:], sqk[:, m * 512:(m + 1) * 512],
                            start=(m == 0), stop=(m == 3)), waits=wl,
                            inc=(b.s_pe if m == 3 else None))
                        wl = []
                    s2 = b.op("scalar", lambda e: e.activation(rrow[0:1, :], psr[0:1, :],
                         AF.Sqrt, bias=self.t_eps6[0:1, :], scale=1.0 / 512),
                         waits=[(b.s_pe, last)], inc=b.s_act)
                    r2 = b.op("vector", lambda e: e.reciprocal(rrow[0:1, :], rrow[0:1, :]),
                         waits=[(b.s_act, s2)], inc=b.s_dve)
                    m2 = b.op("tensor", lambda e: e.matmul(psrep[:], self.t_ones[:],
                         rrow[0:1, :], start=True, stop=True), waits=[(b.s_dve, r2)], inc=b.s_pe)
                    for m in range(4):
                        b.op("vector", lambda e, m=m, stt_=stt_: e.scalar_tensor_tensor(
                            ckvnT[:, m * S + stt_ * 512: m * S + (stt_ + 1) * 512],
                            kvraw[:, m * 512:(m + 1) * 512], kvlnw_sb[:, m:m + 1], psrep[:],
                            ALU.mult, ALU.mult), waits=[(b.s_pe, m2)], inc=b.s_dve)
                kpe_done = (b.s_dve, b.s_dve.n)
                psro_free = None
                for blk in range(4):
                    wl = [kpe_done, kt_ld]
                    if psro_free is not None:
                        wl.append(psro_free)
                    rmv = b.op("tensor", lambda e, blk=blk: e.matmul(
                        psro[:], rotk_sb[:], kpeT[0:64, blk * 512:(blk + 1) * 512],
                        start=True, stop=True), waits=wl, inc=b.s_pe)
                    tv1 = b.op("vector", lambda e, blk=blk: e.tensor_tensor(
                        t1b[0:64, :], kpeT[0:64, blk * 512:(blk + 1) * 512],
                        rkc[0:64, blk * 512:(blk + 1) * 512], ALU.mult), inc=b.s_dve)
                    tv2 = b.op("vector", lambda e, blk=blk: e.tensor_tensor(
                        t2b[0:64, :], psro[:],
                        rks[0:64, blk * 512:(blk + 1) * 512], ALU.mult),
                        waits=[(b.s_pe, rmv)], inc=b.s_dve)
                    fv = b.op("vector", lambda e, blk=blk: e.tensor_tensor(
                        kpeT[0:64, blk * 512:(blk + 1) * 512], t1b[0:64, :], t2b[0:64, :],
                        ALU.add), inc=b.s_dve)
                    psro_free = (b.s_dve, fv)
                b.barrier()
            if stages < 2:
                return
            with ExitStack() as st2:
                qstage = b.sb("b_qst", [128, S], BF16, st2)
                qpw = [b.sb(f"b_qpw{i}", [128, 12 * 128], BF16, st2) for i in range(2)]
                rqc = b.sb("b_rqc", [128, S], BF16, st2)
                rqs = b.sb("b_rqs", [128, S], BF16, st2)
                rotq_sb = b.sb("b_rotq", [128, 128], BF16, st2)
                t1q = b.sb("b_t1q", [128, 512], BF16, st2)
                t2q = b.sb("b_t2q", [128, 512], BF16, st2)
                psq2 = [b.ps(f"b2_pq{i}", [128, 512], F32, st2) for i in range(2)]
                psb = [b.ps(f"b2_pb{i}", [128, 512], F32, st2) for i in range(2)]
                s_t = b.ssem("bqt")
                b.dma("sync", rqc[:], self.ropeqc[:], sem=s_t)
                b.dma("sync", rqs[:], self.ropeqs[:], sem=s_t)
                b.dma("sync", rotq_sb[:], self.rotq[:], sem=s_t)
                tab_ld = (s_t, s_t.n)
                s_qp = [b.ssem("bqp0"), b.ssem("bqp1")]
                qpw_free = [None, None]
                psq_free = [None, None]
                psb_free = [None, None]
                s_qpe_st = b.ssem("bqpes")
                qstage_war = None
                for j in range(8):
                    sl = j % 2
                    wlw = qpw_free[sl] or []
                    wv = self.load_w("qbp", j, qpw[sl], s_qp[sl], waits=wlw)
                    stage_vals = []
                    for tblk in range(4):
                        pi = tblk % 2
                        wl = [wv]
                        wl += psq_free[pi] or []
                        if tblk == 0 and qstage_war is not None:
                            wl.append(qstage_war)
                        last = None
                        for kc in range(12):
                            last = b.op("tensor", lambda e, kc=kc, sl=sl, tblk=tblk, pi=pi: e.matmul(
                                psq2[pi][:], qpw[sl][:, kc * 128:(kc + 1) * 128],
                                qanT[:, kc * S + tblk * 512: kc * S + (tblk + 1) * 512],
                                start=(kc == 0), stop=(kc == 11)),
                                waits=wl, inc=(b.s_pe if kc == 11 else None))
                            wl = []
                        evw = [(b.s_pe, last)]
                        if tblk == 0 and qstage_war is not None:
                            evw.append(qstage_war)
                        ev = b.op("vector", lambda e, tblk=tblk, pi=pi: e.tensor_copy(
                            qstage[:, tblk * 512:(tblk + 1) * 512], psq2[pi][:]),
                            waits=evw, inc=b.s_dve)
                        psq_free[pi] = [(b.s_dve, ev)]
                        stage_vals.append(ev)
                    qpw_free[sl] = [(b.s_pe, b.s_pe.n)]
                    rope_last = None
                    for tblk in range(4):
                        psl = tblk % 2
                        wl = [(b.s_dve, stage_vals[tblk]), tab_ld]
                        wl += psb_free[psl] or []
                        rmv = b.op("tensor", lambda e, tblk=tblk, psl=psl: e.matmul(
                            psb[psl][:], rotq_sb[:], qstage[:, tblk * 512:(tblk + 1) * 512],
                            start=True, stop=True), waits=wl, inc=b.s_pe)
                        b.op("vector", lambda e, tblk=tblk: e.tensor_tensor(
                            t1q[:], qstage[:, tblk * 512:(tblk + 1) * 512],
                            rqc[:, tblk * 512:(tblk + 1) * 512], ALU.mult), inc=b.s_dve)
                        b.op("vector", lambda e, tblk=tblk, psl=psl: e.tensor_tensor(
                            t2q[:], psb[psl][:],
                            rqs[:, tblk * 512:(tblk + 1) * 512], ALU.mult),
                            waits=[(b.s_pe, rmv)], inc=b.s_dve)
                        rope_last = b.op("vector", lambda e, tblk=tblk: e.tensor_tensor(
                            qstage[:, tblk * 512:(tblk + 1) * 512], t1q[:], t2q[:],
                            ALU.add), inc=b.s_dve)
                        psb_free[psl] = [(b.s_dve, rope_last)]
                    b.dma("sync", self.qpe_spill[2 * j, :, :], qstage[0:64, :],
                          waits=[(b.s_dve, rope_last)], sem=s_qpe_st)
                    v2 = b.dma("sync", self.qpe_spill[2 * j + 1, :, :], qstage[64:128, :],
                               sem=s_qpe_st)
                    qstage_war = (s_qpe_st, v2)
                b.barrier()
            if stages < 3:
                return
            with ExitStack() as st3:
                knT = b.sb("b_knT", [128, S], BF16, st3)
                Vb = b.sb("b_V", [128, 16 * 128], BF16, st3)
                qnT = b.sb("b_qnT", [128, S], BF16, st3)
                qpeh = [b.sb(f"b_qpeh{i}", [64, S], BF16, st3) for i in range(2)]
                sq4 = b.sb("b_sq4", [128, S], BF16, st3)
                sq4k = b.sb("b_sq4k", [128, S], BF16, st3)
                sqpe = b.sb("b_sqpe", [64, S], BF16, st3)
                sqkpe = b.sb("b_sqkpe", [64, S], BF16, st3)
                p4 = [b.sb(f"b_p4{i}", [128, 4 * S], BF16, st3) for i in range(2)]
                PT = b.sb("b_PT", [128, 16 * 512], BF16, st3)
                avb = b.sb("b_avb", [128, 512], BF16, st3)
                qbnw = [b.sb(f"b_qbnw{i}", [128, 12 * 128], BF16, st3) for i in range(2)]
                kvbnw = [b.sb(f"b_kvbnw{i}", [128, 4 * 128], BF16, st3) for i in range(2)]
                kvbvw = [b.sb(f"b_kvbvw{i}", [128, 4 * 128], BF16, st3) for i in range(2)]
                nmx = b.sb("b_nmx", [1, 16], F32, st3)
                ctmp = b.sb("b_ctmp", [1, 4], F32, st3)
                cbias = b.sb("b_cb", [128, 1], F32, st3)
                asum = b.sb("b_as", [128, 16], F32, st3)
                atot = b.sb("b_at", [128, 4], F32, st3)
                arcp = b.sb("b_ar", [128, 4], F32, st3)
                pss = [b.ps(f"b_ss{i}", [128, 512], F32, st3) for i in range(2)]
                pst2 = [b.ps(f"b_pt2{i}", [128, 1024], BF16, st3) for i in range(2)]
                psa = [b.ps(f"b_pa{i}", [128, 512], F32, st3) for i in range(2)]
                psn = b.ps("b_psn", [1, 512], F32, st3)
                psc = b.ps("b_psc", [128, 1], F32, st3)
                s_bw = [b.ssem("bbw0"), b.ssem("bbw1")]
                s_qh = [b.ssem("bqh0"), b.ssem("bqh1")]
                s_av = b.ssem("bav")
                kq_last = None
                for blk in range(4):
                    kq_last = b.op("scalar", lambda e, blk=blk: e.activation(
                        sqkpe[0:64, blk * 512:(blk + 1) * 512],
                        kpeT[0:64, blk * 512:(blk + 1) * 512], AF.Square), inc=b.s_act)
                bw_free = [None, None]
                qh_free = [None, None]
                pss_free = [None, None]
                pst2_free = [None, None]
                psa_free = [None, None]
                av_pend = None
                for h in range(16):
                    sl = h % 2
                    wlw = bw_free[sl] or []
                    wq = self.load_w("qbn", h, qbnw[sl], s_bw[sl], waits=wlw)
                    wn = self.load_w("kvbn", h, kvbnw[sl], s_bw[sl])
                    wv_ = self.load_w("kvbv", h, kvbvw[sl], s_bw[sl])
                    wload = (s_bw[sl], s_bw[sl].n)
                    wlq = qh_free[sl] or []
                    qpl = b.dma("sync", qpeh[sl][0:64, :], self.qpe_spill[h, :, :],
                                waits=wlq, sem=s_qh[sl])
                    qpe_ld = (s_qh[sl], qpl)
                    kn_vals = []
                    for blk in range(4):
                        pi = blk % 2
                        wl = [wload]
                        wl += pss_free[pi] or []
                        last = None
                        for kc in range(4):
                            last = b.op("tensor", lambda e, kc=kc, blk=blk, pi=pi, sl=sl: e.matmul(
                                pss[pi][:], kvbnw[sl][:, kc * 128:(kc + 1) * 128],
                                ckvnT[:, kc * S + blk * 512: kc * S + (blk + 1) * 512],
                                start=(kc == 0), stop=(kc == 3)),
                                waits=wl, inc=(b.s_pe if kc == 3 else None))
                            wl = []
                        ev = b.op("vector", lambda e, blk=blk, pi=pi: e.tensor_copy(
                            knT[:, blk * 512:(blk + 1) * 512], pss[pi][:]),
                            waits=[(b.s_pe, last)], inc=b.s_dve)
                        sv = b.op("scalar", lambda e, blk=blk: e.activation(
                            sq4k[:, blk * 512:(blk + 1) * 512], knT[:, blk * 512:(blk + 1) * 512],
                            AF.Square), waits=[(b.s_dve, ev)], inc=b.s_act)
                        pss_free[pi] = [(b.s_dve, ev)]
                        kn_vals.append((ev, sv))
                    v_vals = []
                    for kt in range(16):
                        pi = kt % 2
                        wl = [wload]
                        wl += psa_free[pi] or []
                        last = None
                        for kc in range(4):
                            last = b.op("tensor", lambda e, kc=kc, kt=kt, pi=pi, sl=sl: e.matmul(
                                psa[pi][:, 0:128], ckvnT[:, kc * S + kt * 128: kc * S + (kt + 1) * 128],
                                kvbvw[sl][:, kc * 128:(kc + 1) * 128],
                                start=(kc == 0), stop=(kc == 3)),
                                waits=wl, inc=(b.s_pe if kc == 3 else None))
                            wl = []
                        ev = b.op("vector", lambda e, kt=kt, pi=pi: e.tensor_copy(
                            Vb[:, kt * 128:(kt + 1) * 128], psa[pi][:, 0:128]),
                            waits=[(b.s_pe, last)], inc=b.s_dve)
                        psa_free[pi] = [(b.s_dve, ev)]
                        v_vals.append(ev)
                    v_done = b.s_dve.n
                    qn_vals = []
                    for tblk in range(4):
                        pi = tblk % 2
                        wl = [wload]
                        wl += pss_free[pi] or []
                        last = None
                        for kc in range(12):
                            last = b.op("tensor", lambda e, kc=kc, tblk=tblk, pi=pi, sl=sl: e.matmul(
                                pss[pi][:], qbnw[sl][:, kc * 128:(kc + 1) * 128],
                                qanT[:, kc * S + tblk * 512: kc * S + (tblk + 1) * 512],
                                start=(kc == 0), stop=(kc == 11)),
                                waits=wl, inc=(b.s_pe if kc == 11 else None))
                            wl = []
                        ev = b.op("vector", lambda e, tblk=tblk, pi=pi: e.tensor_copy(
                            qnT[:, tblk * 512:(tblk + 1) * 512], pss[pi][:]),
                            waits=[(b.s_pe, last)], inc=b.s_dve)
                        sv = b.op("scalar", lambda e, tblk=tblk: e.activation(
                            sq4[:, tblk * 512:(tblk + 1) * 512], qnT[:, tblk * 512:(tblk + 1) * 512],
                            AF.Square), waits=[(b.s_dve, ev)], inc=b.s_act)
                        pss_free[pi] = [(b.s_dve, ev)]
                        qn_vals.append((ev, sv))
                    qpe_sq = []
                    for blk in range(4):
                        qpe_sq.append(b.op("scalar", lambda e, blk=blk, sl=sl: e.activation(
                            sqpe[0:64, blk * 512:(blk + 1) * 512],
                            qpeh[sl][0:64, blk * 512:(blk + 1) * 512], AF.Square),
                            waits=[qpe_ld], inc=b.s_act))
                    psn_free = None
                    for blk in range(4):
                        wl = [(b.s_act, kn_vals[blk][1]), (b.s_act, kq_last)]
                        if psn_free is not None:
                            wl.append(psn_free)
                        b.op("tensor", lambda e, blk=blk: e.matmul(
                            psn[0:1, :], self.t_onesb[:], sq4k[:, blk * 512:(blk + 1) * 512],
                            start=True, stop=False), waits=wl, inc=None)
                        last = b.op("tensor", lambda e, blk=blk: e.matmul(
                            psn[0:1, :], self.t_onesb[0:64, :], sqkpe[0:64, blk * 512:(blk + 1) * 512],
                            start=False, stop=True), inc=b.s_pe)
                        rv = b.op("vector", lambda e, blk=blk: e.reduce_max(
                            nmx[0:1, 4 + blk:5 + blk], psn[0:1, :], axis=AX.X),
                            waits=[(b.s_pe, last)], inc=b.s_dve)
                        psn_free = (b.s_dve, rv)
                    psn_free2 = psn_free
                    for blk in range(4):
                        wl = [(b.s_act, qn_vals[blk][1]), (b.s_act, qpe_sq[blk])]
                        if psn_free2 is not None:
                            wl.append(psn_free2)
                        b.op("tensor", lambda e, blk=blk: e.matmul(
                            psn[0:1, :], self.t_onesb[:], sq4[:, blk * 512:(blk + 1) * 512],
                            start=True, stop=False), waits=wl, inc=None)
                        last = b.op("tensor", lambda e, blk=blk: e.matmul(
                            psn[0:1, :], self.t_onesb[0:64, :], sqpe[0:64, blk * 512:(blk + 1) * 512],
                            start=False, stop=True), inc=b.s_pe)
                        rv = b.op("vector", lambda e, blk=blk: e.reduce_max(
                            nmx[0:1, blk:blk + 1], psn[0:1, :], axis=AX.X),
                            waits=[(b.s_pe, last)], inc=b.s_dve)
                        psn_free2 = (b.s_dve, rv)
                    b.op("vector", lambda e: e.reduce_max(ctmp[0:1, 1:2], nmx[0:1, 4:8],
                         axis=AX.X), inc=b.s_dve)
                    dv0 = b.op("vector", lambda e: e.reduce_max(ctmp[0:1, 0:1], nmx[0:1, 0:4],
                         axis=AX.X), inc=b.s_dve)
                    pv = b.op("gpsimd", lambda e: e.tensor_mul(ctmp[0:1, 2:3], ctmp[0:1, 0:1],
                         ctmp[0:1, 1:2]), waits=[(b.s_dve, dv0)], inc=b.s_pool)
                    sv = b.op("scalar", lambda e: e.activation(ctmp[0:1, 3:4], ctmp[0:1, 2:3],
                         AF.Sqrt), waits=[(b.s_pool, pv)], inc=b.s_act)
                    nv = b.op("gpsimd", lambda e: e.tensor_scalar(ctmp[0:1, 2:3], ctmp[0:1, 3:4],
                         -1.0, None, ALU.mult), waits=[(b.s_act, sv)], inc=b.s_pool)
                    mv = b.op("tensor", lambda e: e.matmul(psc[:, 0:1], self.t_ones[:],
                         ctmp[0:1, 2:3], start=True, stop=True), waits=[(b.s_pool, nv)], inc=b.s_pe)
                    cbv = b.op("vector", lambda e: e.tensor_copy(cbias[:], psc[:, 0:1]),
                         waits=[(b.s_pe, mv), (b.s_act, b.s_act.n)], inc=b.s_dve)
                    sum_done = cbv
                    pm_of = {}

                    def stage1(qb, sl=sl, qpe_ld=qpe_ld, cbv=cbv):
                        nonlocal sum_done
                        pb = p4[qb % 2][:]
                        ssi = 0
                        exp_last = [0] * 4
                        for qt in range(4):
                            qtg = qb * 4 + qt
                            klen = (qtg + 1) * 128
                            nblk = (klen + 511) // 512
                            for blk in range(nblk):
                                w = min(512, klen - blk * 512)
                                pi = ssi % 2
                                ssi += 1
                                wl = list(pss_free[pi] or [])
                                last = b.op("tensor", lambda e, qtg=qtg, blk=blk, w=w, pi=pi: e.matmul(
                                    pss[pi][:, 0:w], qnT[:, qtg * 128:(qtg + 1) * 128],
                                    knT[:, blk * 512: blk * 512 + w],
                                    start=True, stop=False), waits=wl, inc=None)
                                last = b.op("tensor", lambda e, qtg=qtg, blk=blk, w=w, pi=pi, sl=sl: e.matmul(
                                    pss[pi][:, 0:w], qpeh[sl][0:64, qtg * 128:(qtg + 1) * 128],
                                    kpeT[0:64, blk * 512: blk * 512 + w],
                                    start=False, stop=True), waits=[qpe_ld], inc=b.s_pe)
                                expw = [(b.s_pe, last)]
                                if blk == nblk - 1:
                                    mkv = b.op("vector", lambda e, w=w, pi=pi: e.tensor_tensor(
                                        pss[pi][:, w - 128:w], pss[pi][:, w - 128:w], self.t_mask[:],
                                        ALU.add), waits=[(b.s_pe, last)], inc=b.s_dve)
                                    expw = [(b.s_dve, mkv)]
                                ev = b.op("scalar", lambda e, qt=qt, blk=blk, w=w, pi=pi, pb=pb: e.activation(
                                    pb[:, qt * S + blk * 512: qt * S + blk * 512 + w], pss[pi][:, 0:w],
                                    AF.Exp, bias=cbias[:], scale=1.0,
                                    accum_out=asum[:, qt * 4 + blk: qt * 4 + blk + 1]),
                                    waits=expw + [(b.s_dve, sum_done), (b.s_dve, cbv)], inc=b.s_act)
                                exp_last[qt] = ev
                                pss_free[pi] = [(b.s_act, ev)]
                        for qt in range(4):
                            qtg = qb * 4 + qt
                            nblk = ((qtg + 1) * 128 + 511) // 512
                            b.op("vector", lambda e, qt=qt, nblk=nblk: e.reduce_sum(
                                atot[:, qt:qt + 1], asum[:, qt * 4: qt * 4 + nblk], axis=AX.X),
                                waits=[(b.s_act, exp_last[qt])], inc=b.s_dve)
                        for qt in range(4):
                            b.op("vector", lambda e, qt=qt: e.reciprocal(
                                arcp[:, qt:qt + 1], atot[:, qt:qt + 1]), inc=b.s_dve)
                        pm_last = None
                        for qt in range(4):
                            qtg = qb * 4 + qt
                            klen = (qtg + 1) * 128
                            pm_last = b.op("vector", lambda e, qt=qt, klen=klen, pb=pb: e.tensor_scalar(
                                pb[:, qt * S: qt * S + klen], pb[:, qt * S: qt * S + klen],
                                arcp[:, qt:qt + 1], None, ALU.mult), inc=b.s_dve)
                        sum_done = pm_last
                        pm_of[qb] = pm_last

                    def stage2(qb, h=h, sl=sl):
                        nonlocal av_pend
                        pb = p4[qb % 2][:]
                        pm_last = pm_of[qb]
                        kmax = (qb + 1) * 4
                        ptev = None
                        for kc in range(kmax):
                            psl = kc % 2
                            wl = [(b.s_dve, pm_last)]
                            wl += pst2_free[psl] or []
                            lastt = None
                            for qt in range(4):
                                if qb * 4 + qt < kc:
                                    continue
                                lastt = b.op("tensor", lambda e, qt=qt, kc=kc, psl=psl, pb=pb: e.transpose(
                                    pst2[psl][:, qt * 128:(qt + 1) * 128],
                                    pb[:, qt * S + kc * 128: qt * S + (kc + 1) * 128],
                                    self.t_identb[:]), waits=wl, inc=b.s_pe)
                                wl = []
                            q0 = max(0, kc - qb * 4)
                            if q0 > 0:
                                b.op("vector", lambda e, kc=kc, q0=q0: e.memset(
                                    PT[:, kc * 512: kc * 512 + q0 * 128], 0.0), inc=b.s_dve)
                            ptev = b.op("vector", lambda e, kc=kc, psl=psl, q0=q0: e.tensor_copy(
                                PT[:, kc * 512 + q0 * 128:(kc + 1) * 512],
                                pst2[psl][:, q0 * 128:512]),
                                waits=[(b.s_pe, lastt)], inc=b.s_dve)
                            pst2_free[psl] = [(b.s_dve, ptev)]
                        pi = qb % 2
                        wl = [(b.s_dve, ptev), (b.s_dve, v_done)]
                        wl += psa_free[pi] or []
                        last = None
                        for kc in range(kmax):
                            last = b.op("tensor", lambda e, kc=kc, pi=pi, sp_=(kc == kmax - 1): e.matmul(
                                psa[pi][:], Vb[:, kc * 128:(kc + 1) * 128],
                                PT[:, kc * 512:(kc + 1) * 512],
                                start=(kc == 0), stop=sp_), waits=wl,
                                inc=(b.s_pe if kc == kmax - 1 else None))
                            wl = []
                        avw = [(b.s_pe, last)]
                        if av_pend is not None:
                            avw.append(av_pend)
                        avv = b.op("vector", lambda e, pi=pi: e.tensor_copy(
                            avb[:], psa[pi][:]), waits=avw, inc=b.s_dve)
                        psa_free[pi] = [(b.s_dve, avv)]
                        stv = b.dma("sync", self.av_spill[h, :, qb * 512:(qb + 1) * 512],
                                    avb[:], waits=[(b.s_dve, avv)], sem=s_av)
                        av_pend = (s_av, stv)

                    stage1(0)
                    stage1(1)
                    stage2(0)
                    stage1(2)
                    stage2(1)
                    stage1(3)
                    stage2(2)
                    stage2(3)
                    bw_free[sl] = [(b.s_pe, b.s_pe.n)]
                    qh_free[sl] = [(b.s_pe, b.s_pe.n)]
                b.barrier()
            if stages < 4:
                return
            with ExitStack() as st4:
                avTs = b.sb("b_avTs", [128, 16 * 512], BF16, st4)
                yT = b.sb("b_yT", [128, 16 * 512], BF16, st4)
                owb = [b.sb(f"b_ow{i}", [128, 16 * 128], BF16, st4) for i in range(2)]
                rx3 = [b.sb(f"b3_rx{i}", [128, D], F32, st4) for i in range(2)]
                psa3 = [b.ps(f"b3_pa{i}", [128, 512], F32, st4) for i in range(2)]
                slab = b.ps("b3_slab", [128, 2048], BF16, st4)
                s_avl = [b.ssem("bavl0"), b.ssem("bavl1")]
                rx_sems = [b.ssem("brx0"), b.ssem("brx1")]
                ow_free = [None, None]
                psa_free = [None, None]
                slab_free = None
                rx_free = [None, None]
                avl_free = [None, None]
                for qb in range(4):
                    asl = qb % 2
                    avw = [avl_free[asl]] if avl_free[asl] is not None else []
                    avv = None
                    for hc in range(16):
                        avv = b.dma("sync", avTs[:, hc * 512:(hc + 1) * 512],
                                    self.av_spill[hc, :, qb * 512:(qb + 1) * 512],
                                    waits=avw, sem=s_avl[asl])
                        avw = []
                    av_ld = (s_avl[asl], avv)
                    for oc in range(16):
                        sl = oc % 2
                        wl = [ow_free[sl]] if ow_free[sl] is not None else []
                        wv = self.load_w("ow", oc, owb[sl], b.ssem(f"bow{sl}"), waits=wl)
                        pi = oc % 2
                        wl = [wv, av_ld]
                        if psa_free[pi] is not None:
                            wl.append(psa_free[pi])
                        last = None
                        for hc in range(16):
                            last = b.op("tensor", lambda e, hc=hc, sl=sl, pi=pi: e.matmul(
                                psa3[pi][:], owb[sl][:, hc * 128:(hc + 1) * 128],
                                avTs[:, hc * 512:(hc + 1) * 512], start=(hc == 0), stop=(hc == 15)),
                                waits=wl, inc=(b.s_pe if hc == 15 else None))
                            wl = []
                        ow_free[sl] = (b.s_pe, last)
                        yv = b.op("vector", lambda e, oc=oc, pi=pi: e.tensor_copy(
                            yT[:, oc * 512:(oc + 1) * 512], psa3[pi][:]),
                            waits=[(b.s_pe, last)], inc=b.s_dve)
                        psa_free[pi] = (b.s_dve, yv)
                    yT_done = b.s_dve.n
                    avl_free[asl] = (b.s_pe, b.s_pe.n)
                    for i in range(4):
                        t = qb * 4 + i
                        ri = i % 2
                        wl = [rx_free[ri]] if rx_free[ri] is not None else []
                        rxl = b.dma("sync", rx3[ri][:], src_fn(t), waits=wl, sem=rx_sems[ri])
                        stv, sfree = self.emit_y_add(yT, (b.s_dve, yT_done), rx3[ri],
                                                     (rx_sems[ri], rxl), i,
                                                     dst_fn(t), slab[:], slab_free, rms_col=None)
                        slab_free = sfree
                        rx_free[ri] = stv
        b.barrier()


def build_nc(chain):
    nc = bass.Bass("TRN2", target_bir_lowering=False, detect_race_conditions=False)
    with ExitStack() as stack:
        mk = MK(nc, stack, chain)
        mk.build()
        block = stack.enter_context(nc.Block())
        mk.b.finalize(block)
    return nc


FULL_CHAIN = [(k, s) for s in range(SEQ_PER_CORE)
              for k in ("attn_a", "ffn", "attn_b", "ffn")]


def make_in_maps(inputs, chain=None):
    g = prep_host(inputs)
    x = np.asarray(inputs["L0_attn_a_x"], np.float32)
    in_maps = []
    for c in range(NCORES):
        m = dict(
            ws=g["ws"], wa=g["wa"], woT=g["woT"], bo=g["bo"], qaT=g["qaT"],
            qalnw=g["qalnw"], qbnT=g["qbnT"], qbpT=g["qbpT"], kvaT=g["kvaT"],
            kvlnw=g["kvlnw"], kvbnT=g["kvbnT"], kvbvT=g["kvbvT"], owT=g["owT"],
            kvapeT=g["kvapeT"],
            gup=g["gup"], dnp=g["dnp"], gusT=g["gusT"], dnsT=g["dnsT"],
            identb=g["identb"], mask=g["mask"], ropeqc=g["ropeqc"], ropeqs=g["ropeqs"],
            ropekc=g["ropekc"], ropeks=g["ropeks"], rotq=g["rotq"], rotk=g["rotk"],
            x_in=np.ascontiguousarray(x[c * SEQ_PER_CORE:(c + 1) * SEQ_PER_CORE]),
        )
        in_maps.append(m)
    return in_maps


def kernel(**inputs):
    chain = FULL_CHAIN
    in_maps = make_in_maps(inputs, chain)
    nc = build_nc(chain)
    res = run_bass_kernel_spmd(nc, in_maps, core_ids=list(range(NCORES)))
    out = np.concatenate([np.asarray(r["y"], np.float32) for r in res.results], axis=0)
    return out.reshape(16, S, D)


# version 15 (final)
# speedup vs baseline: 1.1502x; 1.1429x over previous
import os
import numpy as np
import ml_dtypes
from contextlib import ExitStack

import concourse.bass as bass
import concourse.mybir as mybir
from concourse.bass_utils import run_bass_kernel_spmd

F32 = mybir.dt.float32
BF16 = mybir.dt.bfloat16
I32 = mybir.dt.int32
AF = mybir.ActivationFunctionType
ALU = mybir.AluOpType
AX = mybir.AxisListType

NCORES = 8
SEQ_PER_CORE = 2
S = 2048
D = 2048
NT = S // 128
I_FFN = 5632
GU_OC = 88
DN_OC = 16
SC_B = float(192 ** -0.5)
EPS_LN = 1e-5
EPS_B = 1e-6
MASK_NEG = -30000.0

ENGINES = ("sync", "tensor", "vector", "scalar", "gpsimd")



def _repack_nibbles(packed: np.ndarray, wpr: int) -> np.ndarray:
    u = packed.view(np.uint32)
    o, w = u.shape
    assert w == wpr
    shifts = (4 * np.arange(8, dtype=np.uint32))[None, None, :]
    nib = ((u[:, :, None] >> shifts) & np.uint32(15)).astype(np.uint8)
    unp = nib.reshape(o, w * 8)
    words = np.zeros((o, wpr), np.uint32)
    for pp in range(4):
        blk = unp[:, pp * 2 * wpr:(pp + 1) * 2 * wpr]
        lo = blk[:, 0::2].astype(np.uint32)
        hi = blk[:, 1::2].astype(np.uint32)
        words |= lo << np.uint32(4 * pp)
        words |= hi << np.uint32(4 * pp + 16)
    return words.view(np.int32)


def _rope_tables():
    inv = (1.0 / (10000.0 ** (np.arange(0, 64, 2, dtype=np.float32) / 64.0))).astype(np.float32)
    t = np.arange(S, dtype=np.float32)
    fr = np.outer(t, inv).astype(np.float32)
    emb = np.concatenate([fr, fr], axis=-1)
    cos = np.cos(emb).astype(np.float32)
    sin = np.sin(emb).astype(np.float32)
    cosT = np.ascontiguousarray(cos.T)
    sinT = np.ascontiguousarray(sin.T)
    qcos = np.concatenate([cosT, cosT], axis=0)
    qsin = np.concatenate([sinT, sinT], axis=0)
    return (qcos.astype(ml_dtypes.bfloat16), qsin.astype(ml_dtypes.bfloat16),
            cosT.astype(ml_dtypes.bfloat16), sinT.astype(ml_dtypes.bfloat16))


def _rot_mat():
    R = np.zeros((64, 64), np.float32)
    for i in range(32):
        R[i, i + 32] = -1.0
        R[i + 32, i] = 1.0
    RT = R.T
    blk = np.zeros((128, 128), np.float32)
    blk[:64, :64] = RT
    blk[64:, 64:] = RT
    return (np.ascontiguousarray(blk).astype(ml_dtypes.bfloat16),
            np.ascontiguousarray(RT).astype(ml_dtypes.bfloat16))


def _tile_w(w, W=128):
    K, M = w.shape
    kc = K // 128
    oc = M // W
    t = w.reshape(kc, 128, oc, W).transpose(2, 1, 0, 3).reshape(oc, 128, kc * W)
    return np.ascontiguousarray(t, dtype=np.float32)


def prep_host(inputs):
    g = {}
    f32 = np.float32
    asc = np.ascontiguousarray
    g["ws"] = _tile_w(np.asarray(inputs["L0_attn_a_Ws"], dtype=f32))
    g["wa"] = _tile_w(np.asarray(inputs["L0_attn_a_Wa"], dtype=f32))
    g["woT"] = _tile_w(np.asarray(inputs["L0_attn_a_Wo"], dtype=f32).T)
    g["bo"] = asc(inputs["L0_attn_a_bo"].reshape(16, 128).T, dtype=f32)
    g["qaT"] = _tile_w(np.asarray(inputs["L0_attn_b_q_a_w"], dtype=f32).T)
    g["qalnw"] = asc(inputs["L0_attn_b_q_a_ln_w"].reshape(12, 128).T, dtype=f32)
    qb = np.asarray(inputs["L0_attn_b_q_b_w"], dtype=f32)
    qb3 = qb.reshape(16, 192, 1536)
    qbn = qb3[:, :128, :].reshape(16 * 128, 1536)
    qbp = qb3[:, 128:, :]
    qbp_packed = qbp.reshape(8, 2 * 64, 1536).reshape(16 * 64, 1536)
    g["qbnT"] = _tile_w(asc(qbn.T, dtype=f32))
    g["qbpT"] = _tile_w(asc(qbp_packed.T, dtype=f32))
    kvaT = asc(np.asarray(inputs["L0_attn_b_kv_a_w"], dtype=f32).T)
    g["kvaT"] = _tile_w(kvaT[:, 0:512])
    g["kvapeT"] = _tile_w(kvaT[:, 512:576], W=64)
    g["kvlnw"] = asc(inputs["L0_attn_b_kv_a_ln_w"].reshape(4, 128).T, dtype=f32)
    kvb = np.asarray(inputs["L0_attn_b_kv_b_w"], dtype=f32)
    kvb3 = kvb.reshape(16, 256, 512)
    g["kvbnT"] = _tile_w(asc(kvb3[:, :128, :].reshape(16 * 128, 512).T, dtype=f32))
    g["kvbvT"] = _tile_w(asc(kvb3[:, 128:, :].reshape(16 * 128, 512).T, dtype=f32))
    g["owT"] = _tile_w(np.asarray(inputs["L0_attn_b_o_w"], dtype=f32).T)
    g["gup"] = _repack_nibbles(np.asarray(inputs["L0_ffn_gate_up_packed"]), 256)
    g["dnp"] = _repack_nibbles(np.asarray(inputs["L0_ffn_down_packed"]), 704)
    g["gusT"] = asc(np.asarray(inputs["L0_ffn_gate_up_scales"]).T, dtype=f32)
    g["dnsT"] = asc(np.asarray(inputs["L0_ffn_down_scales"]).T, dtype=f32)
    g["identb"] = np.eye(128, dtype=ml_dtypes.bfloat16)
    m = np.zeros((128, 128), np.float32)
    iu = np.triu_indices(128, 1)
    m[iu] = MASK_NEG
    g["mask"] = m
    qcos, qsin, kcos, ksin = _rope_tables()
    g["ropeqc"], g["ropeqs"], g["ropekc"], g["ropeks"] = qcos, qsin, kcos, ksin
    g["rotq"], g["rotk"] = _rot_mat()
    return g



class Sem:
    __slots__ = ("h", "n", "name")

    def __init__(self, h, name):
        self.h = h
        self.n = 0
        self.name = name


class _Op:
    __slots__ = ("waits", "fn", "incs")

    def __init__(self, waits, fn, incs):
        self.waits = waits
        self.fn = fn
        self.incs = incs


class Builder:
    def __init__(self, nc, stack):
        self.nc = nc
        self.stack = stack
        self.prog = {e: [] for e in ENGINES}
        self.last_wait = {e: {} for e in ENGINES}
        self.last_real = {e: None for e in ENGINES}
        self.last_inc = {e: None for e in ENGINES}
        self.s_pe = self.sem("s_pe")
        self.s_dve = self.sem("s_dve")
        self.s_act = self.sem("s_act")
        self.s_pool = self.sem("s_pool")
        self.s_dsync = self.sem("s_dsync")
        self.s_dgp = self.sem("s_dgp")
        self.s_bar = self.sem("s_bar")
        self.dma_sem = {"sync": self.s_dsync, "gpsimd": self.s_dgp}
        self.eng_sem = {"tensor": self.s_pe, "vector": self.s_dve,
                        "scalar": self.s_act, "gpsimd": self.s_pool}
        self.slot_sems = {}
        self.dma_used = {"sync": set(), "gpsimd": set()}

    def ssem(self, key):
        if key not in self.slot_sems:
            self.slot_sems[key] = self.sem("sl_" + key)
        return self.slot_sems[key]

    def sem(self, name):
        return Sem(self.stack.enter_context(self.nc.semaphore(name)), name)

    _uid = 0

    def _uniq(self, name):
        Builder._uid += 1
        return f"{name}_{Builder._uid}"

    def sb(self, name, shape, dtype, st=None):
        return (st or self.stack).enter_context(
            self.nc.sbuf_tensor(self._uniq(name), shape, dtype))

    def ps(self, name, shape, dtype=F32, st=None):
        return (st or self.stack).enter_context(
            self.nc.psum_tensor(self._uniq(name), shape, dtype))

    def op(self, eng, fn, waits=(), inc=None, k=1, is_dma=False):
        lw = self.last_wait[eng]
        wl = []
        for s, v in waits:
            if v > 0 and lw.get(s, 0) < v:
                lw[s] = v
                wl.append((s.h, v))
        incs = []
        val = None
        if inc is not None:
            kk = 16 if is_dma else k
            inc.n += kk
            val = inc.n
            incs.append((inc, kk))
        o = _Op(wl, fn, incs)
        self.prog[eng].append(o)
        self.last_real[eng] = o
        if inc is not None:
            self.last_inc[eng] = (inc, val)
        return val

    def dma(self, q, out, in_, waits=(), sem=None):
        if sem is None:
            sem = self.dma_sem[q]
        self.dma_used[q].add(sem)
        return self.op(q, (lambda e, out=out, in_=in_: e.dma_start(out=out, in_=in_)),
                       waits=waits, inc=sem, is_dma=True)

    def wait(self, eng, waits):
        lw = self.last_wait[eng]
        wl = []
        for s, v in waits:
            if v > 0 and lw.get(s, 0) < v:
                lw[s] = v
                wl.append((s.h, v))
        if wl:
            self.prog[eng].append(_Op(wl, None, []))

    def barrier(self):
        for eng in ("sync", "gpsimd"):
            for s in sorted(self.dma_used[eng], key=lambda s: s.name):
                self.wait(eng, [(s, s.n)])
            self.dma_used[eng] = set()
        for eng in ENGINES:
            o = self.last_real[eng]
            if o is None:
                continue
            li = self.last_inc[eng]
            if o.incs or (li is None):
                if li is not None:
                    self.wait(eng, [li])
                self.s_bar.n += 1
                self.prog[eng].append(_Op([], (lambda e: e.nop()), [(self.s_bar, 1)]))
            else:
                if li is not None:
                    self.wait(eng, [li])
                o.incs.append((self.s_bar, 1))
                self.s_bar.n += 1
            self.last_real[eng] = None
        tgt = self.s_bar.n
        for eng in ENGINES:
            self.wait(eng, [(self.s_bar, tgt)])

    def finalize(self, block):
        def runner(eng_name):
            def run(e):
                for o in self.prog[eng_name]:
                    for sh, v in o.waits:
                        e.wait_ge(sh, v)
                    if o.fn is not None:
                        ins = o.fn(e)
                        for s, kk in o.incs:
                            ins.then_inc(s.h, kk)
            return run
        block.sync(runner("sync"))
        block.tensor(runner("tensor"))
        block.vector(runner("vector"))
        block.scalar(runner("scalar"))
        block.gpsimd(runner("gpsimd"))



class MK:
    def __init__(self, nc, stack, chain):
        self.nc = nc
        self.st = stack
        self.chain = chain
        b = self.b = Builder(nc, stack)
        dt = nc.dram_tensor
        ap = lambda *a, **k: dt(*a, **k).ap()
        self.x_in = ap("x_in", [SEQ_PER_CORE, S, D], F32, kind="ExternalInput")
        self.ws = ap("ws", [2, 128, 16 * 128], F32, kind="ExternalInput")
        self.wa = ap("wa", [2, 128, 16 * 128], F32, kind="ExternalInput")
        self.woT = ap("woT", [16, 128, 16 * 128], F32, kind="ExternalInput")
        self.bo = ap("bo", [128, 16], F32, kind="ExternalInput")
        self.qaT = ap("qaT", [12, 128, 16 * 128], F32, kind="ExternalInput")
        self.qalnw = ap("qalnw", [128, 12], F32, kind="ExternalInput")
        self.qbnT = ap("qbnT", [16, 128, 12 * 128], F32, kind="ExternalInput")
        self.qbpT = ap("qbpT", [8, 128, 12 * 128], F32, kind="ExternalInput")
        self.kvaT = ap("kvaT", [4, 128, 16 * 128], F32, kind="ExternalInput")
        self.kvapeT = ap("kvapeT", [1, 128, 16 * 64], F32, kind="ExternalInput")
        self.kvlnw = ap("kvlnw", [128, 4], F32, kind="ExternalInput")
        self.kvbnT = ap("kvbnT", [16, 128, 4 * 128], F32, kind="ExternalInput")
        self.kvbvT = ap("kvbvT", [16, 128, 4 * 128], F32, kind="ExternalInput")
        self.owT = ap("owT", [16, 128, 16 * 128], F32, kind="ExternalInput")
        self.wsrc = {"ws": self.ws, "wa": self.wa, "wo": self.woT, "qa": self.qaT,
                     "qbn": self.qbnT, "qbp": self.qbpT, "kva": self.kvaT,
                     "kvape": self.kvapeT, "kvbn": self.kvbnT, "kvbv": self.kvbvT,
                     "ow": self.owT}
        self.gup = ap("gup", [2 * I_FFN, 256], I32, kind="ExternalInput")
        self.dnp = ap("dnp", [D, 704], I32, kind="ExternalInput")
        self.gusT = ap("gusT", [2 * I_FFN, 16], F32, kind="ExternalInput")
        self.dnsT = ap("dnsT", [D, 44], F32, kind="ExternalInput")
        self.identb = ap("identb", [128, 128], BF16, kind="ExternalInput")
        self.mask = ap("mask", [128, 128], F32, kind="ExternalInput")
        self.ropeqc = ap("ropeqc", [128, S], BF16, kind="ExternalInput")
        self.ropeqs = ap("ropeqs", [128, S], BF16, kind="ExternalInput")
        self.ropekc = ap("ropekc", [64, S], BF16, kind="ExternalInput")
        self.ropeks = ap("ropeks", [64, S], BF16, kind="ExternalInput")
        self.rotq = ap("rotq", [128, 128], BF16, kind="ExternalInput")
        self.rotk = ap("rotk", [64, 64], BF16, kind="ExternalInput")
        self.y = ap("y", [SEQ_PER_CORE, S, D], F32, kind="ExternalOutput")
        self.resid = ap("resid", [SEQ_PER_CORE, S, D], F32, kind="Internal")
        self.wgu_dec = ap("wgu_dec", [GU_OC, 128, 16 * 128], BF16, kind="Internal")
        self.wdn_dec = ap("wdn_dec", [DN_OC, 128, 44 * 128], BF16, kind="Internal")
        self.av_spill = ap("av_spill", [16, 128, S], BF16, kind="Internal")
        self.qpe_spill = ap("qpe_spill", [16, 64, S], BF16, kind="Internal")
        self.t_identb = b.sb("g_identb", [128, 128], BF16)
        self.t_mask = b.sb("g_mask", [128, 128], F32)
        self.t_eps5 = b.sb("g_eps5", [128, 1], F32)
        self.t_eps6 = b.sb("g_eps6", [128, 1], F32)
        self.t_ones = b.sb("g_ones", [1, 128], F32)
        self.t_scb = b.sb("g_scb", [1, 128], F32)
        self.t_trash = b.sb("g_trash", [128, 2048], BF16)
        self.t_rs = b.sb("g_rs", [128, 16], F32)
        self.t_ssq = b.sb("g_ssq", [128, 16], F32)
        self.t_mu = b.sb("g_mu", [128, 16], F32)
        self.t_var = b.sb("g_var", [128, 16], F32)
        self.t_std = b.sb("g_std", [128, 16], F32)
        self.t_rstd = b.sb("g_rstd", [128, 16], F32)
        self.t_nmr = b.sb("g_nmr", [128, 16], F32)
        self.t_onesb = b.sb("g_onesb", [128, 1], BF16)
        self.t_one1 = b.sb("g_one1", [128, 1], F32)
        self.wc_done = {}
        self.wc = {}

    def setup(self):
        b = self.b
        b.dma("sync", self.t_identb[:], self.identb[:])
        b.dma("sync", self.t_mask[:], self.mask[:])
        b.op("vector", lambda e: e.memset(self.t_eps5[:], EPS_LN), inc=b.s_dve)
        b.op("vector", lambda e: e.memset(self.t_eps6[:], EPS_B), inc=b.s_dve)
        b.op("vector", lambda e: e.memset(self.t_ones[:], 1.0), inc=b.s_dve)
        b.op("vector", lambda e: e.memset(self.t_scb[:], SC_B), inc=b.s_dve)
        b.op("vector", lambda e: e.memset(self.t_onesb[:], 1.0), inc=b.s_dve)
        b.op("vector", lambda e: e.memset(self.t_one1[:], 1.0), inc=b.s_dve)
        b.barrier()

    WSRC = None

    def load_w(self, name, oc, dst, sem, waits=()):
        srcap = self.wsrc[name]
        w = srcap.shape[2]
        v = self.b.dma("gpsimd", dst[:, 0:w], srcap[oc, :, :], waits=waits, sem=sem)
        return (sem, v)

    def decode_ffn(self):
        b, nc = self.b, self.nc
        from concourse.ap import AP
        with ExitStack() as st:
            words = [b.sb(f"dc_w{i}", [128, 704], I32, st) for i in range(2)]
            stile = [b.sb(f"dc_s{i}", [128, 44], F32, st) for i in range(2)]
            stil2 = [b.sb(f"dc_s2{i}", [128, 44], F32, st) for i in range(2)]
            t_m6 = [b.sb(f"dc_m6{i}", [128, 704], I32, st) for i in range(2)]
            t_s1 = [b.sb(f"dc_sa{i}", [128, 704], I32, st) for i in range(2)]
            t_bits = [b.sb(f"dc_bt{i}", [128, 704], I32, st) for i in range(2)]
            t_cv = [b.sb(f"dc_cv{i}", [128, 1408], F32, st) for i in range(2)]
            t_cv2 = [b.sb(f"dc_c2{i}", [128, 1408], F32, st) for i in range(2)]
            t_i15 = b.sb("dc_i15", [128, 1], I32, st)
            b.op("vector", lambda e: e.memset(t_i15[:], 15), inc=b.s_dve)
            dsc = [b.sb(f"dc_dsc{i}", [128, I_FFN], BF16, st) for i in range(2)]
            stage = [b.sb(f"dc_st{i}", [128, 44 * 128], BF16, st) for i in range(2)]
            pst = [b.ps(f"dc_ps{i}", [128, 1024], BF16, st) for i in range(2)]

            jobs = [(self.gup, self.gusT, self.wgu_dec, GU_OC, 256, 16),
                    (self.dnp, self.dnsT, self.wdn_dec, DN_OC, 704, 44)]
            slot_free = {0: None, 1: None}
            stage_free = {0: None, 1: None}
            pst_free = {0: None, 1: None}
            pair_free = [{"m6": 0, "s": 0, "bits": 0, "cv": 0, "cv2": 0, "srep": 0}
                         for _ in range(2)]
            dsc_free = {0: None, 1: None}
            it = 0
            gpp = 0
            for packed, scales, outd, n_ot, wpr, nkc in jobs:
                ng2 = 2 * wpr // 128
                for ot in range(n_ot):
                    sl = it % 2
                    dscb = dsc[sl]
                    s_w = b.ssem(f"dcw{sl}")
                    w_waits = [slot_free[sl]] if slot_free[sl] else []
                    b.dma("sync", words[sl][:, 0:wpr], packed[ot * 128:(ot + 1) * 128, :],
                          waits=w_waits, sem=s_w)
                    vw = b.dma("sync", stile[sl][:, 0:nkc], scales[ot * 128:(ot + 1) * 128, :],
                               waits=[], sem=s_w)
                    pmv = b.op("vector", lambda e, sl=sl, nkc=nkc: e.tensor_scalar(
                        stil2[sl][:, 0:nkc], stile[sl][:, 0:nkc], float(2.0 ** 63), None,
                        ALU.mult), waits=[(s_w, vw)], inc=b.s_dve)
                    last_act = None
                    for pp in range(4):
                        bb = gpp % 2
                        gpp += 1
                        pv = pair_free[bb]
                        net = 4 * pp - 6
                        shs = (ALU.logical_shift_right if net > 0 else ALU.logical_shift_left)
                        mg = b.op("vector", lambda e, sl=sl, bb=bb, wpr=wpr, net=net, shs=shs: e.tensor_scalar(
                            t_m6[bb][:, 0:wpr], words[sl][:, 0:wpr], abs(net), 0x01C001C0,
                            shs, ALU.bitwise_and),
                            waits=([(s_w, vw)] if pp == 0 else [])
                                  + ([(b.s_pool, pv["m6"])] if pv["m6"] else []),
                            inc=b.s_dve)
                        b.op("vector", lambda e, sl=sl, bb=bb, wpr=wpr, pp=pp: e.tensor_scalar(
                            t_s1[bb][:, 0:wpr], words[sl][:, 0:wpr], 4 * pp + 3, 0x00010001,
                            ALU.logical_shift_right, ALU.bitwise_and), inc=b.s_dve)
                        orv = b.op("vector", lambda e, bb=bb, wpr=wpr: e.scalar_tensor_tensor(
                            t_bits[bb][:, 0:wpr], t_s1[bb][:, 0:wpr], t_i15[:, 0:1],
                            t_m6[bb][:, 0:wpr], ALU.logical_shift_left, ALU.bitwise_or),
                            inc=b.s_dve)
                        cvv = b.op("vector", lambda e, bb=bb, wpr=wpr: e.tensor_copy(
                            t_cv[bb][:, 0:2 * wpr], t_bits[bb][:, 0:wpr].bitcast(BF16)),
                            waits=[(b.s_dve, orv)], inc=b.s_dve)
                        g0 = pp * ng2
                        sap = stil2[sl][:, g0:g0 + ng2]
                        rep = AP(sap.tensor, sap.offset, [[44, 128], [1, ng2], [0, 128]])
                        m1v = b.op("vector", lambda e, bb=bb, wpr=wpr, ng2=ng2, rep=rep: e.tensor_tensor(
                            t_cv2[bb][:, 0:2 * wpr].rearrange("p (g r) -> p g r", g=ng2),
                            t_cv[bb][:, 0:2 * wpr].rearrange("p (g r) -> p g r", g=ng2),
                            rep, ALU.mult),
                            waits=[(b.s_dve, pmv)]
                                  + ([(b.s_act, pv["cv2"])] if pv["cv2"] else []),
                            inc=b.s_dve)
                        fw = []
                        if pp == 0 and dsc_free[sl] is not None:
                            fw.append(dsc_free[sl])
                        last_act = b.op("scalar", lambda e, bb=bb, wpr=wpr, pp=pp, dscb=dscb: e.activation(
                            dscb[:, pp * 2 * wpr:(pp + 1) * 2 * wpr], t_cv2[bb][:, 0:2 * wpr],
                            AF.Identity, scale=float(2.0 ** 63)),
                            waits=[(b.s_dve, m1v)] + fw, inc=b.s_act)
                        pair_free[bb] = {"m6": 0, "s": 0, "bits": 0, "cv": m1v,
                                         "cv2": last_act, "srep": m1v}
                    slot_free[sl] = (b.s_act, last_act)
                    ssl = it % 2
                    st_waits = [stage_free[ssl]] if stage_free[ssl] else []
                    ev_last = None
                    lastt = None
                    for kg in range(0, nkc, 4):
                        psl = (kg // 4) % 2
                        pe_waits = [(b.s_act, last_act)]
                        if pst_free[psl]:
                            pe_waits.append(pst_free[psl])
                        for kk in range(kg, min(kg + 4, nkc)):
                            lastt = b.op("tensor", lambda e, kk=kk, kg=kg, psl=psl, dscb=dscb: e.transpose(
                                pst[psl][:, (kk - kg) * 128:(kk - kg + 1) * 128],
                                dscb[:, kk * 128:(kk + 1) * 128], self.t_identb[:]),
                                waits=pe_waits, inc=b.s_pe)
                            pe_waits = []
                        nb = min(4, nkc - kg)
                        ev_last = b.op("vector", lambda e, kg=kg, psl=psl, nb=nb, ssl=ssl: e.tensor_copy(
                            stage[ssl][:, kg * 128:(kg + nb) * 128], pst[psl][:, 0:nb * 128]),
                            waits=[(b.s_pe, lastt)] + st_waits, inc=b.s_dve)
                        st_waits = []
                        pst_free[psl] = (b.s_dve, ev_last)
                    s_st = b.ssem(f"dcs{ssl}")
                    dv = b.dma("sync", outd[ot, :, :], stage[ssl][:, 0:nkc * 128],
                               waits=[(b.s_dve, ev_last)], sem=s_st)
                    stage_free[ssl] = (s_st, dv)
                    dsc_free[sl] = (b.s_pe, lastt)
                    it += 1
        b.barrier()

    def ln_pass(self, src_tile_fn, ntiles, xn_sink, rx_keep=None):
        raise NotImplementedError

    def emit_ln(self, src_fn, tiles, rx, rx_sems, rx_free, xn_of, xn_free, eps_tile,
                col0=0):
        b = self.b
        n = len(tiles)
        c0, c1 = col0, col0 + n
        ld = []
        for i, tj in enumerate(tiles):
            wl = [rx_free[i]] if rx_free and rx_free[i] is not None else []
            v = b.dma("sync", rx[i][:], src_fn(tj), waits=wl, sem=rx_sems[i])
            ld.append((rx_sems[i], v))
        for i in range(n):
            b.op("vector", lambda e, i=i, c0=c0: e.reduce_sum(
                self.t_rs[:, c0 + i:c0 + i + 1], rx[i][:], axis=AX.X),
                waits=[ld[i]], inc=b.s_dve)
            b.op("scalar", lambda e, i=i, c0=c0: e.activation(
                self.t_trash[:], rx[i][:], AF.Square, accum_out=self.t_ssq[:, c0 + i:c0 + i + 1]),
                waits=[ld[i]], inc=b.s_act)
        dvev = b.s_dve.n
        actv = b.s_act.n
        b.op("gpsimd", lambda e, c0=c0, c1=c1: e.tensor_scalar(
            self.t_mu[:, c0:c1], self.t_rs[:, c0:c1], 1.0 / D, None, ALU.mult),
            waits=[(b.s_dve, dvev), (b.s_act, actv)], inc=b.s_pool)
        b.op("gpsimd", lambda e, c0=c0, c1=c1: e.tensor_scalar(
            self.t_var[:, c0:c1], self.t_ssq[:, c0:c1], 1.0 / D, None, ALU.mult), inc=b.s_pool)
        b.op("gpsimd", lambda e, c0=c0, c1=c1: e.tensor_mul(
            self.t_std[:, c0:c1], self.t_mu[:, c0:c1], self.t_mu[:, c0:c1]), inc=b.s_pool)
        pv = b.op("gpsimd", lambda e, c0=c0, c1=c1: e.tensor_sub(
            self.t_var[:, c0:c1], self.t_var[:, c0:c1], self.t_std[:, c0:c1]), inc=b.s_pool)
        sv = b.op("scalar", lambda e, c0=c0, c1=c1, eps=eps_tile: e.activation(
            self.t_std[:, c0:c1], self.t_var[:, c0:c1], AF.Sqrt, bias=eps[:], scale=1.0),
            waits=[(b.s_pool, pv)], inc=b.s_act)
        rv = b.op("vector", lambda e, c0=c0, c1=c1: e.reciprocal(
            self.t_rstd[:, c0:c1], self.t_std[:, c0:c1]), waits=[(b.s_act, sv)], inc=b.s_dve)
        b.op("gpsimd", lambda e, c0=c0, c1=c1: e.tensor_mul(
            self.t_nmr[:, c0:c1], self.t_mu[:, c0:c1], self.t_rstd[:, c0:c1]),
            waits=[(b.s_dve, rv)], inc=b.s_pool)
        nv = b.op("gpsimd", lambda e, c0=c0, c1=c1: e.tensor_scalar(
            self.t_nmr[:, c0:c1], self.t_nmr[:, c0:c1], -1.0, None, ALU.mult), inc=b.s_pool)
        xn_vals = []
        for i in range(n):
            wl = [(b.s_pool, nv), (b.s_dve, rv)]
            if xn_free and xn_free[i] is not None:
                wl.append(xn_free[i])
            xn_vals.append(b.op("scalar", lambda e, i=i, c0=c0: e.activation(
                xn_of(i), rx[i][:], AF.Identity, bias=self.t_nmr[:, c0 + i:c0 + i + 1],
                scale=self.t_rstd[:, c0 + i:c0 + i + 1]), waits=wl, inc=b.s_act))
        return ld, xn_vals

    def emit_y_add(self, yT, yT_ready, rx_tile, rx_ld, tt_col, dst_ap, slab, slab_free,
                   rms_col=None):
        b = self.b
        lastt = None
        wl = [yT_ready]
        if slab_free is not None:
            wl.append(slab_free)
        for oc in range(16):
            lastt = b.op("tensor", lambda e, oc=oc: e.transpose(
                slab[:, oc * 128:(oc + 1) * 128],
                yT[:, oc * 512 + tt_col * 128: oc * 512 + (tt_col + 1) * 128],
                self.t_identb[:]), waits=wl, inc=b.s_pe)
            wl = []
        if rms_col is not None:
            i = rms_col
            sqv = b.op("scalar", lambda e, i=i: e.activation(self.t_trash[:], slab[:],
                 AF.Square, accum_out=self.t_ssq[:, i:i + 1]),
                 waits=[(b.s_pe, lastt)], inc=b.s_act)
            sv = b.op("scalar", lambda e, i=i: e.activation(self.t_std[:, i:i + 1],
                 self.t_ssq[:, i:i + 1], AF.Sqrt, bias=self.t_eps6[:], scale=1.0 / D),
                 inc=b.s_act)
            rv = b.op("vector", lambda e, i=i: e.reciprocal(self.t_rstd[:, i:i + 1],
                 self.t_std[:, i:i + 1]), waits=[(b.s_act, sv)], inc=b.s_dve)
            scal = self.t_rstd[:, i:i + 1]
            addw = [(b.s_dve, rv), rx_ld]
        else:
            scal = 1.0
            addw = [(b.s_pe, lastt), rx_ld]
        av = b.op("vector", lambda e, scal=scal: e.scalar_tensor_tensor(
            rx_tile[:], slab[:], scal, rx_tile[:], ALU.mult, ALU.add),
            waits=addw, inc=b.s_dve)
        stv = b.dma("sync", dst_ap, rx_tile[:], waits=[(b.s_dve, av)], sem=rx_ld[0])
        return (rx_ld[0], stv), (b.s_dve, av)

    def ffn(self, seq, src, dst):
        b = self.b
        src_fn = lambda t: src[seq, t * 128:(t + 1) * 128, :]
        dst_fn = lambda t: dst[seq, t * 128:(t + 1) * 128, :]
        with ExitStack() as st:
            rx = [b.sb(f"f_rx{i}", [128, D], F32, st) for i in range(4)]
            xt = [b.sb(f"f_xt{i}", [128, D], BF16, st) for i in range(4)]
            xnT = b.sb("f_xnT", [128, 16 * 512], BF16, st)
            hT = b.sb("f_hT", [128, 44 * 512], BF16, st)
            yT = b.sb("f_yT", [128, 16 * 512], BF16, st)
            sil = [b.sb(f"f_sil{i}", [128, 512], BF16, st) for i in range(2)]
            wg = [b.sb(f"f_wg{i}", [128, 16 * 128], BF16, st) for i in range(2)]
            wu = [b.sb(f"f_wu{i}", [128, 16 * 128], BF16, st) for i in range(2)]
            wd = [b.sb(f"f_wd{i}", [128, 44 * 128], BF16, st) for i in range(2)]
            psm = [b.ps(f"f_ps{i}", [128, 512], F32, st) for i in range(4)]
            psbt = [b.ps(f"f_pt{i}", [128, 1024], BF16, st) for i in range(2)]
            slab = [b.ps("f_sl0", [128, 2048], BF16, st)]

            rx_free = [None] * 4
            xn_free = [None] * 4
            wg_free = [None, None]
            wu_free = [None, None]
            wd_free = [None, None]
            psm_free = [None] * 4
            psbt_free = [None, None]
            slab_free = [None]
            sil_free = [None, None]

            rx_sems = [b.ssem(f"frx{i}") for i in range(4)]
            for tb in range(4):
                tiles = [tb * 4 + i for i in range(4)]
                ld, xnv = self.emit_ln(src_fn, tiles, rx, rx_sems, rx_free,
                                       lambda i: xt[i][:], xn_free, self.t_eps5)
                xnT_last = None
                for kc in range(16):
                    psl = kc % 2
                    for i in range(4):
                        wl = [(b.s_act, xnv[i])]
                        if i == 0 and psbt_free[psl] is not None:
                            wl.append(psbt_free[psl])
                        lastt = b.op("tensor", lambda e, kc=kc, i=i, psl=psl: e.transpose(
                            psbt[psl][:, i * 128:(i + 1) * 128],
                            xt[i][:, kc * 128:(kc + 1) * 128], self.t_identb[:]),
                            waits=wl, inc=b.s_pe)
                    xnT_last = b.op("vector", lambda e, kc=kc, psl=psl: e.tensor_copy(
                        xnT[:, kc * 512:(kc + 1) * 512], psbt[psl][:, 0:512]),
                        waits=[(b.s_pe, lastt)], inc=b.s_dve)
                    psbt_free[psl] = (b.s_dve, xnT_last)
                for i in range(4):
                    xn_free[i] = (b.s_dve, xnT_last)

                for pc in range(44):
                    sl = pc % 2
                    s_g = b.ssem(f"fwg{sl}")
                    s_u = b.ssem(f"fwu{sl}")
                    wl = [wg_free[sl]] if wg_free[sl] is not None else []
                    gv = b.dma("sync", wg[sl][:], self.wgu_dec[pc, :, :],
                               waits=wl, sem=s_g)
                    wl = [wu_free[sl]] if wu_free[sl] is not None else []
                    uv = b.dma("sync", wu[sl][:], self.wgu_dec[pc + 44, :, :],
                               waits=wl, sem=s_u)
                    pa = (2 * pc) % 4
                    pb = (2 * pc + 1) % 4
                    wlA = [(s_g, gv), (b.s_dve, xnT_last)]
                    if psm_free[pa] is not None:
                        wlA.append(psm_free[pa])
                    lastA = None
                    for kc in range(16):
                        lastA = b.op("tensor", lambda e, kc=kc, sl=sl, pa=pa: e.matmul(
                            psm[pa][:], wg[sl][:, kc * 128:(kc + 1) * 128],
                            xnT[:, kc * 512:(kc + 1) * 512], start=(kc == 0), stop=(kc == 15)),
                            waits=wlA, inc=(b.s_pe if kc == 15 else None))
                        wlA = []
                    wg_free[sl] = (b.s_pe, lastA)
                    wlB = [(s_u, uv)]
                    if psm_free[pb] is not None:
                        wlB.append(psm_free[pb])
                    lastB = None
                    for kc in range(16):
                        lastB = b.op("tensor", lambda e, kc=kc, sl=sl, pb=pb: e.matmul(
                            psm[pb][:], wu[sl][:, kc * 128:(kc + 1) * 128],
                            xnT[:, kc * 512:(kc + 1) * 512], start=(kc == 0), stop=(kc == 15)),
                            waits=wlB, inc=(b.s_pe if kc == 15 else None))
                        wlB = []
                    wu_free[sl] = (b.s_pe, lastB)
                    ssl = pc % 2
                    wl = [(b.s_pe, lastA)]
                    if sil_free[ssl] is not None:
                        wl.append(sil_free[ssl])
                    sv = b.op("scalar", lambda e, ssl=ssl, pa=pa: e.activation(
                        sil[ssl][:], psm[pa][:], AF.Silu), waits=wl, inc=b.s_act)
                    psm_free[pa] = (b.s_act, sv)
                    hv = b.op("vector", lambda e, pc=pc, pb=pb, ssl=ssl: e.scalar_tensor_tensor(
                        hT[:, pc * 512:(pc + 1) * 512], psm[pb][:], 1.0, sil[ssl][:],
                        ALU.mult, ALU.mult), waits=[(b.s_pe, lastB), (b.s_act, sv)], inc=b.s_dve)
                    psm_free[pb] = (b.s_dve, hv)
                    sil_free[ssl] = (b.s_dve, hv)
                h_done = b.s_dve.n

                for oc in range(16):
                    sl = oc % 2
                    s_d = b.ssem(f"fwd{sl}")
                    wl = [wd_free[sl]] if wd_free[sl] is not None else []
                    dv = b.dma("sync", wd[sl][:], self.wdn_dec[oc, :, :],
                               waits=wl, sem=s_d)
                    pp = oc % 4
                    wl = [(s_d, dv), (b.s_dve, h_done)]
                    if psm_free[pp] is not None:
                        wl.append(psm_free[pp])
                    last = None
                    for kc in range(44):
                        last = b.op("tensor", lambda e, kc=kc, sl=sl, pp=pp: e.matmul(
                            psm[pp][:], wd[sl][:, kc * 128:(kc + 1) * 128],
                            hT[:, kc * 512:(kc + 1) * 512], start=(kc == 0), stop=(kc == 43)),
                            waits=wl, inc=(b.s_pe if kc == 43 else None))
                        wl = []
                    wd_free[sl] = (b.s_pe, last)
                    ev = b.op("vector", lambda e, oc=oc, pp=pp: e.tensor_copy(
                        yT[:, oc * 512:(oc + 1) * 512], psm[pp][:]),
                        waits=[(b.s_pe, last)], inc=b.s_dve)
                    psm_free[pp] = (b.s_dve, ev)
                yT_done = b.s_dve.n

                for i in range(4):
                    stv, sfree = self.emit_y_add(yT, (b.s_dve, yT_done), rx[i], ld[i], i,
                                                 dst_fn(tiles[i]), slab[0][:],
                                                 slab_free[0], rms_col=8 + i)
                    slab_free[0] = sfree
                    rx_free[i] = stv
        b.barrier()

    def build(self):
        self.setup()
        need_ffn = any(p[0] == "ffn" for p in self.chain)
        if need_ffn:
            self.decode_ffn()
        for idx, ph in enumerate(self.chain):
            kind, seq = ph[0], ph[1]
            first = (idx == 0) or all(p[1] != seq for p in self.chain[:idx])
            last = (idx == len(self.chain) - 1) or all(p[1] != seq for p in self.chain[idx + 1:])
            src = self.x_in if first else self.resid
            dst = self.y if last else self.resid
            if kind == "ffn":
                self.ffn(seq, src, dst)
            elif kind == "attn_a":
                self.attn_a(seq, src, dst)
            elif kind == "attn_b":
                self.attn_b(seq, src, dst)
            else:
                raise ValueError(kind)
        self.b.barrier()

    def attn_a(self, seq, src, dst):
        b = self.b
        src_fn = lambda t: src[seq, t * 128:(t + 1) * 128, :]
        dst_fn = lambda t: dst[seq, t * 128:(t + 1) * 128, :]
        with ExitStack() as st:
            xn = b.sb("a_xn", [128, NT * D], BF16, st)
            rx = [b.sb(f"a_rx{i}", [128, D], F32, st) for i in range(2)]
            bo_sb = b.sb("a_bo", [128, 16], F32, st)
            stT = b.sb("a_st", [128, 2 * S], BF16, st)
            acT = b.sb("a_ac", [128, 2 * S], BF16, st)
            cbias = b.sb("a_cb", [128, 1], F32, st)
            s_aw = b.ssem("aw")
            s_awb = b.ssem("awb")
            bo_v = b.dma("sync", bo_sb[:], self.bo[:], sem=s_awb)
            bo_ld = (s_awb, bo_v)
            with ExitStack() as st1:
                wsb = b.sb("a_ws", [128, 2 * 2048], BF16, st1)
                wab = b.sb("a_wa", [128, 2 * 2048], BF16, st1)
                for oc in range(2):
                    self.load_w("ws", oc, wsb[:, oc * 2048:(oc + 1) * 2048], s_aw)
                    self.load_w("wa", oc, wab[:, oc * 2048:(oc + 1) * 2048], s_aw)
                aw_ld = (s_aw, s_aw.n)
                xnT = b.sb("a_xnT", [128, 16 * 512], BF16, st1)
                psbt = [b.ps(f"a_pt{i}", [128, 1024], BF16, st1) for i in range(2)]
                psv = [b.ps(f"a_pv{i}", [128, 512], F32, st1) for i in range(2)]
                rx_free = [None, None]
                psbt_free = [None, None]
                psv_free = [None, None]
                rx_sems = [b.ssem("arx0"), b.ssem("arx1")]
                for tb in range(4):
                    xn_vals = []
                    for gg in range(2):
                        tiles = [tb * 4 + gg * 2, tb * 4 + gg * 2 + 1]
                        ld, xv = self.emit_ln(src_fn, tiles, rx, rx_sems, rx_free,
                                              lambda i, tiles=tiles: xn[:, tiles[i] * D:(tiles[i] + 1) * D],
                                              None, self.t_eps5, col0=gg * 2)
                        xn_vals += xv
                        rx_free = [(b.s_act, xv[0]), (b.s_act, xv[1])]
                    xnT_last = None
                    for kc in range(16):
                        psl = kc % 2
                        for i in range(4):
                            t = tb * 4 + i
                            wl = [(b.s_act, xn_vals[i])]
                            if i == 0 and psbt_free[psl] is not None:
                                wl.append(psbt_free[psl])
                            lastt = b.op("tensor", lambda e, t=t, kc=kc, i=i, psl=psl: e.transpose(
                                psbt[psl][:, i * 128:(i + 1) * 128],
                                xn[:, t * D + kc * 128: t * D + (kc + 1) * 128],
                                self.t_identb[:]), waits=wl, inc=b.s_pe)
                        xnT_last = b.op("vector", lambda e, kc=kc, psl=psl: e.tensor_copy(
                            xnT[:, kc * 512:(kc + 1) * 512], psbt[psl][:, 0:512]),
                            waits=[(b.s_pe, lastt)], inc=b.s_dve)
                        psbt_free[psl] = (b.s_dve, xnT_last)
                    for wbuf, dT, sc in ((wsb, stT, 1.0 / 16), (wab, acT, None)):
                        for m in range(2):
                            pi = m % 2
                            wl = [(b.s_dve, xnT_last), aw_ld]
                            if psv_free[pi] is not None:
                                wl.append(psv_free[pi])
                            last = None
                            for kc in range(16):
                                last = b.op("tensor", lambda e, kc=kc, m=m, pi=pi, wbuf=wbuf: e.matmul(
                                    psv[pi][:], wbuf[:, m * 2048 + kc * 128: m * 2048 + (kc + 1) * 128],
                                    xnT[:, kc * 512:(kc + 1) * 512], start=(kc == 0), stop=(kc == 15)),
                                    waits=wl, inc=(b.s_pe if kc == 15 else None))
                                wl = []
                            if sc is None:
                                ev = b.op("vector", lambda e, m=m, tb=tb, pi=pi, dT=dT: e.tensor_copy(
                                    dT[:, m * S + tb * 512: m * S + (tb + 1) * 512], psv[pi][:]),
                                    waits=[(b.s_pe, last)], inc=b.s_dve)
                            else:
                                ev = b.op("vector", lambda e, m=m, tb=tb, pi=pi, dT=dT, sc=sc: e.tensor_scalar(
                                    dT[:, m * S + tb * 512: m * S + (tb + 1) * 512], psv[pi][:],
                                    sc, None, ALU.mult), waits=[(b.s_pe, last)], inc=b.s_dve)
                            psv_free[pi] = (b.s_dve, ev)
                b.barrier()
            with ExitStack() as st2:
                sqb = [b.sb(f"a_sq{i}", [128, D], BF16, st2) for i in range(2)]
                nmx = b.sb("a_nmx", [1, 16], F32, st2)
                ctmp = b.sb("a_ctmp", [1, 4], F32, st2)
                psn = b.ps("a_psn", [1, 512], F32, st2)
                psc = b.ps("a_psc", [128, 1], F32, st2)
                for sidx, TT in enumerate((stT, acT)):
                    sq_vals = []
                    sq_w = [(b.s_pe, b.s_pe.n)] if sidx > 0 else []
                    for c in range(2):
                        sq_vals.append(b.op("scalar", lambda e, c=c, sidx=sidx, TT=TT: e.activation(
                            sqb[c][:], TT[:, c * S:(c + 1) * S], AF.Square), waits=sq_w, inc=b.s_act))
                        sq_w = []
                    psn_free = None
                    for blk in range(4):
                        wl = [(b.s_act, sq_vals[1])]
                        if psn_free is not None:
                            wl.append(psn_free)
                        last = None
                        for c in range(2):
                            last = b.op("tensor", lambda e, c=c, blk=blk: e.matmul(
                                psn[0:1, :], self.t_onesb[:], sqb[c][:, blk * 512:(blk + 1) * 512],
                                start=(c == 0), stop=(c == 1)), waits=wl, inc=(b.s_pe if c == 1 else None))
                            wl = []
                        rv = b.op("vector", lambda e, sidx=sidx, blk=blk: e.reduce_max(
                            nmx[0:1, sidx * 4 + blk: sidx * 4 + blk + 1], psn[0:1, :], axis=AX.X),
                            waits=[(b.s_pe, last)], inc=b.s_dve)
                        psn_free = (b.s_dve, rv)
                for sidx in range(2):
                    b.op("vector", lambda e, sidx=sidx: e.reduce_max(
                        ctmp[0:1, sidx:sidx + 1], nmx[0:1, sidx * 4:sidx * 4 + 4], axis=AX.X),
                        inc=b.s_dve)
                dv = b.s_dve.n
                pv = b.op("gpsimd", lambda e: e.tensor_mul(ctmp[0:1, 2:3], ctmp[0:1, 0:1],
                     ctmp[0:1, 1:2]), waits=[(b.s_dve, dv)], inc=b.s_pool)
                sv = b.op("scalar", lambda e: e.activation(ctmp[0:1, 3:4], ctmp[0:1, 2:3],
                     AF.Sqrt), waits=[(b.s_pool, pv)], inc=b.s_act)
                nv = b.op("gpsimd", lambda e: e.tensor_scalar(ctmp[0:1, 2:3], ctmp[0:1, 3:4],
                     -1.0, None, ALU.mult), waits=[(b.s_act, sv)], inc=b.s_pool)
                mv = b.op("tensor", lambda e: e.matmul(psc[:, 0:1], self.t_ones[:],
                     ctmp[0:1, 2:3], start=True, stop=True), waits=[(b.s_pool, nv)], inc=b.s_pe)
                b.op("vector", lambda e: e.tensor_copy(cbias[:], psc[:, 0:1]),
                     waits=[(b.s_pe, mv)], inc=b.s_dve)
                b.barrier()
            with ExitStack() as st3:
                p4 = [b.sb(f"a_p4{i}", [128, 4 * S], BF16, st3) for i in range(2)]
                PT = b.sb("a_PT", [128, 16 * 512], BF16, st3)
                avT = b.sb("a_av", [128, 16 * 512], BF16, st3)
                yT = b.sb("a_yT", [128, 16 * 512], BF16, st3)
                wob = [b.sb(f"a_wo{i}", [128, 16 * 128], BF16, st3) for i in range(2)]
                asum = b.sb("a_as", [128, 16], F32, st3)
                atot = b.sb("a_at", [128, 4], F32, st3)
                arcp = b.sb("a_ar", [128, 4], F32, st3)
                pss = [b.ps(f"a_ss{i}", [128, 512], F32, st3) for i in range(2)]
                pst2 = [b.ps(f"a_pt2{i}", [128, 1024], BF16, st3) for i in range(2)]
                psa = [b.ps(f"a_pa{i}", [128, 512], F32, st3) for i in range(2)]
                slab = b.ps("a_slab", [128, 2048], BF16, st3)
                pss_free = [None, None]
                pst2_free = [None, None]
                psa_free = [None, None]
                wob_free = [None, None]
                slab_free = None
                rx_free = [None, None]
                sum_done = 0
                pm_of = {}
                ya_of = {}

                def a_stage1(qb):
                    nonlocal sum_done
                    pb = p4[qb % 2][:]
                    ssi = 0
                    exp_last = [0] * 4
                    for qt in range(4):
                        qtg = qb * 4 + qt
                        klen = (qtg + 1) * 128
                        nblk = (klen + 511) // 512
                        for blk in range(nblk):
                            w = min(512, klen - blk * 512)
                            pi = ssi % 2
                            ssi += 1
                            wl = [pss_free[pi]] if pss_free[pi] is not None else []
                            last = None
                            for m in range(2):
                                last = b.op("tensor", lambda e, m=m, qtg=qtg, blk=blk, w=w, pi=pi: e.matmul(
                                    pss[pi][:, 0:w], stT[:, m * S + qtg * 128: m * S + (qtg + 1) * 128],
                                    acT[:, m * S + blk * 512: m * S + blk * 512 + w],
                                    start=(m == 0), stop=(m == 1)), waits=wl,
                                    inc=(b.s_pe if m == 1 else None))
                                wl = []
                            expw = [(b.s_pe, last)]
                            if blk == nblk - 1:
                                mkv = b.op("vector", lambda e, w=w, pi=pi: e.tensor_tensor(
                                    pss[pi][:, w - 128:w], pss[pi][:, w - 128:w], self.t_mask[:],
                                    ALU.add), waits=[(b.s_pe, last)], inc=b.s_dve)
                                expw = [(b.s_dve, mkv)]
                            ev = b.op("scalar", lambda e, qt=qt, blk=blk, w=w, pi=pi, pb=pb: e.activation(
                                pb[:, qt * S + blk * 512: qt * S + blk * 512 + w], pss[pi][:, 0:w],
                                AF.Exp, bias=cbias[:], scale=1.0,
                                accum_out=asum[:, qt * 4 + blk: qt * 4 + blk + 1]),
                                waits=expw + [(b.s_dve, sum_done)], inc=b.s_act)
                            exp_last[qt] = ev
                            pss_free[pi] = (b.s_act, ev)
                    for qt in range(4):
                        qtg = qb * 4 + qt
                        nblk = ((qtg + 1) * 128 + 511) // 512
                        b.op("vector", lambda e, qt=qt, nblk=nblk: e.reduce_sum(
                            atot[:, qt:qt + 1], asum[:, qt * 4: qt * 4 + nblk], axis=AX.X),
                            waits=[(b.s_act, exp_last[qt])], inc=b.s_dve)
                    for qt in range(4):
                        b.op("vector", lambda e, qt=qt: e.reciprocal(
                            arcp[:, qt:qt + 1], atot[:, qt:qt + 1]), inc=b.s_dve)
                    pm_last = None
                    for qt in range(4):
                        qtg = qb * 4 + qt
                        klen = (qtg + 1) * 128
                        pm_last = b.op("vector", lambda e, qt=qt, klen=klen, pb=pb: e.tensor_scalar(
                            pb[:, qt * S: qt * S + klen], pb[:, qt * S: qt * S + klen],
                            arcp[:, qt:qt + 1], None, ALU.mult), inc=b.s_dve)
                    sum_done = pm_last
                    pm_of[qb] = pm_last

                def a_stage2(qb):
                    pb = p4[qb % 2][:]
                    pm_last = pm_of[qb]
                    kmax = (qb + 1) * 4
                    ptev = None
                    for kc in range(kmax):
                        psl = kc % 2
                        wl = [(b.s_dve, pm_last)]
                        if pst2_free[psl] is not None:
                            wl.append(pst2_free[psl])
                        lastt = None
                        for qt in range(4):
                            if qb * 4 + qt < kc:
                                continue
                            lastt = b.op("tensor", lambda e, qt=qt, kc=kc, psl=psl, pb=pb: e.transpose(
                                pst2[psl][:, qt * 128:(qt + 1) * 128],
                                pb[:, qt * S + kc * 128: qt * S + (kc + 1) * 128],
                                self.t_identb[:]), waits=wl, inc=b.s_pe)
                            wl = []
                        q0 = max(0, kc - qb * 4)
                        if q0 > 0:
                            b.op("vector", lambda e, kc=kc, q0=q0: e.memset(
                                PT[:, kc * 512: kc * 512 + q0 * 128], 0.0), inc=b.s_dve)
                        ptev = b.op("vector", lambda e, kc=kc, psl=psl, q0=q0: e.tensor_copy(
                            PT[:, kc * 512 + q0 * 128:(kc + 1) * 512],
                            pst2[psl][:, q0 * 128:512]),
                            waits=[(b.s_pe, lastt)], inc=b.s_dve)
                        pst2_free[psl] = (b.s_dve, ptev)
                    for vc in range(16):
                        pi = vc % 2
                        wl = [(b.s_dve, ptev)]
                        if psa_free[pi] is not None:
                            wl.append(psa_free[pi])
                        last = None
                        for kc in range(kmax):
                            last = b.op("tensor", lambda e, kc=kc, vc=vc, pi=pi,
                                        sp_=(kc == kmax - 1): e.matmul(
                                psa[pi][:], xn[:, kc * D + vc * 128: kc * D + (vc + 1) * 128],
                                PT[:, kc * 512: (kc + 1) * 512],
                                start=(kc == 0), stop=sp_), waits=wl,
                                inc=(b.s_pe if kc == kmax - 1 else None))
                            wl = []
                        av = b.op("vector", lambda e, vc=vc, pi=pi: e.tensor_copy(
                            avT[:, vc * 512:(vc + 1) * 512], psa[pi][:]),
                            waits=[(b.s_pe, last)], inc=b.s_dve)
                        psa_free[pi] = (b.s_dve, av)
                    av_done = b.s_dve.n
                    for oc in range(16):
                        sl = oc % 2
                        wl = [wob_free[sl]] if wob_free[sl] is not None else []
                        wv = self.load_w("wo", oc, wob[sl], b.ssem(f"awo{sl}"), waits=wl)
                        pi = oc % 2
                        wl = [wv, (b.s_dve, av_done)]
                        if psa_free[pi] is not None:
                            wl.append(psa_free[pi])
                        last = None
                        for hc in range(16):
                            last = b.op("tensor", lambda e, hc=hc, sl=sl, pi=pi: e.matmul(
                                psa[pi][:], wob[sl][:, hc * 128:(hc + 1) * 128],
                                avT[:, hc * 512:(hc + 1) * 512], start=(hc == 0), stop=(hc == 15)),
                                waits=wl, inc=(b.s_pe if hc == 15 else None))
                            wl = []
                        wob_free[sl] = (b.s_pe, last)
                        yv = b.op("vector", lambda e, oc=oc, pi=pi: e.tensor_scalar(
                            yT[:, oc * 512:(oc + 1) * 512], psa[pi][:], bo_sb[:, oc:oc + 1],
                            None, ALU.add), waits=[(b.s_pe, last), bo_ld], inc=b.s_dve)
                        psa_free[pi] = (b.s_dve, yv)
                    ya_of[qb] = b.s_dve.n

                def a_stage3(qb):
                    nonlocal slab_free
                    yT_done = ya_of[qb]
                    for i in range(4):
                        t = qb * 4 + i
                        ri = i % 2
                        wl = [rx_free[ri]] if rx_free[ri] is not None else []
                        rxl = b.dma("sync", rx[ri][:], src_fn(t), waits=wl, sem=rx_sems[ri])
                        stv, sfree = self.emit_y_add(yT, (b.s_dve, yT_done), rx[ri],
                                                     (rx_sems[ri], rxl), i,
                                                     dst_fn(t), slab[:], slab_free,
                                                     rms_col=None)
                        slab_free = sfree
                        rx_free[ri] = stv

                import os as _os
                if _os.environ.get("MK_A_SEQ"):
                    for _q in range(4):
                        a_stage1(_q); a_stage2(_q); a_stage3(_q)
                else:
                    a_stage1(0)
                    a_stage1(1)
                    a_stage2(0)
                    a_stage3(0)
                    a_stage1(2)
                    a_stage2(1)
                    a_stage3(1)
                    a_stage1(3)
                    a_stage2(2)
                    a_stage3(2)
                    a_stage2(3)
                    a_stage3(3)
        b.barrier()

    def attn_b(self, seq, src, dst):
        stages = int(os.environ.get("MK_B_STAGES", "4"))
        b = self.b
        src_fn = lambda t: src[seq, t * 128:(t + 1) * 128, :]
        dst_fn = lambda t: dst[seq, t * 128:(t + 1) * 128, :]
        with ExitStack() as st:
            qanT = b.sb("b_qanT", [128, 12 * S], BF16, st)
            ckvnT = b.sb("b_ckvnT", [128, 4 * S], BF16, st)
            kpeT = b.sb("b_kpeT", [64, S], BF16, st)
            with ExitStack() as st1:
                rx = [b.sb(f"b_rx{i}", [128, D], F32, st1) for i in range(2)]
                xt4 = [b.sb(f"b_xt{i}", [128, D], BF16, st1) for i in range(4)]
                xnT = b.sb("b_xnT", [128, 16 * 512], BF16, st1)
                qaw = [b.sb(f"b_qaw{i}", [128, 16 * 128], BF16, st1) for i in range(2)]
                kvaw = b.sb("b_kvaw", [128, 4 * 2048], BF16, st1)
                kvpw = b.sb("b_kvpw", [128, 16 * 64], BF16, st1)
                qalnw_sb = b.sb("b_qalnw", [128, 12], F32, st1)
                kvlnw_sb = b.sb("b_kvlnw", [128, 4], F32, st1)
                sqq = b.sb("b_sqq", [128, 12 * 512], BF16, st1)
                sqk = b.sb("b_sqk", [128, 4 * 512], BF16, st1)
                qaraw = b.sb("b_qaraw", [128, 12 * 512], BF16, st1)
                kvraw = b.sb("b_kvraw", [128, 4 * 512], BF16, st1)
                rrow = b.sb("b_rrow", [1, 512], F32, st1)
                rkc = b.sb("b_rkc", [64, S], BF16, st1)
                rks = b.sb("b_rks", [64, S], BF16, st1)
                rotk_sb = b.sb("b_rotk", [64, 64], BF16, st1)
                t1b = b.sb("b_t1", [128, 512], BF16, st1)
                t2b = b.sb("b_t2", [128, 512], BF16, st1)
                psq = [b.ps(f"b_pq{i}", [128, 512], F32, st1) for i in range(2)]
                psbt = [b.ps(f"b_pt{i}", [128, 1024], BF16, st1) for i in range(2)]
                psr = b.ps("b_psr", [1, 512], F32, st1)
                psrep = b.ps("b_psrep", [128, 512], F32, st1)
                psro = b.ps("b_psro", [64, 512], F32, st1)
                s_kw = b.ssem("bkw")
                s_kt = b.ssem("bkt")
                self.load_w("kva", 0, kvaw[:, 0:2048], s_kw)
                self.load_w("kva", 1, kvaw[:, 2048:4096], s_kw)
                self.load_w("kva", 2, kvaw[:, 4096:6144], s_kw)
                self.load_w("kva", 3, kvaw[:, 6144:8192], s_kw)
                self.load_w("kvape", 0, kvpw, s_kw)
                b.dma("sync", qalnw_sb[:], self.qalnw[:], sem=s_kt)
                b.dma("sync", kvlnw_sb[:], self.kvlnw[:], sem=s_kt)
                b.dma("sync", rkc[:], self.ropekc[:], sem=s_kt)
                b.dma("sync", rks[:], self.ropeks[:], sem=s_kt)
                b.dma("sync", rotk_sb[:], self.rotk[:], sem=s_kt)
                kv_ld = (s_kw, s_kw.n)
                kt_ld = (s_kt, s_kt.n)
                rx_sems = [b.ssem("brx0"), b.ssem("brx1")]
                rx_free = [None, None]
                psbt_free = [None, None]
                psq_free = [None, None]
                qaw_free = [None, None]
                s_qw = [b.ssem("bqw0"), b.ssem("bqw1")]
                for stt_ in range(4):
                    xn_vals = []
                    for gg in range(2):
                        tiles = [stt_ * 4 + gg * 2, stt_ * 4 + gg * 2 + 1]
                        ld, xv = self.emit_ln(src_fn, tiles, rx, rx_sems, rx_free,
                                              lambda i, gg=gg: xt4[gg * 2 + i][:],
                                              None, self.t_eps5, col0=gg * 2)
                        xn_vals += xv
                        rx_free = [(b.s_act, xv[0]), (b.s_act, xv[1])]
                    xnT_last = None
                    for kc in range(16):
                        psl = kc % 2
                        for i in range(4):
                            wl = [(b.s_act, xn_vals[i])]
                            if i == 0 and psbt_free[psl] is not None:
                                wl.append(psbt_free[psl])
                            lastt = b.op("tensor", lambda e, kc=kc, i=i, psl=psl: e.transpose(
                                psbt[psl][:, i * 128:(i + 1) * 128],
                                xt4[i][:, kc * 128:(kc + 1) * 128],
                                self.t_identb[:]), waits=wl, inc=b.s_pe)
                        xnT_last = b.op("vector", lambda e, kc=kc, psl=psl: e.tensor_copy(
                            xnT[:, kc * 512:(kc + 1) * 512], psbt[psl][:, 0:512]),
                            waits=[(b.s_pe, lastt)], inc=b.s_dve)
                        psbt_free[psl] = (b.s_dve, xnT_last)
                    for m in range(12):
                        sl = m % 2
                        wlw = qaw_free[sl] or []
                        wv = self.load_w("qa", m, qaw[sl], s_qw[sl], waits=wlw)
                        pi = m % 2
                        wl = [wv, (b.s_dve, xnT_last)]
                        wl += psq_free[pi] or []
                        last = None
                        for kc in range(16):
                            last = b.op("tensor", lambda e, kc=kc, sl=sl, pi=pi: e.matmul(
                                psq[pi][:], qaw[sl][:, kc * 128:(kc + 1) * 128],
                                xnT[:, kc * 512:(kc + 1) * 512], start=(kc == 0), stop=(kc == 15)),
                                waits=wl, inc=(b.s_pe if kc == 15 else None))
                            wl = []
                        qaw_free[sl] = [(b.s_pe, last)]
                        ev = b.op("vector", lambda e, m=m, pi=pi: e.tensor_copy(
                            qaraw[:, m * 512:(m + 1) * 512], psq[pi][:]),
                            waits=[(b.s_pe, last)], inc=b.s_dve)
                        b.op("scalar", lambda e, m=m: e.activation(
                            sqq[:, m * 512:(m + 1) * 512], qaraw[:, m * 512:(m + 1) * 512],
                            AF.Square), waits=[(b.s_dve, ev)], inc=b.s_act)
                        psq_free[pi] = [(b.s_dve, ev)]
                    for m in range(4):
                        pi = m % 2
                        wl = [kv_ld, kt_ld, (b.s_dve, xnT_last)]
                        wl += psq_free[pi] or []
                        last = None
                        for kc in range(16):
                            last = b.op("tensor", lambda e, kc=kc, m=m, pi=pi: e.matmul(
                                psq[pi][:], kvaw[:, m * 2048 + kc * 128: m * 2048 + (kc + 1) * 128],
                                xnT[:, kc * 512:(kc + 1) * 512], start=(kc == 0), stop=(kc == 15)),
                                waits=wl, inc=(b.s_pe if kc == 15 else None))
                            wl = []
                        ev = b.op("vector", lambda e, m=m, pi=pi: e.tensor_copy(
                            kvraw[:, m * 512:(m + 1) * 512], psq[pi][:]),
                            waits=[(b.s_pe, last)], inc=b.s_dve)
                        b.op("scalar", lambda e, m=m: e.activation(
                            sqk[:, m * 512:(m + 1) * 512], kvraw[:, m * 512:(m + 1) * 512],
                            AF.Square), waits=[(b.s_dve, ev)], inc=b.s_act)
                        psq_free[pi] = [(b.s_dve, ev)]
                    wl = list(psq_free[0] or [])
                    last = None
                    for kc in range(16):
                        last = b.op("tensor", lambda e, kc=kc: e.matmul(
                            psq[0][0:64, :], kvpw[:, kc * 64:(kc + 1) * 64],
                            xnT[:, kc * 512:(kc + 1) * 512], start=(kc == 0), stop=(kc == 15)),
                            waits=wl, inc=(b.s_pe if kc == 15 else None))
                        wl = []
                    kpv = b.op("vector", lambda e, stt_=stt_: e.tensor_copy(
                        kpeT[0:64, stt_ * 512:(stt_ + 1) * 512], psq[0][0:64, :]),
                        waits=[(b.s_pe, last)], inc=b.s_dve)
                    psq_free[0] = [(b.s_dve, kpv)]
                    sq_done = (b.s_act, b.s_act.n)
                    last = None
                    wl = [sq_done, (b.s_dve, b.s_dve.n)]
                    for m in range(12):
                        last = b.op("tensor", lambda e, m=m: e.matmul(
                            psr[0:1, :], self.t_onesb[:], sqq[:, m * 512:(m + 1) * 512],
                            start=(m == 0), stop=(m == 11)), waits=wl,
                            inc=(b.s_pe if m == 11 else None))
                        wl = []
                    s1 = b.op("scalar", lambda e: e.activation(rrow[0:1, :], psr[0:1, :],
                         AF.Sqrt, bias=self.t_eps6[0:1, :], scale=1.0 / 1536),
                         waits=[(b.s_pe, last)], inc=b.s_act)
                    r1 = b.op("vector", lambda e: e.reciprocal(rrow[0:1, :], rrow[0:1, :]),
                         waits=[(b.s_act, s1)], inc=b.s_dve)
                    m1 = b.op("tensor", lambda e: e.matmul(psrep[:], self.t_scb[:],
                         rrow[0:1, :], start=True, stop=True), waits=[(b.s_dve, r1)], inc=b.s_pe)
                    ap_last = None
                    for m in range(12):
                        ap_last = b.op("vector", lambda e, m=m, stt_=stt_: e.scalar_tensor_tensor(
                            qanT[:, m * S + stt_ * 512: m * S + (stt_ + 1) * 512],
                            qaraw[:, m * 512:(m + 1) * 512], qalnw_sb[:, m:m + 1], psrep[:],
                            ALU.mult, ALU.mult), waits=[(b.s_pe, m1)], inc=b.s_dve)
                    last = None
                    wl = [(b.s_dve, ap_last)]
                    for m in range(4):
                        last = b.op("tensor", lambda e, m=m: e.matmul(
                            psr[0:1, :], self.t_onesb[:], sqk[:, m * 512:(m + 1) * 512],
                            start=(m == 0), stop=(m == 3)), waits=wl,
                            inc=(b.s_pe if m == 3 else None))
                        wl = []
                    s2 = b.op("scalar", lambda e: e.activation(rrow[0:1, :], psr[0:1, :],
                         AF.Sqrt, bias=self.t_eps6[0:1, :], scale=1.0 / 512),
                         waits=[(b.s_pe, last)], inc=b.s_act)
                    r2 = b.op("vector", lambda e: e.reciprocal(rrow[0:1, :], rrow[0:1, :]),
                         waits=[(b.s_act, s2)], inc=b.s_dve)
                    m2 = b.op("tensor", lambda e: e.matmul(psrep[:], self.t_ones[:],
                         rrow[0:1, :], start=True, stop=True), waits=[(b.s_dve, r2)], inc=b.s_pe)
                    for m in range(4):
                        b.op("vector", lambda e, m=m, stt_=stt_: e.scalar_tensor_tensor(
                            ckvnT[:, m * S + stt_ * 512: m * S + (stt_ + 1) * 512],
                            kvraw[:, m * 512:(m + 1) * 512], kvlnw_sb[:, m:m + 1], psrep[:],
                            ALU.mult, ALU.mult), waits=[(b.s_pe, m2)], inc=b.s_dve)
                kpe_done = (b.s_dve, b.s_dve.n)
                psro_free = None
                for blk in range(4):
                    wl = [kpe_done, kt_ld]
                    if psro_free is not None:
                        wl.append(psro_free)
                    rmv = b.op("tensor", lambda e, blk=blk: e.matmul(
                        psro[:], rotk_sb[:], kpeT[0:64, blk * 512:(blk + 1) * 512],
                        start=True, stop=True), waits=wl, inc=b.s_pe)
                    tv1 = b.op("vector", lambda e, blk=blk: e.tensor_tensor(
                        t1b[0:64, :], kpeT[0:64, blk * 512:(blk + 1) * 512],
                        rkc[0:64, blk * 512:(blk + 1) * 512], ALU.mult), inc=b.s_dve)
                    tv2 = b.op("vector", lambda e, blk=blk: e.tensor_tensor(
                        t2b[0:64, :], psro[:],
                        rks[0:64, blk * 512:(blk + 1) * 512], ALU.mult),
                        waits=[(b.s_pe, rmv)], inc=b.s_dve)
                    fv = b.op("vector", lambda e, blk=blk: e.tensor_tensor(
                        kpeT[0:64, blk * 512:(blk + 1) * 512], t1b[0:64, :], t2b[0:64, :],
                        ALU.add), inc=b.s_dve)
                    psro_free = (b.s_dve, fv)
                b.barrier()
            if stages < 2:
                return
            with ExitStack() as st2:
                qstage = b.sb("b_qst", [128, S], BF16, st2)
                qpw = [b.sb(f"b_qpw{i}", [128, 12 * 128], BF16, st2) for i in range(2)]
                rqc = b.sb("b_rqc", [128, S], BF16, st2)
                rqs = b.sb("b_rqs", [128, S], BF16, st2)
                rotq_sb = b.sb("b_rotq", [128, 128], BF16, st2)
                t1q = b.sb("b_t1q", [128, 512], BF16, st2)
                t2q = b.sb("b_t2q", [128, 512], BF16, st2)
                psq2 = [b.ps(f"b2_pq{i}", [128, 512], F32, st2) for i in range(2)]
                psb = [b.ps(f"b2_pb{i}", [128, 512], F32, st2) for i in range(2)]
                s_t = b.ssem("bqt")
                b.dma("sync", rqc[:], self.ropeqc[:], sem=s_t)
                b.dma("sync", rqs[:], self.ropeqs[:], sem=s_t)
                b.dma("sync", rotq_sb[:], self.rotq[:], sem=s_t)
                tab_ld = (s_t, s_t.n)
                s_qp = [b.ssem("bqp0"), b.ssem("bqp1")]
                qpw_free = [None, None]
                psq_free = [None, None]
                psb_free = [None, None]
                s_qpe_st = b.ssem("bqpes")
                qstage_war = None
                for j in range(8):
                    sl = j % 2
                    wlw = qpw_free[sl] or []
                    wv = self.load_w("qbp", j, qpw[sl], s_qp[sl], waits=wlw)
                    stage_vals = []
                    for tblk in range(4):
                        pi = tblk % 2
                        wl = [wv]
                        wl += psq_free[pi] or []
                        if tblk == 0 and qstage_war is not None:
                            wl.append(qstage_war)
                        last = None
                        for kc in range(12):
                            last = b.op("tensor", lambda e, kc=kc, sl=sl, tblk=tblk, pi=pi: e.matmul(
                                psq2[pi][:], qpw[sl][:, kc * 128:(kc + 1) * 128],
                                qanT[:, kc * S + tblk * 512: kc * S + (tblk + 1) * 512],
                                start=(kc == 0), stop=(kc == 11)),
                                waits=wl, inc=(b.s_pe if kc == 11 else None))
                            wl = []
                        evw = [(b.s_pe, last)]
                        if tblk == 0 and qstage_war is not None:
                            evw.append(qstage_war)
                        ev = b.op("vector", lambda e, tblk=tblk, pi=pi: e.tensor_copy(
                            qstage[:, tblk * 512:(tblk + 1) * 512], psq2[pi][:]),
                            waits=evw, inc=b.s_dve)
                        psq_free[pi] = [(b.s_dve, ev)]
                        stage_vals.append(ev)
                    qpw_free[sl] = [(b.s_pe, b.s_pe.n)]
                    rope_last = None
                    for tblk in range(4):
                        psl = tblk % 2
                        wl = [(b.s_dve, stage_vals[tblk]), tab_ld]
                        wl += psb_free[psl] or []
                        rmv = b.op("tensor", lambda e, tblk=tblk, psl=psl: e.matmul(
                            psb[psl][:], rotq_sb[:], qstage[:, tblk * 512:(tblk + 1) * 512],
                            start=True, stop=True), waits=wl, inc=b.s_pe)
                        b.op("vector", lambda e, tblk=tblk: e.tensor_tensor(
                            t1q[:], qstage[:, tblk * 512:(tblk + 1) * 512],
                            rqc[:, tblk * 512:(tblk + 1) * 512], ALU.mult), inc=b.s_dve)
                        b.op("vector", lambda e, tblk=tblk, psl=psl: e.tensor_tensor(
                            t2q[:], psb[psl][:],
                            rqs[:, tblk * 512:(tblk + 1) * 512], ALU.mult),
                            waits=[(b.s_pe, rmv)], inc=b.s_dve)
                        rope_last = b.op("vector", lambda e, tblk=tblk: e.tensor_tensor(
                            qstage[:, tblk * 512:(tblk + 1) * 512], t1q[:], t2q[:],
                            ALU.add), inc=b.s_dve)
                        psb_free[psl] = [(b.s_dve, rope_last)]
                    b.dma("sync", self.qpe_spill[2 * j, :, :], qstage[0:64, :],
                          waits=[(b.s_dve, rope_last)], sem=s_qpe_st)
                    v2 = b.dma("sync", self.qpe_spill[2 * j + 1, :, :], qstage[64:128, :],
                               sem=s_qpe_st)
                    qstage_war = (s_qpe_st, v2)
                b.barrier()
            if stages < 3:
                return
            with ExitStack() as st3:
                knT = b.sb("b_knT", [128, S], BF16, st3)
                Vb = b.sb("b_V", [128, 16 * 128], BF16, st3)
                qnT = b.sb("b_qnT", [128, S], BF16, st3)
                qpeh = [b.sb(f"b_qpeh{i}", [64, S], BF16, st3) for i in range(2)]
                sq4 = b.sb("b_sq4", [128, S], BF16, st3)
                sq4k = b.sb("b_sq4k", [128, S], BF16, st3)
                sqpe = b.sb("b_sqpe", [64, S], BF16, st3)
                sqkpe = b.sb("b_sqkpe", [64, S], BF16, st3)
                p4 = [b.sb(f"b_p4{i}", [128, 4 * S], BF16, st3) for i in range(2)]
                PT = b.sb("b_PT", [128, 16 * 512], BF16, st3)
                avb = b.sb("b_avb", [128, 512], BF16, st3)
                qbnw = [b.sb(f"b_qbnw{i}", [128, 12 * 128], BF16, st3) for i in range(2)]
                kvbnw = [b.sb(f"b_kvbnw{i}", [128, 4 * 128], BF16, st3) for i in range(2)]
                kvbvw = [b.sb(f"b_kvbvw{i}", [128, 4 * 128], BF16, st3) for i in range(2)]
                nmx = b.sb("b_nmx", [1, 16], F32, st3)
                ctmp = b.sb("b_ctmp", [1, 4], F32, st3)
                cbias = b.sb("b_cb", [128, 1], F32, st3)
                asum = b.sb("b_as", [128, 16], F32, st3)
                atot = b.sb("b_at", [128, 4], F32, st3)
                arcp = b.sb("b_ar", [128, 4], F32, st3)
                pss = [b.ps(f"b_ss{i}", [128, 512], F32, st3) for i in range(2)]
                pst2 = [b.ps(f"b_pt2{i}", [128, 1024], BF16, st3) for i in range(2)]
                psa = [b.ps(f"b_pa{i}", [128, 512], F32, st3) for i in range(2)]
                psn = b.ps("b_psn", [1, 512], F32, st3)
                psc = b.ps("b_psc", [128, 1], F32, st3)
                s_bw = [b.ssem("bbw0"), b.ssem("bbw1")]
                s_qh = [b.ssem("bqh0"), b.ssem("bqh1")]
                s_av = b.ssem("bav")
                kq_last = None
                for blk in range(4):
                    kq_last = b.op("scalar", lambda e, blk=blk: e.activation(
                        sqkpe[0:64, blk * 512:(blk + 1) * 512],
                        kpeT[0:64, blk * 512:(blk + 1) * 512], AF.Square), inc=b.s_act)
                bw_free = [None, None]
                qh_free = [None, None]
                pss_free = [None, None]
                pst2_free = [None, None]
                psa_free = [None, None]
                av_pend = None
                for h in range(16):
                    sl = h % 2
                    wlw = bw_free[sl] or []
                    wq = self.load_w("qbn", h, qbnw[sl], s_bw[sl], waits=wlw)
                    wn = self.load_w("kvbn", h, kvbnw[sl], s_bw[sl])
                    wv_ = self.load_w("kvbv", h, kvbvw[sl], s_bw[sl])
                    wload = (s_bw[sl], s_bw[sl].n)
                    wlq = qh_free[sl] or []
                    qpl = b.dma("sync", qpeh[sl][0:64, :], self.qpe_spill[h, :, :],
                                waits=wlq, sem=s_qh[sl])
                    qpe_ld = (s_qh[sl], qpl)
                    kn_vals = []
                    for blk in range(4):
                        pi = blk % 2
                        wl = [wload]
                        wl += pss_free[pi] or []
                        last = None
                        for kc in range(4):
                            last = b.op("tensor", lambda e, kc=kc, blk=blk, pi=pi, sl=sl: e.matmul(
                                pss[pi][:], kvbnw[sl][:, kc * 128:(kc + 1) * 128],
                                ckvnT[:, kc * S + blk * 512: kc * S + (blk + 1) * 512],
                                start=(kc == 0), stop=(kc == 3)),
                                waits=wl, inc=(b.s_pe if kc == 3 else None))
                            wl = []
                        ev = b.op("vector", lambda e, blk=blk, pi=pi: e.tensor_copy(
                            knT[:, blk * 512:(blk + 1) * 512], pss[pi][:]),
                            waits=[(b.s_pe, last)], inc=b.s_dve)
                        sv = b.op("scalar", lambda e, blk=blk: e.activation(
                            sq4k[:, blk * 512:(blk + 1) * 512], knT[:, blk * 512:(blk + 1) * 512],
                            AF.Square), waits=[(b.s_dve, ev)], inc=b.s_act)
                        pss_free[pi] = [(b.s_dve, ev)]
                        kn_vals.append((ev, sv))
                    v_vals = []
                    for kt in range(16):
                        pi = kt % 2
                        wl = [wload]
                        wl += psa_free[pi] or []
                        last = None
                        for kc in range(4):
                            last = b.op("tensor", lambda e, kc=kc, kt=kt, pi=pi, sl=sl: e.matmul(
                                psa[pi][:, 0:128], ckvnT[:, kc * S + kt * 128: kc * S + (kt + 1) * 128],
                                kvbvw[sl][:, kc * 128:(kc + 1) * 128],
                                start=(kc == 0), stop=(kc == 3)),
                                waits=wl, inc=(b.s_pe if kc == 3 else None))
                            wl = []
                        ev = b.op("vector", lambda e, kt=kt, pi=pi: e.tensor_copy(
                            Vb[:, kt * 128:(kt + 1) * 128], psa[pi][:, 0:128]),
                            waits=[(b.s_pe, last)], inc=b.s_dve)
                        psa_free[pi] = [(b.s_dve, ev)]
                        v_vals.append(ev)
                    v_done = b.s_dve.n
                    qn_vals = []
                    for tblk in range(4):
                        pi = tblk % 2
                        wl = [wload]
                        wl += pss_free[pi] or []
                        last = None
                        for kc in range(12):
                            last = b.op("tensor", lambda e, kc=kc, tblk=tblk, pi=pi, sl=sl: e.matmul(
                                pss[pi][:], qbnw[sl][:, kc * 128:(kc + 1) * 128],
                                qanT[:, kc * S + tblk * 512: kc * S + (tblk + 1) * 512],
                                start=(kc == 0), stop=(kc == 11)),
                                waits=wl, inc=(b.s_pe if kc == 11 else None))
                            wl = []
                        ev = b.op("vector", lambda e, tblk=tblk, pi=pi: e.tensor_copy(
                            qnT[:, tblk * 512:(tblk + 1) * 512], pss[pi][:]),
                            waits=[(b.s_pe, last)], inc=b.s_dve)
                        sv = b.op("scalar", lambda e, tblk=tblk: e.activation(
                            sq4[:, tblk * 512:(tblk + 1) * 512], qnT[:, tblk * 512:(tblk + 1) * 512],
                            AF.Square), waits=[(b.s_dve, ev)], inc=b.s_act)
                        pss_free[pi] = [(b.s_dve, ev)]
                        qn_vals.append((ev, sv))
                    qpe_sq = []
                    for blk in range(4):
                        qpe_sq.append(b.op("scalar", lambda e, blk=blk, sl=sl: e.activation(
                            sqpe[0:64, blk * 512:(blk + 1) * 512],
                            qpeh[sl][0:64, blk * 512:(blk + 1) * 512], AF.Square),
                            waits=[qpe_ld], inc=b.s_act))
                    psn_free = None
                    for blk in range(4):
                        wl = [(b.s_act, kn_vals[blk][1]), (b.s_act, kq_last)]
                        if psn_free is not None:
                            wl.append(psn_free)
                        b.op("tensor", lambda e, blk=blk: e.matmul(
                            psn[0:1, :], self.t_onesb[:], sq4k[:, blk * 512:(blk + 1) * 512],
                            start=True, stop=False), waits=wl, inc=None)
                        last = b.op("tensor", lambda e, blk=blk: e.matmul(
                            psn[0:1, :], self.t_onesb[0:64, :], sqkpe[0:64, blk * 512:(blk + 1) * 512],
                            start=False, stop=True), inc=b.s_pe)
                        rv = b.op("vector", lambda e, blk=blk: e.reduce_max(
                            nmx[0:1, 4 + blk:5 + blk], psn[0:1, :], axis=AX.X),
                            waits=[(b.s_pe, last)], inc=b.s_dve)
                        psn_free = (b.s_dve, rv)
                    psn_free2 = psn_free
                    for blk in range(4):
                        wl = [(b.s_act, qn_vals[blk][1]), (b.s_act, qpe_sq[blk])]
                        if psn_free2 is not None:
                            wl.append(psn_free2)
                        b.op("tensor", lambda e, blk=blk: e.matmul(
                            psn[0:1, :], self.t_onesb[:], sq4[:, blk * 512:(blk + 1) * 512],
                            start=True, stop=False), waits=wl, inc=None)
                        last = b.op("tensor", lambda e, blk=blk: e.matmul(
                            psn[0:1, :], self.t_onesb[0:64, :], sqpe[0:64, blk * 512:(blk + 1) * 512],
                            start=False, stop=True), inc=b.s_pe)
                        rv = b.op("vector", lambda e, blk=blk: e.reduce_max(
                            nmx[0:1, blk:blk + 1], psn[0:1, :], axis=AX.X),
                            waits=[(b.s_pe, last)], inc=b.s_dve)
                        psn_free2 = (b.s_dve, rv)
                    b.op("vector", lambda e: e.reduce_max(ctmp[0:1, 1:2], nmx[0:1, 4:8],
                         axis=AX.X), inc=b.s_dve)
                    dv0 = b.op("vector", lambda e: e.reduce_max(ctmp[0:1, 0:1], nmx[0:1, 0:4],
                         axis=AX.X), inc=b.s_dve)
                    pv = b.op("gpsimd", lambda e: e.tensor_mul(ctmp[0:1, 2:3], ctmp[0:1, 0:1],
                         ctmp[0:1, 1:2]), waits=[(b.s_dve, dv0)], inc=b.s_pool)
                    sv = b.op("scalar", lambda e: e.activation(ctmp[0:1, 3:4], ctmp[0:1, 2:3],
                         AF.Sqrt), waits=[(b.s_pool, pv)], inc=b.s_act)
                    nv = b.op("gpsimd", lambda e: e.tensor_scalar(ctmp[0:1, 2:3], ctmp[0:1, 3:4],
                         -1.0, None, ALU.mult), waits=[(b.s_act, sv)], inc=b.s_pool)
                    mv = b.op("tensor", lambda e: e.matmul(psc[:, 0:1], self.t_ones[:],
                         ctmp[0:1, 2:3], start=True, stop=True), waits=[(b.s_pool, nv)], inc=b.s_pe)
                    cbv = b.op("vector", lambda e: e.tensor_copy(cbias[:], psc[:, 0:1]),
                         waits=[(b.s_pe, mv), (b.s_act, b.s_act.n)], inc=b.s_dve)
                    sum_done = cbv
                    pm_of = {}

                    def stage1(qb, sl=sl, qpe_ld=qpe_ld, cbv=cbv):
                        nonlocal sum_done
                        pb = p4[qb % 2][:]
                        ssi = 0
                        exp_last = [0] * 4
                        for qt in range(4):
                            qtg = qb * 4 + qt
                            klen = (qtg + 1) * 128
                            nblk = (klen + 511) // 512
                            for blk in range(nblk):
                                w = min(512, klen - blk * 512)
                                pi = ssi % 2
                                ssi += 1
                                wl = list(pss_free[pi] or [])
                                last = b.op("tensor", lambda e, qtg=qtg, blk=blk, w=w, pi=pi: e.matmul(
                                    pss[pi][:, 0:w], qnT[:, qtg * 128:(qtg + 1) * 128],
                                    knT[:, blk * 512: blk * 512 + w],
                                    start=True, stop=False), waits=wl, inc=None)
                                last = b.op("tensor", lambda e, qtg=qtg, blk=blk, w=w, pi=pi, sl=sl: e.matmul(
                                    pss[pi][:, 0:w], qpeh[sl][0:64, qtg * 128:(qtg + 1) * 128],
                                    kpeT[0:64, blk * 512: blk * 512 + w],
                                    start=False, stop=True), waits=[qpe_ld], inc=b.s_pe)
                                expw = [(b.s_pe, last)]
                                if blk == nblk - 1:
                                    mkv = b.op("vector", lambda e, w=w, pi=pi: e.tensor_tensor(
                                        pss[pi][:, w - 128:w], pss[pi][:, w - 128:w], self.t_mask[:],
                                        ALU.add), waits=[(b.s_pe, last)], inc=b.s_dve)
                                    expw = [(b.s_dve, mkv)]
                                ev = b.op("scalar", lambda e, qt=qt, blk=blk, w=w, pi=pi, pb=pb: e.activation(
                                    pb[:, qt * S + blk * 512: qt * S + blk * 512 + w], pss[pi][:, 0:w],
                                    AF.Exp, bias=cbias[:], scale=1.0,
                                    accum_out=asum[:, qt * 4 + blk: qt * 4 + blk + 1]),
                                    waits=expw + [(b.s_dve, sum_done), (b.s_dve, cbv)], inc=b.s_act)
                                exp_last[qt] = ev
                                pss_free[pi] = [(b.s_act, ev)]
                        for qt in range(4):
                            qtg = qb * 4 + qt
                            nblk = ((qtg + 1) * 128 + 511) // 512
                            b.op("vector", lambda e, qt=qt, nblk=nblk: e.reduce_sum(
                                atot[:, qt:qt + 1], asum[:, qt * 4: qt * 4 + nblk], axis=AX.X),
                                waits=[(b.s_act, exp_last[qt])], inc=b.s_dve)
                        for qt in range(4):
                            b.op("vector", lambda e, qt=qt: e.reciprocal(
                                arcp[:, qt:qt + 1], atot[:, qt:qt + 1]), inc=b.s_dve)
                        pm_last = None
                        for qt in range(4):
                            qtg = qb * 4 + qt
                            klen = (qtg + 1) * 128
                            pm_last = b.op("vector", lambda e, qt=qt, klen=klen, pb=pb: e.tensor_scalar(
                                pb[:, qt * S: qt * S + klen], pb[:, qt * S: qt * S + klen],
                                arcp[:, qt:qt + 1], None, ALU.mult), inc=b.s_dve)
                        sum_done = pm_last
                        pm_of[qb] = pm_last

                    def stage2(qb, h=h, sl=sl):
                        nonlocal av_pend
                        pb = p4[qb % 2][:]
                        pm_last = pm_of[qb]
                        kmax = (qb + 1) * 4
                        ptev = None
                        for kc in range(kmax):
                            psl = kc % 2
                            wl = [(b.s_dve, pm_last)]
                            wl += pst2_free[psl] or []
                            lastt = None
                            for qt in range(4):
                                if qb * 4 + qt < kc:
                                    continue
                                lastt = b.op("tensor", lambda e, qt=qt, kc=kc, psl=psl, pb=pb: e.transpose(
                                    pst2[psl][:, qt * 128:(qt + 1) * 128],
                                    pb[:, qt * S + kc * 128: qt * S + (kc + 1) * 128],
                                    self.t_identb[:]), waits=wl, inc=b.s_pe)
                                wl = []
                            q0 = max(0, kc - qb * 4)
                            if q0 > 0:
                                b.op("vector", lambda e, kc=kc, q0=q0: e.memset(
                                    PT[:, kc * 512: kc * 512 + q0 * 128], 0.0), inc=b.s_dve)
                            ptev = b.op("vector", lambda e, kc=kc, psl=psl, q0=q0: e.tensor_copy(
                                PT[:, kc * 512 + q0 * 128:(kc + 1) * 512],
                                pst2[psl][:, q0 * 128:512]),
                                waits=[(b.s_pe, lastt)], inc=b.s_dve)
                            pst2_free[psl] = [(b.s_dve, ptev)]
                        pi = qb % 2
                        wl = [(b.s_dve, ptev), (b.s_dve, v_done)]
                        wl += psa_free[pi] or []
                        last = None
                        for kc in range(kmax):
                            last = b.op("tensor", lambda e, kc=kc, pi=pi, sp_=(kc == kmax - 1): e.matmul(
                                psa[pi][:], Vb[:, kc * 128:(kc + 1) * 128],
                                PT[:, kc * 512:(kc + 1) * 512],
                                start=(kc == 0), stop=sp_), waits=wl,
                                inc=(b.s_pe if kc == kmax - 1 else None))
                            wl = []
                        avw = [(b.s_pe, last)]
                        if av_pend is not None:
                            avw.append(av_pend)
                        avv = b.op("vector", lambda e, pi=pi: e.tensor_copy(
                            avb[:], psa[pi][:]), waits=avw, inc=b.s_dve)
                        psa_free[pi] = [(b.s_dve, avv)]
                        stv = b.dma("sync", self.av_spill[h, :, qb * 512:(qb + 1) * 512],
                                    avb[:], waits=[(b.s_dve, avv)], sem=s_av)
                        av_pend = (s_av, stv)

                    stage1(0)
                    stage1(1)
                    stage2(0)
                    stage1(2)
                    stage2(1)
                    stage1(3)
                    stage2(2)
                    stage2(3)
                    bw_free[sl] = [(b.s_pe, b.s_pe.n)]
                    qh_free[sl] = [(b.s_pe, b.s_pe.n)]
                b.barrier()
            if stages < 4:
                return
            with ExitStack() as st4:
                avTs = b.sb("b_avTs", [128, 16 * 512], BF16, st4)
                yT = b.sb("b_yT", [128, 16 * 512], BF16, st4)
                owb = [b.sb(f"b_ow{i}", [128, 16 * 128], BF16, st4) for i in range(2)]
                rx3 = [b.sb(f"b3_rx{i}", [128, D], F32, st4) for i in range(2)]
                psa3 = [b.ps(f"b3_pa{i}", [128, 512], F32, st4) for i in range(2)]
                slab = b.ps("b3_slab", [128, 2048], BF16, st4)
                s_avl = [b.ssem("bavl0"), b.ssem("bavl1")]
                rx_sems = [b.ssem("brx0"), b.ssem("brx1")]
                ow_free = [None, None]
                psa_free = [None, None]
                slab_free = None
                rx_free = [None, None]
                avl_free = [None, None]
                for qb in range(4):
                    asl = qb % 2
                    avw = [avl_free[asl]] if avl_free[asl] is not None else []
                    avv = None
                    for hc in range(16):
                        avv = b.dma("sync", avTs[:, hc * 512:(hc + 1) * 512],
                                    self.av_spill[hc, :, qb * 512:(qb + 1) * 512],
                                    waits=avw, sem=s_avl[asl])
                        avw = []
                    av_ld = (s_avl[asl], avv)
                    for oc in range(16):
                        sl = oc % 2
                        wl = [ow_free[sl]] if ow_free[sl] is not None else []
                        wv = self.load_w("ow", oc, owb[sl], b.ssem(f"bow{sl}"), waits=wl)
                        pi = oc % 2
                        wl = [wv, av_ld]
                        if psa_free[pi] is not None:
                            wl.append(psa_free[pi])
                        last = None
                        for hc in range(16):
                            last = b.op("tensor", lambda e, hc=hc, sl=sl, pi=pi: e.matmul(
                                psa3[pi][:], owb[sl][:, hc * 128:(hc + 1) * 128],
                                avTs[:, hc * 512:(hc + 1) * 512], start=(hc == 0), stop=(hc == 15)),
                                waits=wl, inc=(b.s_pe if hc == 15 else None))
                            wl = []
                        ow_free[sl] = (b.s_pe, last)
                        yv = b.op("vector", lambda e, oc=oc, pi=pi: e.tensor_copy(
                            yT[:, oc * 512:(oc + 1) * 512], psa3[pi][:]),
                            waits=[(b.s_pe, last)], inc=b.s_dve)
                        psa_free[pi] = (b.s_dve, yv)
                    yT_done = b.s_dve.n
                    avl_free[asl] = (b.s_pe, b.s_pe.n)
                    for i in range(4):
                        t = qb * 4 + i
                        ri = i % 2
                        wl = [rx_free[ri]] if rx_free[ri] is not None else []
                        rxl = b.dma("sync", rx3[ri][:], src_fn(t), waits=wl, sem=rx_sems[ri])
                        stv, sfree = self.emit_y_add(yT, (b.s_dve, yT_done), rx3[ri],
                                                     (rx_sems[ri], rxl), i,
                                                     dst_fn(t), slab[:], slab_free, rms_col=None)
                        slab_free = sfree
                        rx_free[ri] = stv
        b.barrier()


def build_nc(chain):
    nc = bass.Bass("TRN2", target_bir_lowering=False, detect_race_conditions=False)
    with ExitStack() as stack:
        mk = MK(nc, stack, chain)
        mk.build()
        block = stack.enter_context(nc.Block())
        mk.b.finalize(block)
    return nc


FULL_CHAIN = [(k, s) for s in range(SEQ_PER_CORE)
              for k in ("attn_a", "ffn", "attn_b", "ffn")]


def make_in_maps(inputs, chain=None):
    g = prep_host(inputs)
    x = np.asarray(inputs["L0_attn_a_x"], np.float32)
    in_maps = []
    for c in range(NCORES):
        m = dict(
            ws=g["ws"], wa=g["wa"], woT=g["woT"], bo=g["bo"], qaT=g["qaT"],
            qalnw=g["qalnw"], qbnT=g["qbnT"], qbpT=g["qbpT"], kvaT=g["kvaT"],
            kvlnw=g["kvlnw"], kvbnT=g["kvbnT"], kvbvT=g["kvbvT"], owT=g["owT"],
            kvapeT=g["kvapeT"],
            gup=g["gup"], dnp=g["dnp"], gusT=g["gusT"], dnsT=g["dnsT"],
            identb=g["identb"], mask=g["mask"], ropeqc=g["ropeqc"], ropeqs=g["ropeqs"],
            ropekc=g["ropekc"], ropeks=g["ropeks"], rotq=g["rotq"], rotk=g["rotk"],
            x_in=np.ascontiguousarray(x[c * SEQ_PER_CORE:(c + 1) * SEQ_PER_CORE]),
        )
        in_maps.append(m)
    return in_maps


def kernel(**inputs):
    chain = FULL_CHAIN
    in_maps = make_in_maps(inputs, chain)
    nc = build_nc(chain)
    res = run_bass_kernel_spmd(nc, in_maps, core_ids=list(range(NCORES)))
    out = np.concatenate([np.asarray(r["y"], np.float32) for r in res.results], axis=0)
    return out.reshape(16, S, D)


# version 16
# speedup vs baseline: 1.1967x; 1.0405x over previous
import os
import numpy as np
import ml_dtypes
from contextlib import ExitStack

import concourse.bass as bass
import concourse.mybir as mybir
from concourse.bass_utils import run_bass_kernel_spmd

F32 = mybir.dt.float32
BF16 = mybir.dt.bfloat16
I32 = mybir.dt.int32
AF = mybir.ActivationFunctionType
ALU = mybir.AluOpType
AX = mybir.AxisListType

NCORES = 8
SEQ_PER_CORE = 2
S = 2048
D = 2048
NT = S // 128
I_FFN = 5632
GU_OC = 88
DN_OC = 16
SC_B = float(192 ** -0.5)
EPS_LN = 1e-5
EPS_B = 1e-6
MASK_NEG = -30000.0

ENGINES = ("sync", "tensor", "vector", "scalar", "gpsimd")



def _repack_nibbles(packed: np.ndarray, wpr: int) -> np.ndarray:
    u = packed.view(np.uint32)
    o, w = u.shape
    assert w == wpr
    shifts = (4 * np.arange(8, dtype=np.uint32))[None, None, :]
    nib = ((u[:, :, None] >> shifts) & np.uint32(15)).astype(np.uint8)
    unp = nib.reshape(o, w * 8)
    words = np.zeros((o, wpr), np.uint32)
    for pp in range(4):
        blk = unp[:, pp * 2 * wpr:(pp + 1) * 2 * wpr]
        lo = blk[:, 0::2].astype(np.uint32)
        hi = blk[:, 1::2].astype(np.uint32)
        words |= lo << np.uint32(4 * pp)
        words |= hi << np.uint32(4 * pp + 16)
    return words.view(np.int32)


def _rope_tables():
    inv = (1.0 / (10000.0 ** (np.arange(0, 64, 2, dtype=np.float32) / 64.0))).astype(np.float32)
    t = np.arange(S, dtype=np.float32)
    fr = np.outer(t, inv).astype(np.float32)
    emb = np.concatenate([fr, fr], axis=-1)
    cos = np.cos(emb).astype(np.float32)
    sin = np.sin(emb).astype(np.float32)
    cosT = np.ascontiguousarray(cos.T)
    sinT = np.ascontiguousarray(sin.T)
    qcos = np.concatenate([cosT, cosT], axis=0)
    qsin = np.concatenate([sinT, sinT], axis=0)
    return (qcos.astype(ml_dtypes.bfloat16), qsin.astype(ml_dtypes.bfloat16),
            cosT.astype(ml_dtypes.bfloat16), sinT.astype(ml_dtypes.bfloat16))


def _rot_mat():
    R = np.zeros((64, 64), np.float32)
    for i in range(32):
        R[i, i + 32] = -1.0
        R[i + 32, i] = 1.0
    RT = R.T
    blk = np.zeros((128, 128), np.float32)
    blk[:64, :64] = RT
    blk[64:, 64:] = RT
    return (np.ascontiguousarray(blk).astype(ml_dtypes.bfloat16),
            np.ascontiguousarray(RT).astype(ml_dtypes.bfloat16))


def _tile_w(w, W=128):
    K, M = w.shape
    kc = K // 128
    oc = M // W
    t = w.reshape(kc, 128, oc, W).transpose(2, 1, 0, 3).reshape(oc, 128, kc * W)
    return np.ascontiguousarray(t, dtype=np.float32)


def prep_host(inputs):
    g = {}
    f32 = np.float32
    asc = np.ascontiguousarray
    g["ws"] = _tile_w(np.asarray(inputs["L0_attn_a_Ws"], dtype=f32))
    g["wa"] = _tile_w(np.asarray(inputs["L0_attn_a_Wa"], dtype=f32))
    g["woT"] = _tile_w(np.asarray(inputs["L0_attn_a_Wo"], dtype=f32).T)
    g["bo"] = asc(inputs["L0_attn_a_bo"].reshape(16, 128).T, dtype=f32)
    g["qaT"] = _tile_w(np.asarray(inputs["L0_attn_b_q_a_w"], dtype=f32).T)
    g["qalnw"] = asc(inputs["L0_attn_b_q_a_ln_w"].reshape(12, 128).T, dtype=f32)
    qb = np.asarray(inputs["L0_attn_b_q_b_w"], dtype=f32)
    qb3 = qb.reshape(16, 192, 1536)
    qbn = qb3[:, :128, :].reshape(16 * 128, 1536)
    qbp = qb3[:, 128:, :]
    qbp_packed = qbp.reshape(8, 2 * 64, 1536).reshape(16 * 64, 1536)
    g["qbnT"] = _tile_w(asc(qbn.T, dtype=f32))
    g["qbpT"] = _tile_w(asc(qbp_packed.T, dtype=f32))
    kvaT = asc(np.asarray(inputs["L0_attn_b_kv_a_w"], dtype=f32).T)
    g["kvaT"] = _tile_w(kvaT[:, 0:512])
    g["kvapeT"] = _tile_w(kvaT[:, 512:576], W=64)
    g["kvlnw"] = asc(inputs["L0_attn_b_kv_a_ln_w"].reshape(4, 128).T, dtype=f32)
    kvb = np.asarray(inputs["L0_attn_b_kv_b_w"], dtype=f32)
    kvb3 = kvb.reshape(16, 256, 512)
    g["kvbnT"] = _tile_w(asc(kvb3[:, :128, :].reshape(16 * 128, 512).T, dtype=f32))
    g["kvbvT"] = _tile_w(asc(kvb3[:, 128:, :].reshape(16 * 128, 512).T, dtype=f32))
    g["owT"] = _tile_w(np.asarray(inputs["L0_attn_b_o_w"], dtype=f32).T)
    g["gup"] = _repack_nibbles(np.asarray(inputs["L0_ffn_gate_up_packed"]), 256)
    g["dnp"] = _repack_nibbles(np.asarray(inputs["L0_ffn_down_packed"]), 704)
    g["gusT"] = asc(np.asarray(inputs["L0_ffn_gate_up_scales"]).T, dtype=f32)
    g["dnsT"] = asc(np.asarray(inputs["L0_ffn_down_scales"]).T, dtype=f32)
    g["identb"] = np.eye(128, dtype=ml_dtypes.bfloat16)
    m = np.zeros((128, 128), np.float32)
    iu = np.triu_indices(128, 1)
    m[iu] = MASK_NEG
    g["mask"] = m
    qcos, qsin, kcos, ksin = _rope_tables()
    g["ropeqc"], g["ropeqs"], g["ropekc"], g["ropeks"] = qcos, qsin, kcos, ksin
    g["rotq"], g["rotk"] = _rot_mat()
    return g



class Sem:
    __slots__ = ("h", "n", "name")

    def __init__(self, h, name):
        self.h = h
        self.n = 0
        self.name = name


class _Op:
    __slots__ = ("waits", "fn", "incs")

    def __init__(self, waits, fn, incs):
        self.waits = waits
        self.fn = fn
        self.incs = incs


class Builder:
    def __init__(self, nc, stack):
        self.nc = nc
        self.stack = stack
        self.prog = {e: [] for e in ENGINES}
        self.last_wait = {e: {} for e in ENGINES}
        self.last_real = {e: None for e in ENGINES}
        self.last_inc = {e: None for e in ENGINES}
        self.s_pe = self.sem("s_pe")
        self.s_dve = self.sem("s_dve")
        self.s_act = self.sem("s_act")
        self.s_pool = self.sem("s_pool")
        self.s_dsync = self.sem("s_dsync")
        self.s_dgp = self.sem("s_dgp")
        self.s_bar = self.sem("s_bar")
        self.dma_sem = {"sync": self.s_dsync, "gpsimd": self.s_dgp}
        self.eng_sem = {"tensor": self.s_pe, "vector": self.s_dve,
                        "scalar": self.s_act, "gpsimd": self.s_pool}
        self.slot_sems = {}
        self.dma_used = {"sync": set(), "gpsimd": set()}

    def ssem(self, key):
        if key not in self.slot_sems:
            self.slot_sems[key] = self.sem("sl_" + key)
        return self.slot_sems[key]

    def sem(self, name):
        return Sem(self.stack.enter_context(self.nc.semaphore(name)), name)

    _uid = 0

    def _uniq(self, name):
        Builder._uid += 1
        return f"{name}_{Builder._uid}"

    def sb(self, name, shape, dtype, st=None):
        return (st or self.stack).enter_context(
            self.nc.sbuf_tensor(self._uniq(name), shape, dtype))

    def ps(self, name, shape, dtype=F32, st=None):
        return (st or self.stack).enter_context(
            self.nc.psum_tensor(self._uniq(name), shape, dtype))

    def op(self, eng, fn, waits=(), inc=None, k=1, is_dma=False):
        lw = self.last_wait[eng]
        wl = []
        for s, v in waits:
            if v > 0 and lw.get(s, 0) < v:
                lw[s] = v
                wl.append((s.h, v))
        incs = []
        val = None
        if inc is not None:
            kk = 16 if is_dma else k
            inc.n += kk
            val = inc.n
            incs.append((inc, kk))
        o = _Op(wl, fn, incs)
        self.prog[eng].append(o)
        self.last_real[eng] = o
        if inc is not None:
            self.last_inc[eng] = (inc, val)
        return val

    def dma(self, q, out, in_, waits=(), sem=None):
        if sem is None:
            sem = self.dma_sem[q]
        self.dma_used[q].add(sem)
        return self.op(q, (lambda e, out=out, in_=in_: e.dma_start(out=out, in_=in_)),
                       waits=waits, inc=sem, is_dma=True)

    def wait(self, eng, waits):
        lw = self.last_wait[eng]
        wl = []
        for s, v in waits:
            if v > 0 and lw.get(s, 0) < v:
                lw[s] = v
                wl.append((s.h, v))
        if wl:
            self.prog[eng].append(_Op(wl, None, []))

    def barrier(self):
        for eng in ("sync", "gpsimd"):
            for s in sorted(self.dma_used[eng], key=lambda s: s.name):
                self.wait(eng, [(s, s.n)])
            self.dma_used[eng] = set()
        for eng in ENGINES:
            o = self.last_real[eng]
            if o is None:
                continue
            li = self.last_inc[eng]
            if o.incs or (li is None):
                if li is not None:
                    self.wait(eng, [li])
                self.s_bar.n += 1
                self.prog[eng].append(_Op([], (lambda e: e.nop()), [(self.s_bar, 1)]))
            else:
                if li is not None:
                    self.wait(eng, [li])
                o.incs.append((self.s_bar, 1))
                self.s_bar.n += 1
            self.last_real[eng] = None
        tgt = self.s_bar.n
        for eng in ENGINES:
            self.wait(eng, [(self.s_bar, tgt)])

    def finalize(self, block):
        def runner(eng_name):
            def run(e):
                for o in self.prog[eng_name]:
                    for sh, v in o.waits:
                        e.wait_ge(sh, v)
                    if o.fn is not None:
                        ins = o.fn(e)
                        for s, kk in o.incs:
                            ins.then_inc(s.h, kk)
            return run
        block.sync(runner("sync"))
        block.tensor(runner("tensor"))
        block.vector(runner("vector"))
        block.scalar(runner("scalar"))
        block.gpsimd(runner("gpsimd"))



class MK:
    def __init__(self, nc, stack, chain):
        self.nc = nc
        self.st = stack
        self.chain = chain
        b = self.b = Builder(nc, stack)
        dt = nc.dram_tensor
        ap = lambda *a, **k: dt(*a, **k).ap()
        self.x_in = ap("x_in", [SEQ_PER_CORE, S, D], F32, kind="ExternalInput")
        self.ws = ap("ws", [2, 128, 16 * 128], F32, kind="ExternalInput")
        self.wa = ap("wa", [2, 128, 16 * 128], F32, kind="ExternalInput")
        self.woT = ap("woT", [16, 128, 16 * 128], F32, kind="ExternalInput")
        self.bo = ap("bo", [128, 16], F32, kind="ExternalInput")
        self.qaT = ap("qaT", [12, 128, 16 * 128], F32, kind="ExternalInput")
        self.qalnw = ap("qalnw", [128, 12], F32, kind="ExternalInput")
        self.qbnT = ap("qbnT", [16, 128, 12 * 128], F32, kind="ExternalInput")
        self.qbpT = ap("qbpT", [8, 128, 12 * 128], F32, kind="ExternalInput")
        self.kvaT = ap("kvaT", [4, 128, 16 * 128], F32, kind="ExternalInput")
        self.kvapeT = ap("kvapeT", [1, 128, 16 * 64], F32, kind="ExternalInput")
        self.kvlnw = ap("kvlnw", [128, 4], F32, kind="ExternalInput")
        self.kvbnT = ap("kvbnT", [16, 128, 4 * 128], F32, kind="ExternalInput")
        self.kvbvT = ap("kvbvT", [16, 128, 4 * 128], F32, kind="ExternalInput")
        self.owT = ap("owT", [16, 128, 16 * 128], F32, kind="ExternalInput")
        self.wsrc = {"ws": self.ws, "wa": self.wa, "wo": self.woT, "qa": self.qaT,
                     "qbn": self.qbnT, "qbp": self.qbpT, "kva": self.kvaT,
                     "kvape": self.kvapeT, "kvbn": self.kvbnT, "kvbv": self.kvbvT,
                     "ow": self.owT}
        self.gup = ap("gup", [2 * I_FFN, 256], I32, kind="ExternalInput")
        self.dnp = ap("dnp", [D, 704], I32, kind="ExternalInput")
        self.gusT = ap("gusT", [2 * I_FFN, 16], F32, kind="ExternalInput")
        self.dnsT = ap("dnsT", [D, 44], F32, kind="ExternalInput")
        self.identb = ap("identb", [128, 128], BF16, kind="ExternalInput")
        self.mask = ap("mask", [128, 128], F32, kind="ExternalInput")
        self.ropeqc = ap("ropeqc", [128, S], BF16, kind="ExternalInput")
        self.ropeqs = ap("ropeqs", [128, S], BF16, kind="ExternalInput")
        self.ropekc = ap("ropekc", [64, S], BF16, kind="ExternalInput")
        self.ropeks = ap("ropeks", [64, S], BF16, kind="ExternalInput")
        self.rotq = ap("rotq", [128, 128], BF16, kind="ExternalInput")
        self.rotk = ap("rotk", [64, 64], BF16, kind="ExternalInput")
        self.y = ap("y", [SEQ_PER_CORE, S, D], F32, kind="ExternalOutput")
        self.resid = ap("resid", [SEQ_PER_CORE, S, D], F32, kind="Internal")
        self.wgu_dec = ap("wgu_dec", [GU_OC, 128, 16 * 128], BF16, kind="Internal")
        self.wdn_dec = ap("wdn_dec", [DN_OC, 128, 44 * 128], BF16, kind="Internal")
        self.av_spill = ap("av_spill", [16, 128, S], BF16, kind="Internal")
        self.qpe_spill = ap("qpe_spill", [16, 64, S], BF16, kind="Internal")
        self.t_identb = b.sb("g_identb", [128, 128], BF16)
        self.t_mask = b.sb("g_mask", [128, 128], F32)
        self.t_eps5 = b.sb("g_eps5", [128, 1], F32)
        self.t_eps6 = b.sb("g_eps6", [128, 1], F32)
        self.t_ones = b.sb("g_ones", [1, 128], F32)
        self.t_scb = b.sb("g_scb", [1, 128], F32)
        self.t_trash = b.sb("g_trash", [128, 2048], BF16)
        self.t_rs = b.sb("g_rs", [128, 16], F32)
        self.t_ssq = b.sb("g_ssq", [128, 16], F32)
        self.t_mu = b.sb("g_mu", [128, 16], F32)
        self.t_var = b.sb("g_var", [128, 16], F32)
        self.t_std = b.sb("g_std", [128, 16], F32)
        self.t_rstd = b.sb("g_rstd", [128, 16], F32)
        self.t_nmr = b.sb("g_nmr", [128, 16], F32)
        self.t_onesb = b.sb("g_onesb", [128, 1], BF16)
        self.t_one1 = b.sb("g_one1", [128, 1], F32)
        self.wc_done = {}
        self.wc = {}

    def setup(self):
        b = self.b
        b.dma("sync", self.t_identb[:], self.identb[:])
        b.dma("sync", self.t_mask[:], self.mask[:])
        b.op("vector", lambda e: e.memset(self.t_eps5[:], EPS_LN), inc=b.s_dve)
        b.op("vector", lambda e: e.memset(self.t_eps6[:], EPS_B), inc=b.s_dve)
        b.op("vector", lambda e: e.memset(self.t_ones[:], 1.0), inc=b.s_dve)
        b.op("vector", lambda e: e.memset(self.t_scb[:], SC_B), inc=b.s_dve)
        b.op("vector", lambda e: e.memset(self.t_onesb[:], 1.0), inc=b.s_dve)
        b.op("vector", lambda e: e.memset(self.t_one1[:], 1.0), inc=b.s_dve)
        b.barrier()

    WSRC = None

    def load_w(self, name, oc, dst, sem, waits=()):
        srcap = self.wsrc[name]
        w = srcap.shape[2]
        v = self.b.dma("gpsimd", dst[:, 0:w], srcap[oc, :, :], waits=waits, sem=sem)
        return (sem, v)

    def decode_ffn_gen(self):
        b, nc = self.b, self.nc
        from concourse.ap import AP
        with ExitStack() as st:
            words = [b.sb(f"dc_w{i}", [128, 1024], I32, st) for i in range(2)]
            stile = [b.sb(f"dc_s{i}", [128, 64], F32, st) for i in range(2)]
            stil2 = [b.sb(f"dc_s2{i}", [128, 64], F32, st) for i in range(2)]
            t_m6 = b.sb("dc_m6", [128, 1024], I32, st)
            t_s1 = b.sb("dc_sa", [128, 1024], I32, st)
            t_bits = b.sb("dc_bt", [128, 1024], I32, st)
            t_cv = b.sb("dc_cv", [128, 2048], F32, st)
            t_cv2 = [b.sb(f"dc_c2{i}", [128, 2048], F32, st) for i in range(2)]
            t_i15 = b.sb("dc_i15", [128, 1], I32, st)
            b.op("vector", lambda e: e.memset(t_i15[:], 15), inc=b.s_dve)
            dsc = [b.sb(f"dc_dsc{i}", [128, 8192], BF16, st) for i in range(2)]
            stage = [b.sb(f"dc_st{i}", [128, 44 * 128], BF16, st) for i in range(2)]
            pst = [b.ps("dc_ps0", [128, 1024], BF16, st)]

            jobs = [(self.gup, self.gusT, self.wgu_dec, GU_OC, 256, 16, 4),
                    (self.dnp, self.dnsT, self.wdn_dec, DN_OC, 704, 44, 1)]
            slot_free = {0: None, 1: None}
            stage_free = {0: None, 1: None}
            pst_free = {0: None}
            cv2_free = {0: None, 1: None}
            dsc_free = {0: None, 1: None}
            it = 0
            tt = 0
            gpp = 0
            for packed, scales, outd, n_ot, wpr, nkc, G in jobs:
                ng2 = 2 * wpr // 128
                WW = G * wpr
                scw = G * nkc
                for gi in range(n_ot // G):
                    sl = it % 2
                    dscb = dsc[sl]
                    s_w = b.ssem(f"dcw{sl}")
                    w_waits = [slot_free[sl]] if slot_free[sl] else []
                    if G == 1:
                        src_w = packed[gi * 128:(gi + 1) * 128, :]
                        src_s = scales[gi * 128:(gi + 1) * 128, :]
                        dst_w = words[sl][:, 0:wpr]
                        dst_s = stile[sl][:, 0:nkc]
                    else:
                        src_w = packed[gi * G * 128:(gi + 1) * G * 128, :].rearrange(
                            "(g p) w -> p g w", p=128)
                        src_s = scales[gi * G * 128:(gi + 1) * G * 128, :].rearrange(
                            "(g p) c -> p g c", p=128)
                        dst_w = words[sl][:, 0:WW].rearrange("p (g w) -> p g w", g=G)
                        dst_s = stile[sl][:, 0:scw].rearrange("p (g c) -> p g c", g=G)
                    b.dma("sync", dst_w, src_w, waits=w_waits, sem=s_w)
                    vw = b.dma("sync", dst_s, src_s, waits=[], sem=s_w)
                    pmv = b.op("vector", lambda e, sl=sl, scw=scw: e.tensor_scalar(
                        stil2[sl][:, 0:scw], stile[sl][:, 0:scw], float(2.0 ** 63), None,
                        ALU.mult), waits=[(s_w, vw)], inc=b.s_dve)
                    last_act = None
                    for pp in range(4):
                        bb = gpp % 2
                        gpp += 1
                        net = 4 * pp - 6
                        shs = (ALU.logical_shift_right if net > 0 else ALU.logical_shift_left)
                        b.op("vector", lambda e, sl=sl, WW=WW, net=net, shs=shs: e.tensor_scalar(
                            t_m6[:, 0:WW], words[sl][:, 0:WW], abs(net), 0x01C001C0,
                            shs, ALU.bitwise_and),
                            waits=([(s_w, vw)] if pp == 0 else []), inc=b.s_dve)
                        b.op("vector", lambda e, sl=sl, WW=WW, pp=pp: e.tensor_scalar(
                            t_s1[:, 0:WW], words[sl][:, 0:WW], 4 * pp + 3, 0x00010001,
                            ALU.logical_shift_right, ALU.bitwise_and), inc=b.s_dve)
                        orv = b.op("vector", lambda e, WW=WW: e.scalar_tensor_tensor(
                            t_bits[:, 0:WW], t_s1[:, 0:WW], t_i15[:, 0:1],
                            t_m6[:, 0:WW], ALU.logical_shift_left, ALU.bitwise_or),
                            inc=b.s_dve)
                        cvv = b.op("vector", lambda e, WW=WW: e.tensor_copy(
                            t_cv[:, 0:2 * WW], t_bits[:, 0:WW].bitcast(BF16)),
                            waits=[(b.s_dve, orv)], inc=b.s_dve)
                        sap = stil2[sl][:, pp * ng2: pp * ng2 + 1]
                        pat = ([[64, 128]] + ([[nkc, G]] if G > 1 else [])
                               + [[1, ng2], [0, 128]])
                        rep = AP(sap.tensor, sap.offset, pat)
                        if G > 1:
                            ov = t_cv2[bb][:, 0:2 * WW].rearrange(
                                "p (g q r) -> p g q r", g=G, q=ng2)
                            iv = t_cv[:, 0:2 * WW].rearrange(
                                "p (g q r) -> p g q r", g=G, q=ng2)
                        else:
                            ov = t_cv2[bb][:, 0:2 * WW].rearrange("p (q r) -> p q r", q=ng2)
                            iv = t_cv[:, 0:2 * WW].rearrange("p (q r) -> p q r", q=ng2)
                        m1v = b.op("vector", lambda e, ov=ov, iv=iv, rep=rep: e.tensor_tensor(
                            ov, iv, rep, ALU.mult),
                            waits=[(b.s_dve, pmv)]
                                  + ([cv2_free[bb]] if cv2_free[bb] else []),
                            inc=b.s_dve)
                        fw = []
                        if pp == 0 and dsc_free[sl] is not None:
                            fw.append(dsc_free[sl])
                        last_act = b.op("scalar", lambda e, bb=bb, WW=WW, pp=pp, dscb=dscb: e.activation(
                            dscb[:, pp * 2 * WW:(pp + 1) * 2 * WW], t_cv2[bb][:, 0:2 * WW],
                            AF.Identity, scale=float(2.0 ** 63)),
                            waits=[(b.s_dve, m1v)] + fw, inc=b.s_act)
                        cv2_free[bb] = (b.s_act, last_act)
                        yield
                    slot_free[sl] = (b.s_act, last_act)
                    lastt = None
                    for g in range(G):
                        ssl = tt % 2
                        st_waits = [stage_free[ssl]] if stage_free[ssl] else []
                        ev_last = None
                        for kg in range(0, nkc, 4):
                            pe_waits = [(b.s_act, last_act)]
                            if pst_free[0]:
                                pe_waits.append(pst_free[0])
                            for kk in range(kg, min(kg + 4, nkc)):
                                srcc = (kk // ng2) * 2 * WW + g * 2 * wpr + (kk % ng2) * 128
                                lastt = b.op("tensor", lambda e, kk=kk, kg=kg, srcc=srcc, dscb=dscb: e.transpose(
                                    pst[0][:, (kk - kg) * 128:(kk - kg + 1) * 128],
                                    dscb[:, srcc:srcc + 128], self.t_identb[:]),
                                    waits=pe_waits, inc=b.s_pe)
                                pe_waits = []
                            nb = min(4, nkc - kg)
                            ev_last = b.op("vector", lambda e, kg=kg, nb=nb, ssl=ssl: e.tensor_copy(
                                stage[ssl][:, kg * 128:(kg + nb) * 128], pst[0][:, 0:nb * 128]),
                                waits=[(b.s_pe, lastt)] + st_waits, inc=b.s_dve)
                            st_waits = []
                            pst_free[0] = (b.s_dve, ev_last)
                        s_st = b.ssem(f"dcs{ssl}")
                        dv = b.dma("sync", outd[gi * G + g, :, :], stage[ssl][:, 0:nkc * 128],
                                   waits=[(b.s_dve, ev_last)], sem=s_st)
                        stage_free[ssl] = (s_st, dv)
                        tt += 1
                        yield
                    dsc_free[sl] = (b.s_pe, lastt)
                    it += 1

    def decode_ffn(self):
        for _ in self.decode_ffn_gen():
            pass
        self.b.barrier()

    def ln_pass(self, src_tile_fn, ntiles, xn_sink, rx_keep=None):
        raise NotImplementedError

    def emit_ln(self, src_fn, tiles, rx, rx_sems, rx_free, xn_of, xn_free, eps_tile,
                col0=0):
        b = self.b
        n = len(tiles)
        c0, c1 = col0, col0 + n
        ld = []
        for i, tj in enumerate(tiles):
            wl = [rx_free[i]] if rx_free and rx_free[i] is not None else []
            v = b.dma("sync", rx[i][:], src_fn(tj), waits=wl, sem=rx_sems[i])
            ld.append((rx_sems[i], v))
        for i in range(n):
            b.op("vector", lambda e, i=i, c0=c0: e.reduce_sum(
                self.t_rs[:, c0 + i:c0 + i + 1], rx[i][:], axis=AX.X),
                waits=[ld[i]], inc=b.s_dve)
            b.op("scalar", lambda e, i=i, c0=c0: e.activation(
                self.t_trash[:], rx[i][:], AF.Square, accum_out=self.t_ssq[:, c0 + i:c0 + i + 1]),
                waits=[ld[i]], inc=b.s_act)
        dvev = b.s_dve.n
        actv = b.s_act.n
        b.op("gpsimd", lambda e, c0=c0, c1=c1: e.tensor_scalar(
            self.t_mu[:, c0:c1], self.t_rs[:, c0:c1], 1.0 / D, None, ALU.mult),
            waits=[(b.s_dve, dvev), (b.s_act, actv)], inc=b.s_pool)
        b.op("gpsimd", lambda e, c0=c0, c1=c1: e.tensor_scalar(
            self.t_var[:, c0:c1], self.t_ssq[:, c0:c1], 1.0 / D, None, ALU.mult), inc=b.s_pool)
        b.op("gpsimd", lambda e, c0=c0, c1=c1: e.tensor_mul(
            self.t_std[:, c0:c1], self.t_mu[:, c0:c1], self.t_mu[:, c0:c1]), inc=b.s_pool)
        pv = b.op("gpsimd", lambda e, c0=c0, c1=c1: e.tensor_sub(
            self.t_var[:, c0:c1], self.t_var[:, c0:c1], self.t_std[:, c0:c1]), inc=b.s_pool)
        sv = b.op("scalar", lambda e, c0=c0, c1=c1, eps=eps_tile: e.activation(
            self.t_std[:, c0:c1], self.t_var[:, c0:c1], AF.Sqrt, bias=eps[:], scale=1.0),
            waits=[(b.s_pool, pv)], inc=b.s_act)
        rv = b.op("vector", lambda e, c0=c0, c1=c1: e.reciprocal(
            self.t_rstd[:, c0:c1], self.t_std[:, c0:c1]), waits=[(b.s_act, sv)], inc=b.s_dve)
        b.op("gpsimd", lambda e, c0=c0, c1=c1: e.tensor_mul(
            self.t_nmr[:, c0:c1], self.t_mu[:, c0:c1], self.t_rstd[:, c0:c1]),
            waits=[(b.s_dve, rv)], inc=b.s_pool)
        nv = b.op("gpsimd", lambda e, c0=c0, c1=c1: e.tensor_scalar(
            self.t_nmr[:, c0:c1], self.t_nmr[:, c0:c1], -1.0, None, ALU.mult), inc=b.s_pool)
        xn_vals = []
        for i in range(n):
            wl = [(b.s_pool, nv), (b.s_dve, rv)]
            if xn_free and xn_free[i] is not None:
                wl.append(xn_free[i])
            xn_vals.append(b.op("scalar", lambda e, i=i, c0=c0: e.activation(
                xn_of(i), rx[i][:], AF.Identity, bias=self.t_nmr[:, c0 + i:c0 + i + 1],
                scale=self.t_rstd[:, c0 + i:c0 + i + 1]), waits=wl, inc=b.s_act))
        return ld, xn_vals

    def emit_y_add(self, yT, yT_ready, rx_tile, rx_ld, tt_col, dst_ap, slab, slab_free,
                   rms_col=None):
        b = self.b
        lastt = None
        wl = [yT_ready]
        if slab_free is not None:
            wl.append(slab_free)
        for oc in range(16):
            lastt = b.op("tensor", lambda e, oc=oc: e.transpose(
                slab[:, oc * 128:(oc + 1) * 128],
                yT[:, oc * 512 + tt_col * 128: oc * 512 + (tt_col + 1) * 128],
                self.t_identb[:]), waits=wl, inc=b.s_pe)
            wl = []
        if rms_col is not None:
            i = rms_col
            sqv = b.op("scalar", lambda e, i=i: e.activation(self.t_trash[:], slab[:],
                 AF.Square, accum_out=self.t_ssq[:, i:i + 1]),
                 waits=[(b.s_pe, lastt)], inc=b.s_act)
            sv = b.op("scalar", lambda e, i=i: e.activation(self.t_std[:, i:i + 1],
                 self.t_ssq[:, i:i + 1], AF.Sqrt, bias=self.t_eps6[:], scale=1.0 / D),
                 inc=b.s_act)
            rv = b.op("vector", lambda e, i=i: e.reciprocal(self.t_rstd[:, i:i + 1],
                 self.t_std[:, i:i + 1]), waits=[(b.s_act, sv)], inc=b.s_dve)
            scal = self.t_rstd[:, i:i + 1]
            addw = [(b.s_dve, rv), rx_ld]
        else:
            scal = 1.0
            addw = [(b.s_pe, lastt), rx_ld]
        av = b.op("vector", lambda e, scal=scal: e.scalar_tensor_tensor(
            rx_tile[:], slab[:], scal, rx_tile[:], ALU.mult, ALU.add),
            waits=addw, inc=b.s_dve)
        stv = b.dma("sync", dst_ap, rx_tile[:], waits=[(b.s_dve, av)], sem=rx_ld[0])
        return (rx_ld[0], stv), (b.s_dve, av)

    def ffn(self, seq, src, dst):
        b = self.b
        src_fn = lambda t: src[seq, t * 128:(t + 1) * 128, :]
        dst_fn = lambda t: dst[seq, t * 128:(t + 1) * 128, :]
        with ExitStack() as st:
            rx = [b.sb(f"f_rx{i}", [128, D], F32, st) for i in range(4)]
            xt = [b.sb(f"f_xt{i}", [128, D], BF16, st) for i in range(4)]
            xnT = b.sb("f_xnT", [128, 16 * 512], BF16, st)
            hT = b.sb("f_hT", [128, 44 * 512], BF16, st)
            yT = b.sb("f_yT", [128, 16 * 512], BF16, st)
            sil = [b.sb(f"f_sil{i}", [128, 512], BF16, st) for i in range(2)]
            wg = [b.sb(f"f_wg{i}", [128, 16 * 128], BF16, st) for i in range(2)]
            wu = [b.sb(f"f_wu{i}", [128, 16 * 128], BF16, st) for i in range(2)]
            wd = [b.sb(f"f_wd{i}", [128, 44 * 128], BF16, st) for i in range(2)]
            psm = [b.ps(f"f_ps{i}", [128, 512], F32, st) for i in range(4)]
            psbt = [b.ps(f"f_pt{i}", [128, 1024], BF16, st) for i in range(2)]
            slab = [b.ps("f_sl0", [128, 2048], BF16, st)]

            rx_free = [None] * 4
            xn_free = [None] * 4
            wg_free = [None, None]
            wu_free = [None, None]
            wd_free = [None, None]
            psm_free = [None] * 4
            psbt_free = [None, None]
            slab_free = [None]
            sil_free = [None, None]

            rx_sems = [b.ssem(f"frx{i}") for i in range(4)]
            for tb in range(4):
                tiles = [tb * 4 + i for i in range(4)]
                ld, xnv = self.emit_ln(src_fn, tiles, rx, rx_sems, rx_free,
                                       lambda i: xt[i][:], xn_free, self.t_eps5)
                xnT_last = None
                for kc in range(16):
                    psl = kc % 2
                    for i in range(4):
                        wl = [(b.s_act, xnv[i])]
                        if i == 0 and psbt_free[psl] is not None:
                            wl.append(psbt_free[psl])
                        lastt = b.op("tensor", lambda e, kc=kc, i=i, psl=psl: e.transpose(
                            psbt[psl][:, i * 128:(i + 1) * 128],
                            xt[i][:, kc * 128:(kc + 1) * 128], self.t_identb[:]),
                            waits=wl, inc=b.s_pe)
                    xnT_last = b.op("vector", lambda e, kc=kc, psl=psl: e.tensor_copy(
                        xnT[:, kc * 512:(kc + 1) * 512], psbt[psl][:, 0:512]),
                        waits=[(b.s_pe, lastt)], inc=b.s_dve)
                    psbt_free[psl] = (b.s_dve, xnT_last)
                for i in range(4):
                    xn_free[i] = (b.s_dve, xnT_last)

                for pc in range(44):
                    sl = pc % 2
                    s_g = b.ssem(f"fwg{sl}")
                    s_u = b.ssem(f"fwu{sl}")
                    wl = [wg_free[sl]] if wg_free[sl] is not None else []
                    gv = b.dma("sync", wg[sl][:], self.wgu_dec[pc, :, :],
                               waits=wl, sem=s_g)
                    wl = [wu_free[sl]] if wu_free[sl] is not None else []
                    uv = b.dma("sync", wu[sl][:], self.wgu_dec[pc + 44, :, :],
                               waits=wl, sem=s_u)
                    pa = (2 * pc) % 4
                    pb = (2 * pc + 1) % 4
                    wlA = [(s_g, gv), (b.s_dve, xnT_last)]
                    if psm_free[pa] is not None:
                        wlA.append(psm_free[pa])
                    lastA = None
                    for kc in range(16):
                        lastA = b.op("tensor", lambda e, kc=kc, sl=sl, pa=pa: e.matmul(
                            psm[pa][:], wg[sl][:, kc * 128:(kc + 1) * 128],
                            xnT[:, kc * 512:(kc + 1) * 512], start=(kc == 0), stop=(kc == 15)),
                            waits=wlA, inc=(b.s_pe if kc == 15 else None))
                        wlA = []
                    wg_free[sl] = (b.s_pe, lastA)
                    wlB = [(s_u, uv)]
                    if psm_free[pb] is not None:
                        wlB.append(psm_free[pb])
                    lastB = None
                    for kc in range(16):
                        lastB = b.op("tensor", lambda e, kc=kc, sl=sl, pb=pb: e.matmul(
                            psm[pb][:], wu[sl][:, kc * 128:(kc + 1) * 128],
                            xnT[:, kc * 512:(kc + 1) * 512], start=(kc == 0), stop=(kc == 15)),
                            waits=wlB, inc=(b.s_pe if kc == 15 else None))
                        wlB = []
                    wu_free[sl] = (b.s_pe, lastB)
                    ssl = pc % 2
                    wl = [(b.s_pe, lastA)]
                    if sil_free[ssl] is not None:
                        wl.append(sil_free[ssl])
                    sv = b.op("scalar", lambda e, ssl=ssl, pa=pa: e.activation(
                        sil[ssl][:], psm[pa][:], AF.Silu), waits=wl, inc=b.s_act)
                    psm_free[pa] = (b.s_act, sv)
                    hv = b.op("vector", lambda e, pc=pc, pb=pb, ssl=ssl: e.scalar_tensor_tensor(
                        hT[:, pc * 512:(pc + 1) * 512], psm[pb][:], 1.0, sil[ssl][:],
                        ALU.mult, ALU.mult), waits=[(b.s_pe, lastB), (b.s_act, sv)], inc=b.s_dve)
                    psm_free[pb] = (b.s_dve, hv)
                    sil_free[ssl] = (b.s_dve, hv)
                h_done = b.s_dve.n

                for oc in range(16):
                    sl = oc % 2
                    s_d = b.ssem(f"fwd{sl}")
                    wl = [wd_free[sl]] if wd_free[sl] is not None else []
                    dv = b.dma("sync", wd[sl][:], self.wdn_dec[oc, :, :],
                               waits=wl, sem=s_d)
                    pp = oc % 4
                    wl = [(s_d, dv), (b.s_dve, h_done)]
                    if psm_free[pp] is not None:
                        wl.append(psm_free[pp])
                    last = None
                    for kc in range(44):
                        last = b.op("tensor", lambda e, kc=kc, sl=sl, pp=pp: e.matmul(
                            psm[pp][:], wd[sl][:, kc * 128:(kc + 1) * 128],
                            hT[:, kc * 512:(kc + 1) * 512], start=(kc == 0), stop=(kc == 43)),
                            waits=wl, inc=(b.s_pe if kc == 43 else None))
                        wl = []
                    wd_free[sl] = (b.s_pe, last)
                    ev = b.op("vector", lambda e, oc=oc, pp=pp: e.tensor_copy(
                        yT[:, oc * 512:(oc + 1) * 512], psm[pp][:]),
                        waits=[(b.s_pe, last)], inc=b.s_dve)
                    psm_free[pp] = (b.s_dve, ev)
                yT_done = b.s_dve.n

                for i in range(4):
                    stv, sfree = self.emit_y_add(yT, (b.s_dve, yT_done), rx[i], ld[i], i,
                                                 dst_fn(tiles[i]), slab[0][:],
                                                 slab_free[0], rms_col=8 + i)
                    slab_free[0] = sfree
                    rx_free[i] = stv
        b.barrier()

    def build(self):
        self.setup()
        need_ffn = any(p[0] == "ffn" for p in self.chain)
        if need_ffn:
            self.decode_ffn()
        for idx, ph in enumerate(self.chain):
            kind, seq = ph[0], ph[1]
            first = (idx == 0) or all(p[1] != seq for p in self.chain[:idx])
            last = (idx == len(self.chain) - 1) or all(p[1] != seq for p in self.chain[idx + 1:])
            src = self.x_in if first else self.resid
            dst = self.y if last else self.resid
            if kind == "ffn":
                self.ffn(seq, src, dst)
            elif kind == "attn_a":
                self.attn_a(seq, src, dst)
            elif kind == "attn_b":
                self.attn_b(seq, src, dst)
            else:
                raise ValueError(kind)
        self.b.barrier()

    def attn_a(self, seq, src, dst):
        b = self.b
        src_fn = lambda t: src[seq, t * 128:(t + 1) * 128, :]
        dst_fn = lambda t: dst[seq, t * 128:(t + 1) * 128, :]
        with ExitStack() as st:
            xn = b.sb("a_xn", [128, NT * D], BF16, st)
            rx = [b.sb(f"a_rx{i}", [128, D], F32, st) for i in range(2)]
            bo_sb = b.sb("a_bo", [128, 16], F32, st)
            stT = b.sb("a_st", [128, 2 * S], BF16, st)
            acT = b.sb("a_ac", [128, 2 * S], BF16, st)
            cbias = b.sb("a_cb", [128, 1], F32, st)
            s_aw = b.ssem("aw")
            s_awb = b.ssem("awb")
            bo_v = b.dma("sync", bo_sb[:], self.bo[:], sem=s_awb)
            bo_ld = (s_awb, bo_v)
            with ExitStack() as st1:
                wsb = b.sb("a_ws", [128, 2 * 2048], BF16, st1)
                wab = b.sb("a_wa", [128, 2 * 2048], BF16, st1)
                for oc in range(2):
                    self.load_w("ws", oc, wsb[:, oc * 2048:(oc + 1) * 2048], s_aw)
                    self.load_w("wa", oc, wab[:, oc * 2048:(oc + 1) * 2048], s_aw)
                aw_ld = (s_aw, s_aw.n)
                xnT = b.sb("a_xnT", [128, 16 * 512], BF16, st1)
                psbt = [b.ps(f"a_pt{i}", [128, 1024], BF16, st1) for i in range(2)]
                psv = [b.ps(f"a_pv{i}", [128, 512], F32, st1) for i in range(2)]
                rx_free = [None, None]
                psbt_free = [None, None]
                psv_free = [None, None]
                rx_sems = [b.ssem("arx0"), b.ssem("arx1")]
                for tb in range(4):
                    xn_vals = []
                    for gg in range(2):
                        tiles = [tb * 4 + gg * 2, tb * 4 + gg * 2 + 1]
                        ld, xv = self.emit_ln(src_fn, tiles, rx, rx_sems, rx_free,
                                              lambda i, tiles=tiles: xn[:, tiles[i] * D:(tiles[i] + 1) * D],
                                              None, self.t_eps5, col0=gg * 2)
                        xn_vals += xv
                        rx_free = [(b.s_act, xv[0]), (b.s_act, xv[1])]
                    xnT_last = None
                    for kc in range(16):
                        psl = kc % 2
                        for i in range(4):
                            t = tb * 4 + i
                            wl = [(b.s_act, xn_vals[i])]
                            if i == 0 and psbt_free[psl] is not None:
                                wl.append(psbt_free[psl])
                            lastt = b.op("tensor", lambda e, t=t, kc=kc, i=i, psl=psl: e.transpose(
                                psbt[psl][:, i * 128:(i + 1) * 128],
                                xn[:, t * D + kc * 128: t * D + (kc + 1) * 128],
                                self.t_identb[:]), waits=wl, inc=b.s_pe)
                        xnT_last = b.op("vector", lambda e, kc=kc, psl=psl: e.tensor_copy(
                            xnT[:, kc * 512:(kc + 1) * 512], psbt[psl][:, 0:512]),
                            waits=[(b.s_pe, lastt)], inc=b.s_dve)
                        psbt_free[psl] = (b.s_dve, xnT_last)
                    for wbuf, dT, sc in ((wsb, stT, 1.0 / 16), (wab, acT, None)):
                        for m in range(2):
                            pi = m % 2
                            wl = [(b.s_dve, xnT_last), aw_ld]
                            if psv_free[pi] is not None:
                                wl.append(psv_free[pi])
                            last = None
                            for kc in range(16):
                                last = b.op("tensor", lambda e, kc=kc, m=m, pi=pi, wbuf=wbuf: e.matmul(
                                    psv[pi][:], wbuf[:, m * 2048 + kc * 128: m * 2048 + (kc + 1) * 128],
                                    xnT[:, kc * 512:(kc + 1) * 512], start=(kc == 0), stop=(kc == 15)),
                                    waits=wl, inc=(b.s_pe if kc == 15 else None))
                                wl = []
                            if sc is None:
                                ev = b.op("vector", lambda e, m=m, tb=tb, pi=pi, dT=dT: e.tensor_copy(
                                    dT[:, m * S + tb * 512: m * S + (tb + 1) * 512], psv[pi][:]),
                                    waits=[(b.s_pe, last)], inc=b.s_dve)
                            else:
                                ev = b.op("vector", lambda e, m=m, tb=tb, pi=pi, dT=dT, sc=sc: e.tensor_scalar(
                                    dT[:, m * S + tb * 512: m * S + (tb + 1) * 512], psv[pi][:],
                                    sc, None, ALU.mult), waits=[(b.s_pe, last)], inc=b.s_dve)
                            psv_free[pi] = (b.s_dve, ev)
                b.barrier()
            with ExitStack() as st2:
                sqb = [b.sb(f"a_sq{i}", [128, D], BF16, st2) for i in range(2)]
                nmx = b.sb("a_nmx", [1, 16], F32, st2)
                ctmp = b.sb("a_ctmp", [1, 4], F32, st2)
                psn = b.ps("a_psn", [1, 512], F32, st2)
                psc = b.ps("a_psc", [128, 1], F32, st2)
                for sidx, TT in enumerate((stT, acT)):
                    sq_vals = []
                    sq_w = [(b.s_pe, b.s_pe.n)] if sidx > 0 else []
                    for c in range(2):
                        sq_vals.append(b.op("scalar", lambda e, c=c, sidx=sidx, TT=TT: e.activation(
                            sqb[c][:], TT[:, c * S:(c + 1) * S], AF.Square), waits=sq_w, inc=b.s_act))
                        sq_w = []
                    psn_free = None
                    for blk in range(4):
                        wl = [(b.s_act, sq_vals[1])]
                        if psn_free is not None:
                            wl.append(psn_free)
                        last = None
                        for c in range(2):
                            last = b.op("tensor", lambda e, c=c, blk=blk: e.matmul(
                                psn[0:1, :], self.t_onesb[:], sqb[c][:, blk * 512:(blk + 1) * 512],
                                start=(c == 0), stop=(c == 1)), waits=wl, inc=(b.s_pe if c == 1 else None))
                            wl = []
                        rv = b.op("vector", lambda e, sidx=sidx, blk=blk: e.reduce_max(
                            nmx[0:1, sidx * 4 + blk: sidx * 4 + blk + 1], psn[0:1, :], axis=AX.X),
                            waits=[(b.s_pe, last)], inc=b.s_dve)
                        psn_free = (b.s_dve, rv)
                for sidx in range(2):
                    b.op("vector", lambda e, sidx=sidx: e.reduce_max(
                        ctmp[0:1, sidx:sidx + 1], nmx[0:1, sidx * 4:sidx * 4 + 4], axis=AX.X),
                        inc=b.s_dve)
                dv = b.s_dve.n
                pv = b.op("gpsimd", lambda e: e.tensor_mul(ctmp[0:1, 2:3], ctmp[0:1, 0:1],
                     ctmp[0:1, 1:2]), waits=[(b.s_dve, dv)], inc=b.s_pool)
                sv = b.op("scalar", lambda e: e.activation(ctmp[0:1, 3:4], ctmp[0:1, 2:3],
                     AF.Sqrt), waits=[(b.s_pool, pv)], inc=b.s_act)
                nv = b.op("gpsimd", lambda e: e.tensor_scalar(ctmp[0:1, 2:3], ctmp[0:1, 3:4],
                     -1.0, None, ALU.mult), waits=[(b.s_act, sv)], inc=b.s_pool)
                mv = b.op("tensor", lambda e: e.matmul(psc[:, 0:1], self.t_ones[:],
                     ctmp[0:1, 2:3], start=True, stop=True), waits=[(b.s_pool, nv)], inc=b.s_pe)
                b.op("vector", lambda e: e.tensor_copy(cbias[:], psc[:, 0:1]),
                     waits=[(b.s_pe, mv)], inc=b.s_dve)
                b.barrier()
            with ExitStack() as st3:
                p4 = [b.sb(f"a_p4{i}", [128, 4 * S], BF16, st3) for i in range(2)]
                PT = b.sb("a_PT", [128, 16 * 512], BF16, st3)
                avT = b.sb("a_av", [128, 16 * 512], BF16, st3)
                yT = b.sb("a_yT", [128, 16 * 512], BF16, st3)
                wob = [b.sb(f"a_wo{i}", [128, 16 * 128], BF16, st3) for i in range(2)]
                asum = b.sb("a_as", [128, 16], F32, st3)
                atot = b.sb("a_at", [128, 4], F32, st3)
                arcp = b.sb("a_ar", [128, 4], F32, st3)
                pss = [b.ps(f"a_ss{i}", [128, 512], F32, st3) for i in range(2)]
                pst2 = [b.ps(f"a_pt2{i}", [128, 1024], BF16, st3) for i in range(2)]
                psa = [b.ps(f"a_pa{i}", [128, 512], F32, st3) for i in range(2)]
                slab = b.ps("a_slab", [128, 2048], BF16, st3)
                pss_free = [None, None]
                pst2_free = [None, None]
                psa_free = [None, None]
                wob_free = [None, None]
                slab_free = None
                rx_free = [None, None]
                sum_done = 0
                pm_of = {}
                ya_of = {}

                def a_stage1(qb):
                    nonlocal sum_done
                    pb = p4[qb % 2][:]
                    ssi = 0
                    exp_last = [0] * 4
                    for qt in range(4):
                        qtg = qb * 4 + qt
                        klen = (qtg + 1) * 128
                        nblk = (klen + 511) // 512
                        for blk in range(nblk):
                            w = min(512, klen - blk * 512)
                            pi = ssi % 2
                            ssi += 1
                            wl = [pss_free[pi]] if pss_free[pi] is not None else []
                            last = None
                            for m in range(2):
                                last = b.op("tensor", lambda e, m=m, qtg=qtg, blk=blk, w=w, pi=pi: e.matmul(
                                    pss[pi][:, 0:w], stT[:, m * S + qtg * 128: m * S + (qtg + 1) * 128],
                                    acT[:, m * S + blk * 512: m * S + blk * 512 + w],
                                    start=(m == 0), stop=(m == 1)), waits=wl,
                                    inc=(b.s_pe if m == 1 else None))
                                wl = []
                            expw = [(b.s_pe, last)]
                            if blk == nblk - 1:
                                mkv = b.op("vector", lambda e, w=w, pi=pi: e.tensor_tensor(
                                    pss[pi][:, w - 128:w], pss[pi][:, w - 128:w], self.t_mask[:],
                                    ALU.add), waits=[(b.s_pe, last)], inc=b.s_dve)
                                expw = [(b.s_dve, mkv)]
                            ev = b.op("scalar", lambda e, qt=qt, blk=blk, w=w, pi=pi, pb=pb: e.activation(
                                pb[:, qt * S + blk * 512: qt * S + blk * 512 + w], pss[pi][:, 0:w],
                                AF.Exp, bias=cbias[:], scale=1.0,
                                accum_out=asum[:, qt * 4 + blk: qt * 4 + blk + 1]),
                                waits=expw + [(b.s_dve, sum_done)], inc=b.s_act)
                            exp_last[qt] = ev
                            pss_free[pi] = (b.s_act, ev)
                    for qt in range(4):
                        qtg = qb * 4 + qt
                        nblk = ((qtg + 1) * 128 + 511) // 512
                        b.op("vector", lambda e, qt=qt, nblk=nblk: e.reduce_sum(
                            atot[:, qt:qt + 1], asum[:, qt * 4: qt * 4 + nblk], axis=AX.X),
                            waits=[(b.s_act, exp_last[qt])], inc=b.s_dve)
                    for qt in range(4):
                        b.op("vector", lambda e, qt=qt: e.reciprocal(
                            arcp[:, qt:qt + 1], atot[:, qt:qt + 1]), inc=b.s_dve)
                    pm_last = None
                    for qt in range(4):
                        qtg = qb * 4 + qt
                        klen = (qtg + 1) * 128
                        pm_last = b.op("vector", lambda e, qt=qt, klen=klen, pb=pb: e.tensor_scalar(
                            pb[:, qt * S: qt * S + klen], pb[:, qt * S: qt * S + klen],
                            arcp[:, qt:qt + 1], None, ALU.mult), inc=b.s_dve)
                    sum_done = pm_last
                    pm_of[qb] = pm_last

                def a_stage2(qb):
                    pb = p4[qb % 2][:]
                    pm_last = pm_of[qb]
                    kmax = (qb + 1) * 4
                    ptev = None
                    for kc in range(kmax):
                        psl = kc % 2
                        wl = [(b.s_dve, pm_last)]
                        if pst2_free[psl] is not None:
                            wl.append(pst2_free[psl])
                        lastt = None
                        for qt in range(4):
                            if qb * 4 + qt < kc:
                                continue
                            lastt = b.op("tensor", lambda e, qt=qt, kc=kc, psl=psl, pb=pb: e.transpose(
                                pst2[psl][:, qt * 128:(qt + 1) * 128],
                                pb[:, qt * S + kc * 128: qt * S + (kc + 1) * 128],
                                self.t_identb[:]), waits=wl, inc=b.s_pe)
                            wl = []
                        q0 = max(0, kc - qb * 4)
                        if q0 > 0:
                            b.op("vector", lambda e, kc=kc, q0=q0: e.memset(
                                PT[:, kc * 512: kc * 512 + q0 * 128], 0.0), inc=b.s_dve)
                        ptev = b.op("vector", lambda e, kc=kc, psl=psl, q0=q0: e.tensor_copy(
                            PT[:, kc * 512 + q0 * 128:(kc + 1) * 512],
                            pst2[psl][:, q0 * 128:512]),
                            waits=[(b.s_pe, lastt)], inc=b.s_dve)
                        pst2_free[psl] = (b.s_dve, ptev)
                    for vc in range(16):
                        pi = vc % 2
                        wl = [(b.s_dve, ptev)]
                        if psa_free[pi] is not None:
                            wl.append(psa_free[pi])
                        last = None
                        for kc in range(kmax):
                            last = b.op("tensor", lambda e, kc=kc, vc=vc, pi=pi,
                                        sp_=(kc == kmax - 1): e.matmul(
                                psa[pi][:], xn[:, kc * D + vc * 128: kc * D + (vc + 1) * 128],
                                PT[:, kc * 512: (kc + 1) * 512],
                                start=(kc == 0), stop=sp_), waits=wl,
                                inc=(b.s_pe if kc == kmax - 1 else None))
                            wl = []
                        av = b.op("vector", lambda e, vc=vc, pi=pi: e.tensor_copy(
                            avT[:, vc * 512:(vc + 1) * 512], psa[pi][:]),
                            waits=[(b.s_pe, last)], inc=b.s_dve)
                        psa_free[pi] = (b.s_dve, av)
                    av_done = b.s_dve.n
                    for oc in range(16):
                        sl = oc % 2
                        wl = [wob_free[sl]] if wob_free[sl] is not None else []
                        wv = self.load_w("wo", oc, wob[sl], b.ssem(f"awo{sl}"), waits=wl)
                        pi = oc % 2
                        wl = [wv, (b.s_dve, av_done)]
                        if psa_free[pi] is not None:
                            wl.append(psa_free[pi])
                        last = None
                        for hc in range(16):
                            last = b.op("tensor", lambda e, hc=hc, sl=sl, pi=pi: e.matmul(
                                psa[pi][:], wob[sl][:, hc * 128:(hc + 1) * 128],
                                avT[:, hc * 512:(hc + 1) * 512], start=(hc == 0), stop=(hc == 15)),
                                waits=wl, inc=(b.s_pe if hc == 15 else None))
                            wl = []
                        wob_free[sl] = (b.s_pe, last)
                        yv = b.op("vector", lambda e, oc=oc, pi=pi: e.tensor_scalar(
                            yT[:, oc * 512:(oc + 1) * 512], psa[pi][:], bo_sb[:, oc:oc + 1],
                            None, ALU.add), waits=[(b.s_pe, last), bo_ld], inc=b.s_dve)
                        psa_free[pi] = (b.s_dve, yv)
                    ya_of[qb] = b.s_dve.n

                def a_stage3(qb):
                    nonlocal slab_free
                    yT_done = ya_of[qb]
                    for i in range(4):
                        t = qb * 4 + i
                        ri = i % 2
                        wl = [rx_free[ri]] if rx_free[ri] is not None else []
                        rxl = b.dma("sync", rx[ri][:], src_fn(t), waits=wl, sem=rx_sems[ri])
                        stv, sfree = self.emit_y_add(yT, (b.s_dve, yT_done), rx[ri],
                                                     (rx_sems[ri], rxl), i,
                                                     dst_fn(t), slab[:], slab_free,
                                                     rms_col=None)
                        slab_free = sfree
                        rx_free[ri] = stv

                import os as _os
                if _os.environ.get("MK_A_SEQ"):
                    for _q in range(4):
                        a_stage1(_q); a_stage2(_q); a_stage3(_q)
                else:
                    a_stage1(0)
                    a_stage1(1)
                    a_stage2(0)
                    a_stage3(0)
                    a_stage1(2)
                    a_stage2(1)
                    a_stage3(1)
                    a_stage1(3)
                    a_stage2(2)
                    a_stage3(2)
                    a_stage2(3)
                    a_stage3(3)
        b.barrier()

    def attn_b(self, seq, src, dst):
        stages = int(os.environ.get("MK_B_STAGES", "4"))
        b = self.b
        src_fn = lambda t: src[seq, t * 128:(t + 1) * 128, :]
        dst_fn = lambda t: dst[seq, t * 128:(t + 1) * 128, :]
        with ExitStack() as st:
            qanT = b.sb("b_qanT", [128, 12 * S], BF16, st)
            ckvnT = b.sb("b_ckvnT", [128, 4 * S], BF16, st)
            kpeT = b.sb("b_kpeT", [64, S], BF16, st)
            with ExitStack() as st1:
                rx = [b.sb(f"b_rx{i}", [128, D], F32, st1) for i in range(2)]
                xt4 = [b.sb(f"b_xt{i}", [128, D], BF16, st1) for i in range(4)]
                xnT = b.sb("b_xnT", [128, 16 * 512], BF16, st1)
                qaw = [b.sb(f"b_qaw{i}", [128, 16 * 128], BF16, st1) for i in range(2)]
                kvaw = b.sb("b_kvaw", [128, 4 * 2048], BF16, st1)
                kvpw = b.sb("b_kvpw", [128, 16 * 64], BF16, st1)
                qalnw_sb = b.sb("b_qalnw", [128, 12], F32, st1)
                kvlnw_sb = b.sb("b_kvlnw", [128, 4], F32, st1)
                sqq = b.sb("b_sqq", [128, 12 * 512], BF16, st1)
                sqk = b.sb("b_sqk", [128, 4 * 512], BF16, st1)
                qaraw = b.sb("b_qaraw", [128, 12 * 512], BF16, st1)
                kvraw = b.sb("b_kvraw", [128, 4 * 512], BF16, st1)
                rrow = b.sb("b_rrow", [1, 512], F32, st1)
                rkc = b.sb("b_rkc", [64, S], BF16, st1)
                rks = b.sb("b_rks", [64, S], BF16, st1)
                rotk_sb = b.sb("b_rotk", [64, 64], BF16, st1)
                t1b = b.sb("b_t1", [128, 512], BF16, st1)
                t2b = b.sb("b_t2", [128, 512], BF16, st1)
                psq = [b.ps(f"b_pq{i}", [128, 512], F32, st1) for i in range(2)]
                psbt = [b.ps(f"b_pt{i}", [128, 1024], BF16, st1) for i in range(2)]
                psr = b.ps("b_psr", [1, 512], F32, st1)
                psrep = b.ps("b_psrep", [128, 512], F32, st1)
                psro = b.ps("b_psro", [64, 512], F32, st1)
                s_kw = b.ssem("bkw")
                s_kt = b.ssem("bkt")
                self.load_w("kva", 0, kvaw[:, 0:2048], s_kw)
                self.load_w("kva", 1, kvaw[:, 2048:4096], s_kw)
                self.load_w("kva", 2, kvaw[:, 4096:6144], s_kw)
                self.load_w("kva", 3, kvaw[:, 6144:8192], s_kw)
                self.load_w("kvape", 0, kvpw, s_kw)
                b.dma("sync", qalnw_sb[:], self.qalnw[:], sem=s_kt)
                b.dma("sync", kvlnw_sb[:], self.kvlnw[:], sem=s_kt)
                b.dma("sync", rkc[:], self.ropekc[:], sem=s_kt)
                b.dma("sync", rks[:], self.ropeks[:], sem=s_kt)
                b.dma("sync", rotk_sb[:], self.rotk[:], sem=s_kt)
                kv_ld = (s_kw, s_kw.n)
                kt_ld = (s_kt, s_kt.n)
                rx_sems = [b.ssem("brx0"), b.ssem("brx1")]
                rx_free = [None, None]
                psbt_free = [None, None]
                psq_free = [None, None]
                qaw_free = [None, None]
                s_qw = [b.ssem("bqw0"), b.ssem("bqw1")]
                for stt_ in range(4):
                    xn_vals = []
                    for gg in range(2):
                        tiles = [stt_ * 4 + gg * 2, stt_ * 4 + gg * 2 + 1]
                        ld, xv = self.emit_ln(src_fn, tiles, rx, rx_sems, rx_free,
                                              lambda i, gg=gg: xt4[gg * 2 + i][:],
                                              None, self.t_eps5, col0=gg * 2)
                        xn_vals += xv
                        rx_free = [(b.s_act, xv[0]), (b.s_act, xv[1])]
                    xnT_last = None
                    for kc in range(16):
                        psl = kc % 2
                        for i in range(4):
                            wl = [(b.s_act, xn_vals[i])]
                            if i == 0 and psbt_free[psl] is not None:
                                wl.append(psbt_free[psl])
                            lastt = b.op("tensor", lambda e, kc=kc, i=i, psl=psl: e.transpose(
                                psbt[psl][:, i * 128:(i + 1) * 128],
                                xt4[i][:, kc * 128:(kc + 1) * 128],
                                self.t_identb[:]), waits=wl, inc=b.s_pe)
                        xnT_last = b.op("vector", lambda e, kc=kc, psl=psl: e.tensor_copy(
                            xnT[:, kc * 512:(kc + 1) * 512], psbt[psl][:, 0:512]),
                            waits=[(b.s_pe, lastt)], inc=b.s_dve)
                        psbt_free[psl] = (b.s_dve, xnT_last)
                    for m in range(12):
                        sl = m % 2
                        wlw = qaw_free[sl] or []
                        wv = self.load_w("qa", m, qaw[sl], s_qw[sl], waits=wlw)
                        pi = m % 2
                        wl = [wv, (b.s_dve, xnT_last)]
                        wl += psq_free[pi] or []
                        last = None
                        for kc in range(16):
                            last = b.op("tensor", lambda e, kc=kc, sl=sl, pi=pi: e.matmul(
                                psq[pi][:], qaw[sl][:, kc * 128:(kc + 1) * 128],
                                xnT[:, kc * 512:(kc + 1) * 512], start=(kc == 0), stop=(kc == 15)),
                                waits=wl, inc=(b.s_pe if kc == 15 else None))
                            wl = []
                        qaw_free[sl] = [(b.s_pe, last)]
                        ev = b.op("vector", lambda e, m=m, pi=pi: e.tensor_copy(
                            qaraw[:, m * 512:(m + 1) * 512], psq[pi][:]),
                            waits=[(b.s_pe, last)], inc=b.s_dve)
                        b.op("scalar", lambda e, m=m: e.activation(
                            sqq[:, m * 512:(m + 1) * 512], qaraw[:, m * 512:(m + 1) * 512],
                            AF.Square), waits=[(b.s_dve, ev)], inc=b.s_act)
                        psq_free[pi] = [(b.s_dve, ev)]
                    for m in range(4):
                        pi = m % 2
                        wl = [kv_ld, kt_ld, (b.s_dve, xnT_last)]
                        wl += psq_free[pi] or []
                        last = None
                        for kc in range(16):
                            last = b.op("tensor", lambda e, kc=kc, m=m, pi=pi: e.matmul(
                                psq[pi][:], kvaw[:, m * 2048 + kc * 128: m * 2048 + (kc + 1) * 128],
                                xnT[:, kc * 512:(kc + 1) * 512], start=(kc == 0), stop=(kc == 15)),
                                waits=wl, inc=(b.s_pe if kc == 15 else None))
                            wl = []
                        ev = b.op("vector", lambda e, m=m, pi=pi: e.tensor_copy(
                            kvraw[:, m * 512:(m + 1) * 512], psq[pi][:]),
                            waits=[(b.s_pe, last)], inc=b.s_dve)
                        b.op("scalar", lambda e, m=m: e.activation(
                            sqk[:, m * 512:(m + 1) * 512], kvraw[:, m * 512:(m + 1) * 512],
                            AF.Square), waits=[(b.s_dve, ev)], inc=b.s_act)
                        psq_free[pi] = [(b.s_dve, ev)]
                    wl = list(psq_free[0] or [])
                    last = None
                    for kc in range(16):
                        last = b.op("tensor", lambda e, kc=kc: e.matmul(
                            psq[0][0:64, :], kvpw[:, kc * 64:(kc + 1) * 64],
                            xnT[:, kc * 512:(kc + 1) * 512], start=(kc == 0), stop=(kc == 15)),
                            waits=wl, inc=(b.s_pe if kc == 15 else None))
                        wl = []
                    kpv = b.op("vector", lambda e, stt_=stt_: e.tensor_copy(
                        kpeT[0:64, stt_ * 512:(stt_ + 1) * 512], psq[0][0:64, :]),
                        waits=[(b.s_pe, last)], inc=b.s_dve)
                    psq_free[0] = [(b.s_dve, kpv)]
                    sq_done = (b.s_act, b.s_act.n)
                    last = None
                    wl = [sq_done, (b.s_dve, b.s_dve.n)]
                    for m in range(12):
                        last = b.op("tensor", lambda e, m=m: e.matmul(
                            psr[0:1, :], self.t_onesb[:], sqq[:, m * 512:(m + 1) * 512],
                            start=(m == 0), stop=(m == 11)), waits=wl,
                            inc=(b.s_pe if m == 11 else None))
                        wl = []
                    s1 = b.op("scalar", lambda e: e.activation(rrow[0:1, :], psr[0:1, :],
                         AF.Sqrt, bias=self.t_eps6[0:1, :], scale=1.0 / 1536),
                         waits=[(b.s_pe, last)], inc=b.s_act)
                    r1 = b.op("vector", lambda e: e.reciprocal(rrow[0:1, :], rrow[0:1, :]),
                         waits=[(b.s_act, s1)], inc=b.s_dve)
                    m1 = b.op("tensor", lambda e: e.matmul(psrep[:], self.t_scb[:],
                         rrow[0:1, :], start=True, stop=True), waits=[(b.s_dve, r1)], inc=b.s_pe)
                    ap_last = None
                    for m in range(12):
                        ap_last = b.op("vector", lambda e, m=m, stt_=stt_: e.scalar_tensor_tensor(
                            qanT[:, m * S + stt_ * 512: m * S + (stt_ + 1) * 512],
                            qaraw[:, m * 512:(m + 1) * 512], qalnw_sb[:, m:m + 1], psrep[:],
                            ALU.mult, ALU.mult), waits=[(b.s_pe, m1)], inc=b.s_dve)
                    last = None
                    wl = [(b.s_dve, ap_last)]
                    for m in range(4):
                        last = b.op("tensor", lambda e, m=m: e.matmul(
                            psr[0:1, :], self.t_onesb[:], sqk[:, m * 512:(m + 1) * 512],
                            start=(m == 0), stop=(m == 3)), waits=wl,
                            inc=(b.s_pe if m == 3 else None))
                        wl = []
                    s2 = b.op("scalar", lambda e: e.activation(rrow[0:1, :], psr[0:1, :],
                         AF.Sqrt, bias=self.t_eps6[0:1, :], scale=1.0 / 512),
                         waits=[(b.s_pe, last)], inc=b.s_act)
                    r2 = b.op("vector", lambda e: e.reciprocal(rrow[0:1, :], rrow[0:1, :]),
                         waits=[(b.s_act, s2)], inc=b.s_dve)
                    m2 = b.op("tensor", lambda e: e.matmul(psrep[:], self.t_ones[:],
                         rrow[0:1, :], start=True, stop=True), waits=[(b.s_dve, r2)], inc=b.s_pe)
                    for m in range(4):
                        b.op("vector", lambda e, m=m, stt_=stt_: e.scalar_tensor_tensor(
                            ckvnT[:, m * S + stt_ * 512: m * S + (stt_ + 1) * 512],
                            kvraw[:, m * 512:(m + 1) * 512], kvlnw_sb[:, m:m + 1], psrep[:],
                            ALU.mult, ALU.mult), waits=[(b.s_pe, m2)], inc=b.s_dve)
                kpe_done = (b.s_dve, b.s_dve.n)
                psro_free = None
                for blk in range(4):
                    wl = [kpe_done, kt_ld]
                    if psro_free is not None:
                        wl.append(psro_free)
                    rmv = b.op("tensor", lambda e, blk=blk: e.matmul(
                        psro[:], rotk_sb[:], kpeT[0:64, blk * 512:(blk + 1) * 512],
                        start=True, stop=True), waits=wl, inc=b.s_pe)
                    tv1 = b.op("vector", lambda e, blk=blk: e.tensor_tensor(
                        t1b[0:64, :], kpeT[0:64, blk * 512:(blk + 1) * 512],
                        rkc[0:64, blk * 512:(blk + 1) * 512], ALU.mult), inc=b.s_dve)
                    tv2 = b.op("vector", lambda e, blk=blk: e.tensor_tensor(
                        t2b[0:64, :], psro[:],
                        rks[0:64, blk * 512:(blk + 1) * 512], ALU.mult),
                        waits=[(b.s_pe, rmv)], inc=b.s_dve)
                    fv = b.op("vector", lambda e, blk=blk: e.tensor_tensor(
                        kpeT[0:64, blk * 512:(blk + 1) * 512], t1b[0:64, :], t2b[0:64, :],
                        ALU.add), inc=b.s_dve)
                    psro_free = (b.s_dve, fv)
                b.barrier()
            if stages < 2:
                return
            with ExitStack() as st2:
                qstage = b.sb("b_qst", [128, S], BF16, st2)
                qpw = [b.sb(f"b_qpw{i}", [128, 12 * 128], BF16, st2) for i in range(2)]
                rqc = b.sb("b_rqc", [128, S], BF16, st2)
                rqs = b.sb("b_rqs", [128, S], BF16, st2)
                rotq_sb = b.sb("b_rotq", [128, 128], BF16, st2)
                t1q = b.sb("b_t1q", [128, 512], BF16, st2)
                t2q = b.sb("b_t2q", [128, 512], BF16, st2)
                psq2 = [b.ps(f"b2_pq{i}", [128, 512], F32, st2) for i in range(2)]
                psb = [b.ps(f"b2_pb{i}", [128, 512], F32, st2) for i in range(2)]
                s_t = b.ssem("bqt")
                b.dma("sync", rqc[:], self.ropeqc[:], sem=s_t)
                b.dma("sync", rqs[:], self.ropeqs[:], sem=s_t)
                b.dma("sync", rotq_sb[:], self.rotq[:], sem=s_t)
                tab_ld = (s_t, s_t.n)
                s_qp = [b.ssem("bqp0"), b.ssem("bqp1")]
                qpw_free = [None, None]
                psq_free = [None, None]
                psb_free = [None, None]
                s_qpe_st = b.ssem("bqpes")
                qstage_war = None
                for j in range(8):
                    sl = j % 2
                    wlw = qpw_free[sl] or []
                    wv = self.load_w("qbp", j, qpw[sl], s_qp[sl], waits=wlw)
                    stage_vals = []
                    for tblk in range(4):
                        pi = tblk % 2
                        wl = [wv]
                        wl += psq_free[pi] or []
                        if tblk == 0 and qstage_war is not None:
                            wl.append(qstage_war)
                        last = None
                        for kc in range(12):
                            last = b.op("tensor", lambda e, kc=kc, sl=sl, tblk=tblk, pi=pi: e.matmul(
                                psq2[pi][:], qpw[sl][:, kc * 128:(kc + 1) * 128],
                                qanT[:, kc * S + tblk * 512: kc * S + (tblk + 1) * 512],
                                start=(kc == 0), stop=(kc == 11)),
                                waits=wl, inc=(b.s_pe if kc == 11 else None))
                            wl = []
                        evw = [(b.s_pe, last)]
                        if tblk == 0 and qstage_war is not None:
                            evw.append(qstage_war)
                        ev = b.op("vector", lambda e, tblk=tblk, pi=pi: e.tensor_copy(
                            qstage[:, tblk * 512:(tblk + 1) * 512], psq2[pi][:]),
                            waits=evw, inc=b.s_dve)
                        psq_free[pi] = [(b.s_dve, ev)]
                        stage_vals.append(ev)
                    qpw_free[sl] = [(b.s_pe, b.s_pe.n)]
                    rope_last = None
                    for tblk in range(4):
                        psl = tblk % 2
                        wl = [(b.s_dve, stage_vals[tblk]), tab_ld]
                        wl += psb_free[psl] or []
                        rmv = b.op("tensor", lambda e, tblk=tblk, psl=psl: e.matmul(
                            psb[psl][:], rotq_sb[:], qstage[:, tblk * 512:(tblk + 1) * 512],
                            start=True, stop=True), waits=wl, inc=b.s_pe)
                        b.op("vector", lambda e, tblk=tblk: e.tensor_tensor(
                            t1q[:], qstage[:, tblk * 512:(tblk + 1) * 512],
                            rqc[:, tblk * 512:(tblk + 1) * 512], ALU.mult), inc=b.s_dve)
                        b.op("vector", lambda e, tblk=tblk, psl=psl: e.tensor_tensor(
                            t2q[:], psb[psl][:],
                            rqs[:, tblk * 512:(tblk + 1) * 512], ALU.mult),
                            waits=[(b.s_pe, rmv)], inc=b.s_dve)
                        rope_last = b.op("vector", lambda e, tblk=tblk: e.tensor_tensor(
                            qstage[:, tblk * 512:(tblk + 1) * 512], t1q[:], t2q[:],
                            ALU.add), inc=b.s_dve)
                        psb_free[psl] = [(b.s_dve, rope_last)]
                    b.dma("sync", self.qpe_spill[2 * j, :, :], qstage[0:64, :],
                          waits=[(b.s_dve, rope_last)], sem=s_qpe_st)
                    v2 = b.dma("sync", self.qpe_spill[2 * j + 1, :, :], qstage[64:128, :],
                               sem=s_qpe_st)
                    qstage_war = (s_qpe_st, v2)
                b.barrier()
            if stages < 3:
                return
            with ExitStack() as st3:
                knT = b.sb("b_knT", [128, S], BF16, st3)
                Vb = b.sb("b_V", [128, 16 * 128], BF16, st3)
                qnT = b.sb("b_qnT", [128, S], BF16, st3)
                qpeh = [b.sb(f"b_qpeh{i}", [64, S], BF16, st3) for i in range(2)]
                sq4 = b.sb("b_sq4", [128, S], BF16, st3)
                sq4k = b.sb("b_sq4k", [128, S], BF16, st3)
                sqpe = b.sb("b_sqpe", [64, S], BF16, st3)
                sqkpe = b.sb("b_sqkpe", [64, S], BF16, st3)
                p4 = [b.sb(f"b_p4{i}", [128, 4 * S], BF16, st3) for i in range(2)]
                PT = b.sb("b_PT", [128, 16 * 512], BF16, st3)
                avb = b.sb("b_avb", [128, 512], BF16, st3)
                qbnw = [b.sb(f"b_qbnw{i}", [128, 12 * 128], BF16, st3) for i in range(2)]
                kvbnw = [b.sb(f"b_kvbnw{i}", [128, 4 * 128], BF16, st3) for i in range(2)]
                kvbvw = [b.sb(f"b_kvbvw{i}", [128, 4 * 128], BF16, st3) for i in range(2)]
                nmx = b.sb("b_nmx", [1, 16], F32, st3)
                ctmp = b.sb("b_ctmp", [1, 4], F32, st3)
                cbias = b.sb("b_cb", [128, 1], F32, st3)
                asum = b.sb("b_as", [128, 16], F32, st3)
                atot = b.sb("b_at", [128, 4], F32, st3)
                arcp = b.sb("b_ar", [128, 4], F32, st3)
                pss = [b.ps(f"b_ss{i}", [128, 512], F32, st3) for i in range(2)]
                pst2 = [b.ps(f"b_pt2{i}", [128, 1024], BF16, st3) for i in range(2)]
                psa = [b.ps(f"b_pa{i}", [128, 512], F32, st3) for i in range(2)]
                psn = b.ps("b_psn", [1, 512], F32, st3)
                psc = b.ps("b_psc", [128, 1], F32, st3)
                s_bw = [b.ssem("bbw0"), b.ssem("bbw1")]
                s_qh = [b.ssem("bqh0"), b.ssem("bqh1")]
                s_av = b.ssem("bav")
                kq_last = None
                for blk in range(4):
                    kq_last = b.op("scalar", lambda e, blk=blk: e.activation(
                        sqkpe[0:64, blk * 512:(blk + 1) * 512],
                        kpeT[0:64, blk * 512:(blk + 1) * 512], AF.Square), inc=b.s_act)
                bw_free = [None, None]
                qh_free = [None, None]
                pss_free = [None, None]
                pst2_free = [None, None]
                psa_free = [None, None]
                av_pend = None
                for h in range(16):
                    sl = h % 2
                    wlw = bw_free[sl] or []
                    wq = self.load_w("qbn", h, qbnw[sl], s_bw[sl], waits=wlw)
                    wn = self.load_w("kvbn", h, kvbnw[sl], s_bw[sl])
                    wv_ = self.load_w("kvbv", h, kvbvw[sl], s_bw[sl])
                    wload = (s_bw[sl], s_bw[sl].n)
                    wlq = qh_free[sl] or []
                    qpl = b.dma("sync", qpeh[sl][0:64, :], self.qpe_spill[h, :, :],
                                waits=wlq, sem=s_qh[sl])
                    qpe_ld = (s_qh[sl], qpl)
                    kn_vals = []
                    for blk in range(4):
                        pi = blk % 2
                        wl = [wload]
                        wl += pss_free[pi] or []
                        last = None
                        for kc in range(4):
                            last = b.op("tensor", lambda e, kc=kc, blk=blk, pi=pi, sl=sl: e.matmul(
                                pss[pi][:], kvbnw[sl][:, kc * 128:(kc + 1) * 128],
                                ckvnT[:, kc * S + blk * 512: kc * S + (blk + 1) * 512],
                                start=(kc == 0), stop=(kc == 3)),
                                waits=wl, inc=(b.s_pe if kc == 3 else None))
                            wl = []
                        ev = b.op("vector", lambda e, blk=blk, pi=pi: e.tensor_copy(
                            knT[:, blk * 512:(blk + 1) * 512], pss[pi][:]),
                            waits=[(b.s_pe, last)], inc=b.s_dve)
                        sv = b.op("scalar", lambda e, blk=blk: e.activation(
                            sq4k[:, blk * 512:(blk + 1) * 512], knT[:, blk * 512:(blk + 1) * 512],
                            AF.Square), waits=[(b.s_dve, ev)], inc=b.s_act)
                        pss_free[pi] = [(b.s_dve, ev)]
                        kn_vals.append((ev, sv))
                    v_vals = []
                    for kt in range(16):
                        pi = kt % 2
                        wl = [wload]
                        wl += psa_free[pi] or []
                        last = None
                        for kc in range(4):
                            last = b.op("tensor", lambda e, kc=kc, kt=kt, pi=pi, sl=sl: e.matmul(
                                psa[pi][:, 0:128], ckvnT[:, kc * S + kt * 128: kc * S + (kt + 1) * 128],
                                kvbvw[sl][:, kc * 128:(kc + 1) * 128],
                                start=(kc == 0), stop=(kc == 3)),
                                waits=wl, inc=(b.s_pe if kc == 3 else None))
                            wl = []
                        ev = b.op("vector", lambda e, kt=kt, pi=pi: e.tensor_copy(
                            Vb[:, kt * 128:(kt + 1) * 128], psa[pi][:, 0:128]),
                            waits=[(b.s_pe, last)], inc=b.s_dve)
                        psa_free[pi] = [(b.s_dve, ev)]
                        v_vals.append(ev)
                    v_done = b.s_dve.n
                    qn_vals = []
                    for tblk in range(4):
                        pi = tblk % 2
                        wl = [wload]
                        wl += pss_free[pi] or []
                        last = None
                        for kc in range(12):
                            last = b.op("tensor", lambda e, kc=kc, tblk=tblk, pi=pi, sl=sl: e.matmul(
                                pss[pi][:], qbnw[sl][:, kc * 128:(kc + 1) * 128],
                                qanT[:, kc * S + tblk * 512: kc * S + (tblk + 1) * 512],
                                start=(kc == 0), stop=(kc == 11)),
                                waits=wl, inc=(b.s_pe if kc == 11 else None))
                            wl = []
                        ev = b.op("vector", lambda e, tblk=tblk, pi=pi: e.tensor_copy(
                            qnT[:, tblk * 512:(tblk + 1) * 512], pss[pi][:]),
                            waits=[(b.s_pe, last)], inc=b.s_dve)
                        sv = b.op("scalar", lambda e, tblk=tblk: e.activation(
                            sq4[:, tblk * 512:(tblk + 1) * 512], qnT[:, tblk * 512:(tblk + 1) * 512],
                            AF.Square), waits=[(b.s_dve, ev)], inc=b.s_act)
                        pss_free[pi] = [(b.s_dve, ev)]
                        qn_vals.append((ev, sv))
                    qpe_sq = []
                    for blk in range(4):
                        qpe_sq.append(b.op("scalar", lambda e, blk=blk, sl=sl: e.activation(
                            sqpe[0:64, blk * 512:(blk + 1) * 512],
                            qpeh[sl][0:64, blk * 512:(blk + 1) * 512], AF.Square),
                            waits=[qpe_ld], inc=b.s_act))
                    psn_free = None
                    for blk in range(4):
                        wl = [(b.s_act, kn_vals[blk][1]), (b.s_act, kq_last)]
                        if psn_free is not None:
                            wl.append(psn_free)
                        b.op("tensor", lambda e, blk=blk: e.matmul(
                            psn[0:1, :], self.t_onesb[:], sq4k[:, blk * 512:(blk + 1) * 512],
                            start=True, stop=False), waits=wl, inc=None)
                        last = b.op("tensor", lambda e, blk=blk: e.matmul(
                            psn[0:1, :], self.t_onesb[0:64, :], sqkpe[0:64, blk * 512:(blk + 1) * 512],
                            start=False, stop=True), inc=b.s_pe)
                        rv = b.op("vector", lambda e, blk=blk: e.reduce_max(
                            nmx[0:1, 4 + blk:5 + blk], psn[0:1, :], axis=AX.X),
                            waits=[(b.s_pe, last)], inc=b.s_dve)
                        psn_free = (b.s_dve, rv)
                    psn_free2 = psn_free
                    for blk in range(4):
                        wl = [(b.s_act, qn_vals[blk][1]), (b.s_act, qpe_sq[blk])]
                        if psn_free2 is not None:
                            wl.append(psn_free2)
                        b.op("tensor", lambda e, blk=blk: e.matmul(
                            psn[0:1, :], self.t_onesb[:], sq4[:, blk * 512:(blk + 1) * 512],
                            start=True, stop=False), waits=wl, inc=None)
                        last = b.op("tensor", lambda e, blk=blk: e.matmul(
                            psn[0:1, :], self.t_onesb[0:64, :], sqpe[0:64, blk * 512:(blk + 1) * 512],
                            start=False, stop=True), inc=b.s_pe)
                        rv = b.op("vector", lambda e, blk=blk: e.reduce_max(
                            nmx[0:1, blk:blk + 1], psn[0:1, :], axis=AX.X),
                            waits=[(b.s_pe, last)], inc=b.s_dve)
                        psn_free2 = (b.s_dve, rv)
                    b.op("vector", lambda e: e.reduce_max(ctmp[0:1, 1:2], nmx[0:1, 4:8],
                         axis=AX.X), inc=b.s_dve)
                    dv0 = b.op("vector", lambda e: e.reduce_max(ctmp[0:1, 0:1], nmx[0:1, 0:4],
                         axis=AX.X), inc=b.s_dve)
                    pv = b.op("gpsimd", lambda e: e.tensor_mul(ctmp[0:1, 2:3], ctmp[0:1, 0:1],
                         ctmp[0:1, 1:2]), waits=[(b.s_dve, dv0)], inc=b.s_pool)
                    sv = b.op("scalar", lambda e: e.activation(ctmp[0:1, 3:4], ctmp[0:1, 2:3],
                         AF.Sqrt), waits=[(b.s_pool, pv)], inc=b.s_act)
                    nv = b.op("gpsimd", lambda e: e.tensor_scalar(ctmp[0:1, 2:3], ctmp[0:1, 3:4],
                         -1.0, None, ALU.mult), waits=[(b.s_act, sv)], inc=b.s_pool)
                    mv = b.op("tensor", lambda e: e.matmul(psc[:, 0:1], self.t_ones[:],
                         ctmp[0:1, 2:3], start=True, stop=True), waits=[(b.s_pool, nv)], inc=b.s_pe)
                    cbv = b.op("vector", lambda e: e.tensor_copy(cbias[:], psc[:, 0:1]),
                         waits=[(b.s_pe, mv), (b.s_act, b.s_act.n)], inc=b.s_dve)
                    sum_done = cbv
                    pm_of = {}

                    def stage1(qb, sl=sl, qpe_ld=qpe_ld, cbv=cbv):
                        nonlocal sum_done
                        pb = p4[qb % 2][:]
                        ssi = 0
                        exp_last = [0] * 4
                        for qt in range(4):
                            qtg = qb * 4 + qt
                            klen = (qtg + 1) * 128
                            nblk = (klen + 511) // 512
                            for blk in range(nblk):
                                w = min(512, klen - blk * 512)
                                pi = ssi % 2
                                ssi += 1
                                wl = list(pss_free[pi] or [])
                                last = b.op("tensor", lambda e, qtg=qtg, blk=blk, w=w, pi=pi: e.matmul(
                                    pss[pi][:, 0:w], qnT[:, qtg * 128:(qtg + 1) * 128],
                                    knT[:, blk * 512: blk * 512 + w],
                                    start=True, stop=False), waits=wl, inc=None)
                                last = b.op("tensor", lambda e, qtg=qtg, blk=blk, w=w, pi=pi, sl=sl: e.matmul(
                                    pss[pi][:, 0:w], qpeh[sl][0:64, qtg * 128:(qtg + 1) * 128],
                                    kpeT[0:64, blk * 512: blk * 512 + w],
                                    start=False, stop=True), waits=[qpe_ld], inc=b.s_pe)
                                expw = [(b.s_pe, last)]
                                if blk == nblk - 1:
                                    mkv = b.op("vector", lambda e, w=w, pi=pi: e.tensor_tensor(
                                        pss[pi][:, w - 128:w], pss[pi][:, w - 128:w], self.t_mask[:],
                                        ALU.add), waits=[(b.s_pe, last)], inc=b.s_dve)
                                    expw = [(b.s_dve, mkv)]
                                ev = b.op("scalar", lambda e, qt=qt, blk=blk, w=w, pi=pi, pb=pb: e.activation(
                                    pb[:, qt * S + blk * 512: qt * S + blk * 512 + w], pss[pi][:, 0:w],
                                    AF.Exp, bias=cbias[:], scale=1.0,
                                    accum_out=asum[:, qt * 4 + blk: qt * 4 + blk + 1]),
                                    waits=expw + [(b.s_dve, sum_done), (b.s_dve, cbv)], inc=b.s_act)
                                exp_last[qt] = ev
                                pss_free[pi] = [(b.s_act, ev)]
                        for qt in range(4):
                            qtg = qb * 4 + qt
                            nblk = ((qtg + 1) * 128 + 511) // 512
                            b.op("vector", lambda e, qt=qt, nblk=nblk: e.reduce_sum(
                                atot[:, qt:qt + 1], asum[:, qt * 4: qt * 4 + nblk], axis=AX.X),
                                waits=[(b.s_act, exp_last[qt])], inc=b.s_dve)
                        for qt in range(4):
                            b.op("vector", lambda e, qt=qt: e.reciprocal(
                                arcp[:, qt:qt + 1], atot[:, qt:qt + 1]), inc=b.s_dve)
                        pm_last = None
                        for qt in range(4):
                            qtg = qb * 4 + qt
                            klen = (qtg + 1) * 128
                            pm_last = b.op("vector", lambda e, qt=qt, klen=klen, pb=pb: e.tensor_scalar(
                                pb[:, qt * S: qt * S + klen], pb[:, qt * S: qt * S + klen],
                                arcp[:, qt:qt + 1], None, ALU.mult), inc=b.s_dve)
                        sum_done = pm_last
                        pm_of[qb] = pm_last

                    def stage2(qb, h=h, sl=sl):
                        nonlocal av_pend
                        pb = p4[qb % 2][:]
                        pm_last = pm_of[qb]
                        kmax = (qb + 1) * 4
                        ptev = None
                        for kc in range(kmax):
                            psl = kc % 2
                            wl = [(b.s_dve, pm_last)]
                            wl += pst2_free[psl] or []
                            lastt = None
                            for qt in range(4):
                                if qb * 4 + qt < kc:
                                    continue
                                lastt = b.op("tensor", lambda e, qt=qt, kc=kc, psl=psl, pb=pb: e.transpose(
                                    pst2[psl][:, qt * 128:(qt + 1) * 128],
                                    pb[:, qt * S + kc * 128: qt * S + (kc + 1) * 128],
                                    self.t_identb[:]), waits=wl, inc=b.s_pe)
                                wl = []
                            q0 = max(0, kc - qb * 4)
                            if q0 > 0:
                                b.op("vector", lambda e, kc=kc, q0=q0: e.memset(
                                    PT[:, kc * 512: kc * 512 + q0 * 128], 0.0), inc=b.s_dve)
                            ptev = b.op("vector", lambda e, kc=kc, psl=psl, q0=q0: e.tensor_copy(
                                PT[:, kc * 512 + q0 * 128:(kc + 1) * 512],
                                pst2[psl][:, q0 * 128:512]),
                                waits=[(b.s_pe, lastt)], inc=b.s_dve)
                            pst2_free[psl] = [(b.s_dve, ptev)]
                        pi = qb % 2
                        wl = [(b.s_dve, ptev), (b.s_dve, v_done)]
                        wl += psa_free[pi] or []
                        last = None
                        for kc in range(kmax):
                            last = b.op("tensor", lambda e, kc=kc, pi=pi, sp_=(kc == kmax - 1): e.matmul(
                                psa[pi][:], Vb[:, kc * 128:(kc + 1) * 128],
                                PT[:, kc * 512:(kc + 1) * 512],
                                start=(kc == 0), stop=sp_), waits=wl,
                                inc=(b.s_pe if kc == kmax - 1 else None))
                            wl = []
                        avw = [(b.s_pe, last)]
                        if av_pend is not None:
                            avw.append(av_pend)
                        avv = b.op("vector", lambda e, pi=pi: e.tensor_copy(
                            avb[:], psa[pi][:]), waits=avw, inc=b.s_dve)
                        psa_free[pi] = [(b.s_dve, avv)]
                        stv = b.dma("sync", self.av_spill[h, :, qb * 512:(qb + 1) * 512],
                                    avb[:], waits=[(b.s_dve, avv)], sem=s_av)
                        av_pend = (s_av, stv)

                    stage1(0)
                    stage1(1)
                    stage2(0)
                    stage1(2)
                    stage2(1)
                    stage1(3)
                    stage2(2)
                    stage2(3)
                    bw_free[sl] = [(b.s_pe, b.s_pe.n)]
                    qh_free[sl] = [(b.s_pe, b.s_pe.n)]
                b.barrier()
            if stages < 4:
                return
            with ExitStack() as st4:
                avTs = b.sb("b_avTs", [128, 16 * 512], BF16, st4)
                yT = b.sb("b_yT", [128, 16 * 512], BF16, st4)
                owb = [b.sb(f"b_ow{i}", [128, 16 * 128], BF16, st4) for i in range(2)]
                rx3 = [b.sb(f"b3_rx{i}", [128, D], F32, st4) for i in range(2)]
                psa3 = [b.ps(f"b3_pa{i}", [128, 512], F32, st4) for i in range(2)]
                slab = b.ps("b3_slab", [128, 2048], BF16, st4)
                s_avl = [b.ssem("bavl0"), b.ssem("bavl1")]
                rx_sems = [b.ssem("brx0"), b.ssem("brx1")]
                ow_free = [None, None]
                psa_free = [None, None]
                slab_free = None
                rx_free = [None, None]
                avl_free = [None, None]
                for qb in range(4):
                    asl = qb % 2
                    avw = [avl_free[asl]] if avl_free[asl] is not None else []
                    avv = None
                    for hc in range(16):
                        avv = b.dma("sync", avTs[:, hc * 512:(hc + 1) * 512],
                                    self.av_spill[hc, :, qb * 512:(qb + 1) * 512],
                                    waits=avw, sem=s_avl[asl])
                        avw = []
                    av_ld = (s_avl[asl], avv)
                    for oc in range(16):
                        sl = oc % 2
                        wl = [ow_free[sl]] if ow_free[sl] is not None else []
                        wv = self.load_w("ow", oc, owb[sl], b.ssem(f"bow{sl}"), waits=wl)
                        pi = oc % 2
                        wl = [wv, av_ld]
                        if psa_free[pi] is not None:
                            wl.append(psa_free[pi])
                        last = None
                        for hc in range(16):
                            last = b.op("tensor", lambda e, hc=hc, sl=sl, pi=pi: e.matmul(
                                psa3[pi][:], owb[sl][:, hc * 128:(hc + 1) * 128],
                                avTs[:, hc * 512:(hc + 1) * 512], start=(hc == 0), stop=(hc == 15)),
                                waits=wl, inc=(b.s_pe if hc == 15 else None))
                            wl = []
                        ow_free[sl] = (b.s_pe, last)
                        yv = b.op("vector", lambda e, oc=oc, pi=pi: e.tensor_copy(
                            yT[:, oc * 512:(oc + 1) * 512], psa3[pi][:]),
                            waits=[(b.s_pe, last)], inc=b.s_dve)
                        psa_free[pi] = (b.s_dve, yv)
                    yT_done = b.s_dve.n
                    avl_free[asl] = (b.s_pe, b.s_pe.n)
                    for i in range(4):
                        t = qb * 4 + i
                        ri = i % 2
                        wl = [rx_free[ri]] if rx_free[ri] is not None else []
                        rxl = b.dma("sync", rx3[ri][:], src_fn(t), waits=wl, sem=rx_sems[ri])
                        stv, sfree = self.emit_y_add(yT, (b.s_dve, yT_done), rx3[ri],
                                                     (rx_sems[ri], rxl), i,
                                                     dst_fn(t), slab[:], slab_free, rms_col=None)
                        slab_free = sfree
                        rx_free[ri] = stv
        b.barrier()


def build_nc(chain):
    nc = bass.Bass("TRN2", target_bir_lowering=False, detect_race_conditions=False)
    with ExitStack() as stack:
        mk = MK(nc, stack, chain)
        mk.build()
        block = stack.enter_context(nc.Block())
        mk.b.finalize(block)
    return nc


FULL_CHAIN = [(k, s) for s in range(SEQ_PER_CORE)
              for k in ("attn_a", "ffn", "attn_b", "ffn")]


def make_in_maps(inputs, chain=None):
    g = prep_host(inputs)
    x = np.asarray(inputs["L0_attn_a_x"], np.float32)
    in_maps = []
    for c in range(NCORES):
        m = dict(
            ws=g["ws"], wa=g["wa"], woT=g["woT"], bo=g["bo"], qaT=g["qaT"],
            qalnw=g["qalnw"], qbnT=g["qbnT"], qbpT=g["qbpT"], kvaT=g["kvaT"],
            kvlnw=g["kvlnw"], kvbnT=g["kvbnT"], kvbvT=g["kvbvT"], owT=g["owT"],
            kvapeT=g["kvapeT"],
            gup=g["gup"], dnp=g["dnp"], gusT=g["gusT"], dnsT=g["dnsT"],
            identb=g["identb"], mask=g["mask"], ropeqc=g["ropeqc"], ropeqs=g["ropeqs"],
            ropekc=g["ropekc"], ropeks=g["ropeks"], rotq=g["rotq"], rotk=g["rotk"],
            x_in=np.ascontiguousarray(x[c * SEQ_PER_CORE:(c + 1) * SEQ_PER_CORE]),
        )
        in_maps.append(m)
    return in_maps


def kernel(**inputs):
    chain = FULL_CHAIN
    in_maps = make_in_maps(inputs, chain)
    nc = build_nc(chain)
    res = run_bass_kernel_spmd(nc, in_maps, core_ids=list(range(NCORES)))
    out = np.concatenate([np.asarray(r["y"], np.float32) for r in res.results], axis=0)
    return out.reshape(16, S, D)
